# version 11
# baseline (speedup 1.0000x reference)
import contextlib
import numpy as np
import ml_dtypes
import concourse.bass as bass
import concourse.mybir as mybir
from concourse.bass_utils import run_bass_kernel_spmd

F32 = mybir.dt.float32
BF16 = mybir.dt.bfloat16
I32 = mybir.dt.int32
AF = mybir.ActivationFunctionType
ALU = mybir.AluOpType
AX = mybir.AxisListType

D = 2048
S = 4096
HEADS = 16
DK = 128
DC = D // 128
TT = S // 128
FF = 8192
FC = FF // 128
E = 7
OWN = 2048
NEG = -1.0e9
ISQ = 0.08838834764831845

ENGS = ("tensor", "vector", "scalar", "sync", "gpsimd")
DMA_ENGS = ("sync", "gpsimd", "scalar")



class Op:
    __slots__ = ("eng", "fn", "deps", "seq", "is_dma", "slot", "slot_ord",
                 "mark_ord", "is_barrier", "dma_snapshot")

    def __init__(self, eng, fn, is_dma, seq):
        self.eng = eng
        self.fn = fn
        self.deps = []
        self.seq = seq
        self.is_dma = is_dma
        self.slot = None
        self.slot_ord = 0
        self.mark_ord = None
        self.is_barrier = False
        self.dma_snapshot = None


class Buf:
    __slots__ = ("w", "r", "name", "dma_cnt", "sem_idx", "qclass")

    def __init__(self, name=""):
        self.w = None
        self.r = []
        self.name = name
        self.dma_cnt = 0
        self.sem_idx = None
        self.qclass = None

    def rd(self):
        return [self.w] if self.w is not None else []

    def wd(self):
        d = list(self.r)
        if self.w is not None:
            d.append(self.w)
        return d


class Sched:

    def __init__(self, nc):
        self.nc = nc
        self.ops = []
        self.slots = []
        self.last_by_eng = {}

    def op(self, eng, fn, reads=(), writes=(), dma=False):
        o = Op(eng, fn, dma, len(self.ops))
        if dma:
            assert len(writes) == 1, "each DMA must write exactly one Buf"
            b = writes[0]
            qc = "sw" if eng == "gpsimd" else "hw"
            if b.qclass is None:
                b.qclass = qc
                self.slots.append(b)
            assert b.qclass == qc, f"buf {b.name} written from both SWDGE and HWDGE queues"
            b.dma_cnt += 1
            o.slot = b
            o.slot_ord = b.dma_cnt
        deps = []
        for b in reads:
            deps += b.rd()
        for b in writes:
            deps += b.wd()
        seen = set()
        for d in deps:
            if d is None or d.seq in seen:
                continue
            seen.add(d.seq)
            if d.eng == "tensor" and eng == "tensor" and not d.is_dma and not dma:
                continue
            o.deps.append(d)
        self.ops.append(o)
        self.last_by_eng[(eng, dma)] = o
        for b in reads:
            b.r.append(o)
        for b in writes:
            b.w = o
            b.r = []
        return o

    def dma(self, eng, fn, reads=(), writes=()):
        return self.op(eng, fn, reads, writes, dma=True)

    def barrier(self):
        lasts = [d for d in self.last_by_eng.values()
                 if d.fn is not None and not d.is_dma]
        snap = [(b, b.dma_cnt) for b in self.slots if b.dma_cnt > 0]
        for e in ENGS:
            o = Op(e, None, False, len(self.ops))
            o.is_barrier = True
            o.deps = [d for d in lasts if d.eng != e]
            o.dma_snapshot = snap
            self.ops.append(o)

    def finalize(self, stack):
        nc = self.nc
        csem = {e: stack.enter_context(nc.semaphore(f"c_{e}")) for e in ENGS}
        barrier_seqs = [o.seq for o in self.ops if o.is_barrier]
        first = {}
        last = {}
        for o in self.ops:
            if o.is_dma:
                b = o.slot
                first.setdefault(id(b), o.seq)
                last[id(b)] = o.seq
            for d in o.deps:
                if d.is_dma:
                    last[id(d.slot)] = o.seq
        import bisect
        slots_sorted = sorted((b for b in self.slots if id(b) in first),
                              key=lambda b: first[id(b)])
        sem_free_at = []
        for b in slots_sorted:
            fs = first[id(b)]
            got = None
            for i, (qc, fa) in enumerate(sem_free_at):
                if qc == b.qclass and fa is not None and fa <= fs:
                    got = i
                    break
            if got is None:
                got = len(sem_free_at)
                sem_free_at.append((b.qclass, None))
            b.sem_idx = got
            ls = last[id(b)]
            j = bisect.bisect_right(barrier_seqs, ls)
            sem_free_at[got] = (b.qclass,
                                barrier_seqs[j] if j < len(barrier_seqs) else None)
        n_sems = len(sem_free_at)
        avail = len(bass.get_kernel_semaphore_range())
        assert n_sems + len(ENGS) <= avail - 2, \
            f"need {n_sems} slot sems + {len(ENGS)}, have {avail}"
        ssem = [stack.enter_context(nc.semaphore(f"s{i}")) for i in range(n_sems)]
        marked = set()
        for o in self.ops:
            for d in o.deps:
                if not d.is_dma:
                    marked.add(d.seq)
        ords = {e: 0 for e in ENGS}
        for o in self.ops:
            if not o.is_dma and not o.is_barrier and o.seq in marked:
                ords[o.eng] += 1
                o.mark_ord = ords[o.eng]
        semcnt = [0] * n_sems
        dma_semord = {}
        bar_semsnap = {}
        for o in self.ops:
            if o.is_dma:
                i = o.slot.sem_idx
                semcnt[i] += 1
                dma_semord[o.seq] = (i, semcnt[i])
            if o.is_barrier:
                bar_semsnap[o.seq] = list(semcnt)
        NCH = len(ENGS) + n_sems
        eng_ch = {e: i for i, e in enumerate(ENGS)}
        ZERO = [0] * NCH
        vc_of = {}
        last_eng_op = {}
        streams = {e: [] for e in ENGS}

        def dep_chan_val(d):
            if d.is_dma:
                i, ordn = dma_semord[d.seq]
                return len(ENGS) + i, 16 * ordn, ssem[i]
            if d.mark_ord is None:
                return None
            return eng_ch[d.eng], d.mark_ord, csem[d.eng]

        for o in self.ops:
            key = (o.eng, o.is_dma)
            prev = last_eng_op.get(key)
            ukey = o.eng
            prev = last_eng_op.get(ukey)
            vc = list(vc_of[prev]) if prev is not None else list(ZERO)
            if prev is not None:
                pass
            waits = []
            for d in o.deps:
                cv = dep_chan_val(d)
                if cv is None:
                    continue
                ch, val, sem = cv
                dvc = vc_of.get(d.seq)
                need = val > vc[ch]
                if need:
                    waits.append((sem, val))
                    vc[ch] = val
                if dvc is not None:
                    for c2 in range(NCH):
                        if dvc[c2] > vc[c2]:
                            vc[c2] = dvc[c2]
            if o.is_barrier:
                snap = bar_semsnap[o.seq]
                for i, cnt in enumerate(snap):
                    if cnt > 0 and 16 * cnt > vc[len(ENGS) + i]:
                        waits.append((ssem[i], 16 * cnt))
                        vc[len(ENGS) + i] = 16 * cnt
            vc_of[o.seq] = vc
            last_eng_op[o.eng] = o.seq
            if not o.is_dma and o.mark_ord is not None:
                pass
            incs = []
            if o.is_dma:
                i, ordn = dma_semord[o.seq]
                incs.append((ssem[i], 16))
            elif o.mark_ord is not None:
                incs.append((csem[o.eng], 1))
            streams[o.eng].append((waits, o.fn, incs))
        tail = []
        sync_last = last_eng_op.get("sync")
        sync_vc = vc_of.get(sync_last, list(ZERO)) if sync_last is not None else list(ZERO)
        for i in range(n_sems):
            if semcnt[i] > 0 and 16 * semcnt[i] > sync_vc[len(ENGS) + i]:
                tail.append((ssem[i], 16 * semcnt[i]))
        streams["sync"].append((tail, None, []))

        with nc.Block() as block:
            def run(engine, seq):
                for waits, fn, incs in seq:
                    for s, v in waits:
                        engine.wait_ge(s, v)
                    if fn is None:
                        continue
                    inst = fn(engine)
                    for s, n in incs:
                        inst.then_inc(s, n)

            @block.tensor
            def _(te):
                run(te, streams["tensor"])

            @block.vector
            def _(v):
                run(v, streams["vector"])

            @block.scalar
            def _(sc):
                run(sc, streams["scalar"])

            @block.sync
            def _(sy):
                run(sy, streams["sync"])

            @block.gpsimd
            def _(g):
                run(g, streams["gpsimd"])


class SbAlloc:

    def __init__(self, nc, stack, name, total_cols, dtype):
        self.t = stack.enter_context(nc.sbuf_tensor(name, [128, total_cols], dtype))
        self.total = total_cols
        self.off = 0
        self.marks = []
        self.hiwater = 0
        self.regions = []

    def alloc(self, cols, name="", buf=None):
        assert self.off + cols <= self.total, \
            f"SBUF pool overflow: {self.off}+{cols}>{self.total} ({name})"
        ap = self.t[:, self.off:self.off + cols]
        b = buf if buf is not None else Buf(name)
        s, e2 = self.off, self.off + cols
        keep = []
        pend = []
        for (rs, re, rb) in self.regions:
            if rs < e2 and re > s:
                pend += rb.r
                if rb.w is not None:
                    pend.append(rb.w)
            else:
                keep.append((rs, re, rb))
        self.regions = keep + [(s, e2, b)]
        seen = set()
        for o in pend:
            if o.seq not in seen:
                seen.add(o.seq)
                b.r.append(o)
        self.off = e2
        self.hiwater = max(self.hiwater, self.off)
        return ap, b

    def push(self):
        self.marks.append(self.off)

    def pop(self):
        self.off = self.marks.pop()



def emit_ln_stats(sh, xt_ap, xt_b, width, eps, junk_ap, junk_b, st_ap, st_b):
    s_sum, s_sq = st_ap[:, 0:1], st_ap[:, 1:2]
    s_m, s_rstd = st_ap[:, 2:3], st_ap[:, 3:4]
    s_tmp, s_var = st_ap[:, 4:5], st_ap[:, 5:6]
    s_negm = st_ap[:, 6:7]
    sh.op("vector", lambda e: e.reduce_sum(s_sum, xt_ap, axis=AX.X),
          reads=[xt_b], writes=[st_b])
    sh.op("scalar", lambda e: e.activation(junk_ap, xt_ap, AF.Square, accum_out=s_sq),
          reads=[xt_b], writes=[junk_b, st_b])
    sh.op("vector", lambda e: e.tensor_scalar(s_m, s_sum, 1.0 / width, None, ALU.mult),
          reads=[st_b], writes=[st_b])
    sh.op("vector", lambda e: e.tensor_scalar(s_negm, s_m, -1.0, None, ALU.mult),
          reads=[st_b], writes=[st_b])
    sh.op("vector", lambda e: e.tensor_scalar(s_tmp, s_m, s_m, None, ALU.mult),
          reads=[st_b], writes=[st_b])
    sh.op("vector", lambda e: e.tensor_scalar(s_var, s_sq, 1.0 / width, s_tmp, ALU.mult, ALU.subtract),
          reads=[st_b], writes=[st_b])
    sh.op("vector", lambda e: e.tensor_scalar(s_var, s_var, float(eps), None, ALU.add),
          reads=[st_b], writes=[st_b])
    sh.op("scalar", lambda e: e.activation(s_tmp, s_var, AF.Sqrt),
          reads=[st_b], writes=[st_b])
    sh.op("vector", lambda e: e.reciprocal(s_rstd, s_tmp),
          reads=[st_b], writes=[st_b])



def build_program(blocks, boundaries):
    nc = bass.Bass("TRN2", target_bir_lowering=False, debug=False,
                   detect_race_conditions=False)
    sh = Sched(nc)

    def stream_tensor(name, shape, dtype=F32):
        if name in boundaries:
            kind = "ExternalInput" if boundaries[name] == "in" else "ExternalOutput"
        else:
            kind = "Internal"
        return nc.dram_tensor(name, list(shape), dtype, kind=kind).ap()

    x0 = nc.dram_tensor("x0", [S, D], F32, kind="ExternalInput").ap() if "a0" in blocks else None
    x1 = stream_tensor("x1", [S, D]) if ("a0" in blocks or "f0" in blocks) else None
    x2 = stream_tensor("x2", [S, D]) if ("f0" in blocks or "a1" in blocks) else None
    x3 = stream_tensor("x3", [S, D]) if ("a1" in blocks or "moe" in blocks) else None
    yout = nc.dram_tensor("y", [OWN, D], F32, kind="ExternalOutput").ap() if "moe" in blocks else None

    ident_d = nc.dram_tensor("identc", [128, 128], BF16, kind="ExternalInput").ap()
    ones_d = nc.dram_tensor("onesc", [128, 128], BF16, kind="ExternalInput").ap()
    mask_d = nc.dram_tensor("maskc", [128, 896], F32, kind="ExternalInput").ap()
    identf_d = nc.dram_tensor("identfc", [128, 128], F32, kind="ExternalInput").ap()
    onesf_d = nc.dram_tensor("onesfc", [128, 128], F32, kind="ExternalInput").ap()

    W = {}

    def win(nm, shape, dt=F32):
        W[nm] = nc.dram_tensor(nm, list(shape), dt, kind="ExternalInput").ap()

    if "a0" in blocks:
        for nm in ("wq", "wk", "wv", "wo"):
            win(nm, [D, D])
    if "f0" in blocks:
        win("wfc", [D, FF])
        win("wproj", [FF, D])
    if "a1" in blocks:
        win("wqkv", [D, 3 * D])
        win("wo1", [D, D])
    if "moe" in blocks:
        win("wup", [E, D, D])
        win("wgate", [E, D, D])
        win("wdown", [E, D, D])
        win("wsu", [D, D])
        win("wsg", [D, D])
        win("wsd", [D, D])
        win("wr", [D, E])
        win("br", [1, E])
        win("gbias", [1, E])
        win("bup", [E, DC, 128])
        win("bgate", [E, DC, 128])
        win("bdown", [E, DC, 128])
        win("bsu", [DC, 128])
        win("bsg", [DC, 128])
        win("bsd", [1, D])
        win("bdown_r", [E, D])
        W["rowsel"] = nc.dram_tensor("rowsel", [128, OWN // 128], I32, kind="ExternalInput").ap()

    dsc = {}
    dsc["xn_fm"] = nc.dram_tensor("xn_fm", [DC, 128, S], BF16, kind="Internal").ap()
    dsc["q_fm"] = nc.dram_tensor("q_fm", [DC, 128, S], BF16, kind="Internal").ap()
    dsc["k_fm"] = nc.dram_tensor("k_fm", [DC, 128, S], BF16, kind="Internal").ap()
    dsc["attn_fm"] = nc.dram_tensor("attn_fm", [DC, 128, S], BF16, kind="Internal").ap()
    dsc["v_img"] = nc.dram_tensor("v_img", [HEADS, 128, TT, 128], BF16, kind="Internal").ap()
    dsc["h_tm"] = nc.dram_tensor("h_tm", [S, FF], BF16, kind="Internal").ap()
    dsc["xh2_fm"] = nc.dram_tensor("xh2_fm", [128, FC, S], BF16, kind="Internal").ap()
    dsc["t_proj"] = nc.dram_tensor("t_proj", [FF, D], BF16, kind="Internal").ap()
    dsc["out_proj"] = nc.dram_tensor("out_proj", [S, D], F32, kind="Internal").ap()
    dsc["u_d"] = nc.dram_tensor("u_d", [128, DC, OWN], BF16, kind="Internal").ap()
    dsc["g_d"] = nc.dram_tensor("g_d", [128, DC, OWN], BF16, kind="Internal").ap()
    dsc["acc_d"] = nc.dram_tensor("acc_d", [OWN, D], F32, kind="Internal").ap()
    dsc["dbg"] = nc.dram_tensor("dbg", [128, 16], F32, kind="Internal").ap()
    dbuf = {nm: Buf(nm) for nm in
            ("xn_fm", "q_fm", "k_fm", "v_img", "attn_fm", "h_tm", "xh2_fm",
             "t_proj", "out_proj", "u_d", "g_d", "acc_d", "x1", "x2", "x3", "y")}

    with contextlib.ExitStack() as stack:
        sb = SbAlloc(nc, stack, "sb_bf", 62000, BF16)
        sf = SbAlloc(nc, stack, "sb_f32", 18600, F32)

        ident, ident_b = sb.alloc(128, "ident")
        ones, ones_b = sb.alloc(128, "ones")
        mask, mask_b = sf.alloc(896, "mask")
        sh.dma("sync", lambda e: e.dma_start(out=ident, in_=ident_d[:, :]), writes=[ident_b])
        sh.dma("sync", lambda e: e.dma_start(out=ones, in_=ones_d[:, :]), writes=[ones_b])
        sh.dma("sync", lambda e: e.dma_start(out=mask, in_=mask_d[:, :]), writes=[mask_b])
        identf, identf_b = sf.alloc(128, "identf")
        onesf, onesf_b = sf.alloc(128, "onesf")
        sh.dma("sync", lambda e: e.dma_start(out=identf, in_=identf_d[:, :]), writes=[identf_b])
        sh.dma("sync", lambda e: e.dma_start(out=onesf, in_=onesf_d[:, :]), writes=[onesf_b])

        pb, pbb = [], []
        for i in range(6):
            t = stack.enter_context(nc.psum_tensor(f"pb{i}", [128, 512], F32))
            pb.append(t)
            pbb.append(Buf(f"pb{i}"))
        ptp, ptb = [], []
        for i in range(2):
            t = stack.enter_context(nc.psum_tensor(f"pt{i}", [128, 1024], BF16))
            ptp.append(t)
            ptb.append(Buf(f"pt{i}"))

        rsel_t = stack.enter_context(nc.sbuf_tensor("rsel_i32", [128, 16], I32))
        ctx = dict(nc=nc, sh=sh, sb=sb, sf=sf, W=W, dsc=dsc, dbuf=dbuf, rsel_t=rsel_t,
                   ident=ident, ident_b=ident_b, ones=ones, ones_b=ones_b,
                   mask=mask, mask_b=mask_b, identf=identf, identf_b=identf_b,
                   onesf=onesf, onesf_b=onesf_b, pb=pb, pbb=pbb, ptp=ptp, ptb=ptb)

        if "a0" in blocks:
            emit_attn(ctx, layer=0, x_in=x0, xin_b=Buf("x0"), x_out=x1, xout_b=dbuf["x1"])
        if "f0" in blocks:
            sh.barrier()
            emit_ffn0(ctx, x_in=x1, xin_b=dbuf["x1"], x_out=x2, xout_b=dbuf["x2"])
        if "a1" in blocks:
            sh.barrier()
            emit_attn(ctx, layer=1, x_in=x2, xin_b=dbuf["x2"], x_out=x3, xout_b=dbuf["x3"])
        if "moe" in blocks:
            sh.barrier()
            emit_moe(ctx, x_in=x3, xin_b=dbuf["x3"], y_out=yout, yout_b=dbuf["y"])

        sh.finalize(stack)
    return nc



def emit_attn(ctx, layer, x_in, xin_b, x_out, xout_b):
    sh, sb, sf = ctx["sh"], ctx["sb"], ctx["sf"]
    W, dsc, dbuf = ctx["W"], ctx["dsc"], ctx["dbuf"]
    ident, ident_b = ctx["ident"], ctx["ident_b"]
    mask, mask_b = ctx["mask"], ctx["mask_b"]
    pb, pbb, ptp, ptb = ctx["pb"], ctx["pbb"], ctx["ptp"], ctx["ptb"]
    EPS = 1e-5
    xn_fm, xnb = dsc["xn_fm"], dbuf["xn_fm"]
    q_fm, qb = dsc["q_fm"], dbuf["q_fm"]
    k_fm, kb = dsc["k_fm"], dbuf["k_fm"]
    v_img, vb = dsc["v_img"], dbuf["v_img"]
    attn_fm, ab = dsc["attn_fm"], dbuf["attn_fm"]

    sb.push(); sf.push()
    xt = [sf.alloc(D, f"xt{i}") for i in range(2)]
    xn_tm4 = [sb.alloc(D, f"xntm{i}") for i in range(4)]
    stats = [sf.alloc(8, f"st{i}") for i in range(4)]
    tstg = [sb.alloc(512, f"tstg{i}") for i in range(2)]

    for t in range(TT):
        xa, xab = xt[t % 2]
        oa, ob = xn_tm4[t % 4]
        sa, sab = stats[t % 4]
        sh.dma("sync", lambda e, xa=xa, t=t: e.dma_start(out=xa, in_=x_in[t * 128:(t + 1) * 128, :]),
               reads=[xin_b], writes=[xab])
        emit_ln_stats(sh, xa, xab, D, EPS, oa, ob, sa, sab)
        sh.op("vector", lambda e, oa=oa, xa=xa, sa=sa:
              e.tensor_scalar(oa, xa, sa[:, 2:3], sa[:, 3:4], ALU.subtract, ALU.mult),
              reads=[xab, sab], writes=[ob])
        if t % 4 == 3:
            for c in range(DC):
                bank, bankb = ptp[c % 2], ptb[c % 2]
                for j in range(4):
                    src, srcb = xn_tm4[j]
                    sh.op("tensor",
                          lambda e, bank=bank, j=j, src=src, c=c:
                          e.transpose(bank[:, j * 128:(j + 1) * 128],
                                      src[:, c * 128:(c + 1) * 128], ident),
                          reads=[srcb, ident_b], writes=[bankb])
                sg, sgb = tstg[c % 2]
                sh.op("scalar", lambda e, sg=sg, bank=bank: e.copy(sg, bank[:, 0:512]),
                      reads=[bankb], writes=[sgb])
                col0 = (t - 3) * 128
                sh.dma("sync",
                       lambda e, sg=sg, c=c, col0=col0:
                       e.dma_start(out=xn_fm[c, :, col0:col0 + 512], in_=sg),
                       reads=[sgb], writes=[xnb])
    sb.pop(); sf.pop()

    sb.push(); sf.push()
    wsb_shared = Buf("wsb_g")
    wsb = []
    for d in range(DC):
        ap, b = sb.alloc(D, f"wsb{d}", buf=wsb_shared)
        wsb.append((ap, wsb_shared))
    xnT_shared = Buf("xnT_g")
    xnT_all, _ = sb.alloc(DC * 512, "xnT_all", buf=xnT_shared)
    xnT = [(xnT_all[:, d * 512:(d + 1) * 512], xnT_shared) for d in range(DC)]
    pstg = [sb.alloc(512, f"pstg{i}") for i in range(4)]

    if layer == 0:
        wsrc = {"q": (W["wq"], 0), "k": (W["wk"], 0), "v": (W["wv"], 0),
                "o": (W["wo"], 0)}
    else:
        wsrc = {"q": (W["wqkv"], 0), "k": (W["wqkv"], D), "v": (W["wqkv"], 2 * D),
                "o": (W["wo1"], 0)}

    def load_weights(tag):
        wd, off = wsrc[tag]
        for d in range(DC):
            wa, wab = wsb[d]
            sh.dma("gpsimd",
                   lambda e, wa=wa, d=d, wd=wd, off=off:
                   e.dma_start(out=wa, in_=wd[d * 128:(d + 1) * 128, off:off + D]),
                   writes=[wab])

    def load_xnT(T):
        sh.dma("sync",
               lambda e, T=T:
               e.dma_start(out=xnT_all.rearrange("p (c k) -> p c k", k=512),
                           in_=xn_fm[:, :, T * 512:(T + 1) * 512].rearrange("c p k -> p c k")),
               reads=[xnb], writes=[xnT_shared])

    for tag, dst, dstb in (("q", q_fm, qb), ("k", k_fm, kb)):
        load_weights(tag)
        for T in range(S // 512):
            load_xnT(T)
            for o in range(DC):
                bank, bankb = pb[o % 4], pbb[o % 4]
                for d in range(DC):
                    wa, wab = wsb[d]
                    xa, xab = xnT[d]
                    sh.op("tensor",
                          lambda e, bank=bank, wa=wa, xa=xa, o=o, d=d:
                          e.matmul(bank[:], wa[:, o * 128:(o + 1) * 128], xa,
                                   start=(d == 0), stop=(d == DC - 1)),
                          reads=[wab, xab], writes=[bankb])
                sg, sgb = pstg[o % 4]
                sh.op("scalar", lambda e, sg=sg, bank=bank: e.copy(sg, bank[:]),
                      reads=[bankb], writes=[sgb])
                sh.dma("sync",
                       lambda e, sg=sg, o=o, T=T, dst=dst:
                       e.dma_start(out=dst[o, :, T * 512:(T + 1) * 512], in_=sg),
                       reads=[sgb], writes=[dstb])

    load_weights("v")
    for T in range(S // 512):
        load_xnT(T)
        for tl in range(4):
            t = T * 4 + tl
            for og in range(D // 512):
                bank, bankb = pb[(tl * 4 + og) % 4], pbb[(tl * 4 + og) % 4]
                for d in range(DC):
                    wa, wab = wsb[d]
                    xa, xab = xnT[d]
                    sh.op("tensor",
                          lambda e, bank=bank, xa=xa, wa=wa, tl=tl, og=og, d=d:
                          e.matmul(bank[:], xa[:, tl * 128:(tl + 1) * 128],
                                   wa[:, og * 512:(og + 1) * 512],
                                   start=(d == 0), stop=(d == DC - 1)),
                          reads=[xab, wab], writes=[bankb])
                sg, sgb = pstg[(tl * 4 + og) % 4]
                sh.op("scalar", lambda e, sg=sg, bank=bank: e.copy(sg, bank[:]),
                      reads=[bankb], writes=[sgb])
                for j in range(4):
                    h = og * 4 + j
                    sh.dma("sync",
                           lambda e, sg=sg, h=h, t=t, j=j:
                           e.dma_start(out=v_img[h, :, t, :], in_=sg[:, j * 128:(j + 1) * 128]),
                           reads=[sgb], writes=[vb])
    sb.pop(); sf.pop()

    sb.push(); sf.push()
    qh = [sb.alloc(S, f"qh{i}") for i in range(2)]
    kh = [sb.alloc(S, f"kh{i}") for i in range(2)]
    vh = [sb.alloc(S, f"vh{i}") for i in range(2)]
    psb = [sb.alloc(S, f"p{i}") for i in range(8)]
    pts = [sb.alloc(512, f"pts{i}") for i in range(2)]
    avstg = [sb.alloc(512, f"avs{i}") for i in range(2)]
    scores = [sf.alloc(S, f"sc{i}") for i in range(4)]
    rowst = [sf.alloc(8, f"rs{i}") for i in range(4)]

    NQB = S // 512
    for h in range(HEADS):
        qa, qab = qh[h % 2]
        ka, kab = kh[h % 2]
        va, vab = vh[h % 2]
        sh.dma("sync", lambda e, qa=qa, h=h: e.dma_start(out=qa, in_=q_fm[h, :, :]),
               reads=[qb], writes=[qab])
        sh.dma("sync", lambda e, ka=ka, h=h: e.dma_start(out=ka, in_=k_fm[h, :, :]),
               reads=[kb], writes=[kab])
        sh.dma("sync", lambda e, va=va, h=h: e.dma_start(
            out=va, in_=v_img[h].rearrange("p t d -> p (t d)")),
            reads=[vb], writes=[vab])
        def emit_scores(qblk):
            nkt = qblk + 1
            for kt in range(nkt):
                for qs in range(4):
                    bank, bankb = pb[qs], pbb[qs]
                    sh.op("tensor",
                          lambda e, bank=bank, qa=qa, ka=ka, qblk=qblk, qs=qs, kt=kt:
                          e.matmul(bank[:], qa[:, qblk * 512 + qs * 128:qblk * 512 + (qs + 1) * 128],
                                   ka[:, kt * 512:(kt + 1) * 512], start=True, stop=True),
                          reads=[qab, kab], writes=[bankb])
                    dst, dstb = scores[qs]
                    if kt == qblk:
                        msl = mask[:, 384 - qs * 128: 896 - qs * 128]
                        sh.op("vector",
                              lambda e, dst=dst, bank=bank, msl=msl, kt=kt:
                              e.scalar_tensor_tensor(dst[:, kt * 512:(kt + 1) * 512], bank[:],
                                                     ISQ, msl, ALU.mult, ALU.add),
                              reads=[bankb, mask_b], writes=[dstb])
                    else:
                        sh.op("vector" if kt % 2 == 0 else "scalar",
                              lambda e, dst=dst, bank=bank, kt=kt:
                              e.tensor_scalar(dst[:, kt * 512:(kt + 1) * 512], bank[:],
                                              ISQ, None, ALU.mult)
                              if kt % 2 == 0 else
                              e.mul(dst[:, kt * 512:(kt + 1) * 512], bank[:], ISQ),
                              reads=[bankb], writes=[dstb])
            for qs in range(4):
                sca, scab = scores[qs]
                pa, pab = psb[qs + 4 * (qblk % 2)]
                ra, rab = rowst[qs]
                klen = nkt * 512
                sh.op("vector", lambda e, ra=ra, sca=sca, klen=klen:
                      e.reduce_max(ra[:, 0:1], sca[:, 0:klen], axis=AX.X),
                      reads=[scab], writes=[rab])
                sh.op("vector", lambda e, ra=ra:
                      e.tensor_scalar(ra[:, 3:4], ra[:, 0:1], -1.0, None, ALU.mult),
                      reads=[rab], writes=[rab])
                sh.op("scalar", lambda e, pa=pa, sca=sca, ra=ra, klen=klen:
                      e.activation(pa[:, 0:klen], sca[:, 0:klen], AF.Exp,
                                   bias=ra[:, 3:4], scale=1.0, accum_out=ra[:, 1:2]),
                      reads=[scab, rab], writes=[pab, rab])
                sh.op("vector", lambda e, ra=ra: e.reciprocal(ra[:, 2:3], ra[:, 1:2]),
                      reads=[rab], writes=[rab])
                sh.op("gpsimd", lambda e, pa=pa, ra=ra, klen=klen:
                      e.tensor_scalar(pa[:, 0:klen], pa[:, 0:klen], ra[:, 2:3], None, ALU.mult),
                      reads=[rab, pab], writes=[pab])

        def emit_av(qblk):
            nkt = qblk + 1
            avbank, avbankb = pb[4], pbb[4]
            for kc in range(nkt * 4):
                bank, bankb = ptp[kc % 2], ptb[kc % 2]
                for qs in range(4):
                    pa, pab = psb[qs + 4 * (qblk % 2)]
                    sh.op("tensor",
                          lambda e, bank=bank, pa=pa, qs=qs, kc=kc:
                          e.transpose(bank[:, qs * 128:(qs + 1) * 128],
                                      pa[:, kc * 128:(kc + 1) * 128], ident),
                          reads=[pab, ident_b], writes=[bankb])
                sg, sgb = pts[kc % 2]
                sh.op("vector", lambda e, sg=sg, bank=bank: e.tensor_copy(sg, bank[:, 0:512]),
                      reads=[bankb], writes=[sgb])
                sh.op("tensor",
                      lambda e, va=va, sg=sg, kc=kc, first=(kc == 0), last=(kc == nkt * 4 - 1),
                      avbank=avbank:
                      e.matmul(avbank[:], va[:, kc * 128:(kc + 1) * 128], sg,
                               start=first, stop=last),
                      reads=[vab, sgb], writes=[avbankb])
            sg, sgb = avstg[qblk % 2]
            sh.op("scalar", lambda e, sg=sg, avbank=avbank: e.copy(sg, avbank[:]),
                  reads=[avbankb], writes=[sgb])
            sh.dma("sync",
                   lambda e, sg=sg, h=h, qblk=qblk:
                   e.dma_start(out=attn_fm[h, :, qblk * 512:(qblk + 1) * 512], in_=sg),
                   reads=[sgb], writes=[ab])

        emit_scores(0)
        for qblk in range(NQB):
            if qblk + 1 < NQB:
                emit_scores(qblk + 1)
            emit_av(qblk)
    sb.pop(); sf.pop()

    sb.push(); sf.push()
    wosb_shared = Buf("wosb_g")
    wosb = []
    for d in range(DC):
        ap, b = sb.alloc(D, f"wo{d}", buf=wosb_shared)
        wosb.append((ap, wosb_shared))
        wd, off = wsrc["o"]
        sh.dma("gpsimd",
               lambda e, ap=ap, d=d, wd=wd, off=off:
               e.dma_start(out=ap, in_=wd[d * 128:(d + 1) * 128, off:off + D]),
               writes=[wosb_shared])
    att = [sb.alloc(D, f"att{i}") for i in range(2)]
    junk, junkb = sb.alloc(D, "junk")
    ot = [sf.alloc(D, f"ot{i}") for i in range(2)]
    res = [sf.alloc(D, f"res{i}") for i in range(2)]
    ost = [sf.alloc(8, f"ost{i}") for i in range(2)]

    for t in range(TT):
        aa, aab = att[t % 2]
        sh.dma("sync", lambda e, aa=aa, t=t: e.dma_start(
            out=aa.rearrange("p (c k) -> p c k", k=128),
            in_=attn_fm[:, :, t * 128:(t + 1) * 128].rearrange("c p k -> p c k")),
            reads=[ab], writes=[aab])
        oa, oab = ot[t % 2]
        for og in range(D // 512):
            bank, bankb = pb[og % 4], pbb[og % 4]
            for d in range(DC):
                wa, wab = wosb[d]
                sh.op("tensor",
                      lambda e, bank=bank, aa=aa, wa=wa, d=d, og=og:
                      e.matmul(bank[:], aa[:, d * 128:(d + 1) * 128],
                               wa[:, og * 512:(og + 1) * 512],
                               start=(d == 0), stop=(d == DC - 1)),
                      reads=[aab, wab], writes=[bankb])
            sh.op("scalar", lambda e, oa=oa, og=og, bank=bank:
                  e.copy(oa[:, og * 512:(og + 1) * 512], bank[:]),
                  reads=[bankb], writes=[oab])
        ra, rab = ost[t % 2]
        emit_ln_stats(sh, oa, oab, D, EPS, junk, junkb, ra, rab)
        xa2, xab2 = res[t % 2]
        sh.dma("scalar", lambda e, xa2=xa2, t=t:
               e.dma_start(out=xa2, in_=x_in[t * 128:(t + 1) * 128, :]),
               reads=[xin_b], writes=[xab2])
        sh.op("vector", lambda e, oa=oa, ra=ra:
              e.tensor_scalar(oa, oa, ra[:, 2:3], ra[:, 3:4], ALU.subtract, ALU.mult),
              reads=[oab, rab], writes=[oab])
        sh.op("vector", lambda e, oa=oa, xa2=xa2: e.tensor_add(oa, oa, xa2),
              reads=[oab, xab2], writes=[oab])
        sh.dma("sync", lambda e, oa=oa, t=t: e.dma_start(out=x_out[t * 128:(t + 1) * 128, :], in_=oa),
               reads=[oab], writes=[xout_b])
    sb.pop(); sf.pop()


def emit_ffn0(ctx, x_in, xin_b, x_out, xout_b):
    sh, sb, sf = ctx["sh"], ctx["sb"], ctx["sf"]
    W, dsc, dbuf = ctx["W"], ctx["dsc"], ctx["dbuf"]
    ident, ident_b = ctx["ident"], ctx["ident_b"]
    identf, identf_b = ctx["identf"], ctx["identf_b"]
    onesf, onesf_b = ctx["onesf"], ctx["onesf_b"]
    pb, pbb, ptp, ptb = ctx["pb"], ctx["pbb"], ctx["ptp"], ctx["ptb"]
    wfc, wproj = W["wfc"], W["wproj"]
    h_tm, hb = dsc["h_tm"], dbuf["h_tm"]
    xh2_fm, x2fb = dsc["xh2_fm"], dbuf["xh2_fm"]
    t_proj, tpb = dsc["t_proj"], dbuf["t_proj"]
    out_proj, opb = dsc["out_proj"], dbuf["out_proj"]
    CLIP = 128.0 - 1e-6

    sb.push(); sf.push()
    gb_fc, gb_fcb = sf.alloc(4, "gb_fc")
    gb_pr, gb_prb = sf.alloc(4, "gb_pr")
    hsum, hsumb = sf.alloc(TT, "hsum")
    hsq, hsqb = sf.alloc(TT, "hsq")
    hmax, hmaxb = sf.alloc(TT, "hmax")
    hmin, hminb = sf.alloc(TT, "hmin")
    sct, sctb = sf.alloc(TT, "sct")
    sc2, sc2b = sf.alloc(TT, "sc2")
    ln3, ln3b = sf.alloc(3 * TT, "ln3")

    sb.push(); sf.push()
    gsc, gscb = sf.alloc(16, "gsc")
    grow, growb = sf.alloc(16, "grow")
    rowx, rowxb = sf.alloc(128, "rowx")
    slab = [sf.alloc(FF, f"gslab{i}") for i in range(2)]
    acc_fc = gsc[:, 0:1]; acc_pr = gsc[:, 1:2]; tmp0 = gsc[:, 2:3]

    def gamma_pass(wd, n_slabs, width, acc_col):
        for i in range(n_slabs):
            sa, sab = slab[i % 2]
            sh.dma("sync", lambda e, sa=sa, i=i, wd=wd, width=width:
                   e.dma_start(out=sa[:, 0:width], in_=wd[i * 128:(i + 1) * 128, :]),
                   writes=[sab])
            sh.op("vector", lambda e, sa=sa, width=width:
                  e.reduce_sum(tmp0, sa[:, 0:width], axis=AX.X, apply_absolute_value=True),
                  reads=[sab], writes=[gscb])
            if i == 0:
                sh.op("vector", lambda e, acc_col=acc_col:
                      e.tensor_scalar(acc_col, tmp0, 1.0, None, ALU.mult),
                      reads=[gscb], writes=[gscb])
            else:
                sh.op("vector", lambda e, acc_col=acc_col:
                      e.tensor_add(acc_col, acc_col, tmp0),
                      reads=[gscb], writes=[gscb])

    gamma_pass(wfc, DC, FF, acc_fc)
    gamma_pass(wproj, FC, D, acc_pr)


    for idx, (acc_col, nelem, gb, gbb) in enumerate(
            ((acc_fc, D * FF, gb_fc, gb_fcb), (acc_pr, FF * D, gb_pr, gb_prb))):
        bank, bankb = pb[idx % 2], pbb[idx % 2]
        sh.op("tensor", lambda e, bank=bank, acc_col=acc_col:
              e.transpose(bank[0:1, 0:128], acc_col, identf),
              reads=[gscb, identf_b], writes=[bankb])
        r0 = grow[0:1, idx * 4:idx * 4 + 1]
        r1 = grow[0:1, idx * 4 + 1:idx * 4 + 2]
        r2 = grow[0:1, idx * 4 + 2:idx * 4 + 3]
        sh.op("vector", lambda e, bank=bank, r0=r0:
              e.reduce_sum(r0, bank[0:1, 0:128], axis=AX.X),
              reads=[bankb], writes=[growb])
        sh.op("vector", lambda e, r0=r0, r1=r1, nelem=nelem:
              e.tensor_scalar(r1, r0, 1.0 / nelem, None, ALU.mult),
              reads=[growb], writes=[growb])
        sh.op("vector", lambda e, r1=r1, r2=r2:
              e.tensor_scalar(r2, r1, 1e-6, 0.5, ALU.add, ALU.mult),
              reads=[growb], writes=[growb])
        bank2, bank2b = pb[2 + idx], pbb[2 + idx]
        sh.op("tensor", lambda e, bank2=bank2, idx=idx:
              e.matmul(bank2[:, 0:2], onesf[0:1, 0:128], grow[0:1, idx * 4 + 1:idx * 4 + 3],
                       start=True, stop=True),
              reads=[onesf_b, growb], writes=[bank2b])
        sh.op("scalar", lambda e, gb=gb, bank2=bank2: e.copy(gb[:, 0:2], bank2[:, 0:2]),
              reads=[bank2b], writes=[gbb])
        sh.op("vector", lambda e, gb=gb:
              e.tensor_scalar(gb[:, 2:3], gb[:, 1:2], -1.0, None, ALU.mult),
              reads=[gbb], writes=[gbb])
    sb.pop(); sf.pop()
    sh.barrier()

    sb.push(); sf.push()
    xhat_fm = []
    for c in range(DC):
        ap, bb = sb.alloc(S // 2, f"xhfm{c}")
        xhat_fm.append((ap, bb))
    xt = [sf.alloc(D, f"fxt{i}") for i in range(2)]
    xq4 = [sb.alloc(D, f"fxq{i}") for i in range(4)]
    stats = [sf.alloc(16, f"fst{i}") for i in range(4)]
    junkf, junkfb = sf.alloc(D, "fjunk")
    tfc = []
    for d in range(DC):
        ap, bb = sb.alloc(512, f"tfc{d}")
        tfc.append((ap, bb))
    wtile = [sf.alloc(512, f"fwt{i}") for i in range(2)]
    m1t = [sf.alloc(512, f"fm1{i}") for i in range(2)]
    hstg = [sf.alloc(512, f"fhs{i}") for i in range(4)]
    hbf = [sb.alloc(512, f"fhb{i}") for i in range(4)]
    hjunk, hjunkb = sf.alloc(512, "fhjunk")
    hps = [sf.alloc(4, f"fhp{i}") for i in range(4)]
    NOB = FF // 512
    HTT = TT // 2

    for half in range(2):
        t0h = half * HTT
        for tl in range(HTT):
            t = t0h + tl
            xa, xab = xt[t % 2]
            qa, qab = xq4[t % 4]
            sa, sab = stats[t % 4]
            sh.dma("scalar", lambda e, xa=xa, t=t: e.dma_start(out=xa, in_=x_in[t * 128:(t + 1) * 128, :]),
                   reads=[xin_b], writes=[xab])
            emit_ln_stats(sh, xa, xab, D, 1e-5, junkf, junkfb, sa, sab)
            sh.op("vector", lambda e, sa=sa: e.tensor_scalar(sa[:, 7:8], sa[:, 5:6], -1e-5, None, ALU.add),
                  reads=[sab], writes=[sab])
            sh.op("vector", lambda e, sa=sa: e.tensor_scalar(sa[:, 8:9], sa[:, 3:4], sa[:, 3:4], None, ALU.mult),
                  reads=[sab], writes=[sab])
            sh.op("vector", lambda e, sa=sa: e.tensor_scalar(sa[:, 8:9], sa[:, 8:9], sa[:, 7:8], None, ALU.mult),
                  reads=[sab], writes=[sab])
            sh.op("vector", lambda e, sa=sa: e.tensor_scalar(sa[:, 8:9], sa[:, 8:9], 1e-6, None, ALU.add),
                  reads=[sab], writes=[sab])
            sh.op("scalar", lambda e, sa=sa: e.activation(sa[:, 9:10], sa[:, 8:9], AF.Sqrt),
                  reads=[sab], writes=[sab])
            sh.op("vector", lambda e, sa=sa: e.reciprocal(sa[:, 10:11], sa[:, 9:10]),
                  reads=[sab], writes=[sab])
            sh.op("vector", lambda e, sa=sa: e.tensor_scalar(sa[:, 11:12], sa[:, 3:4], sa[:, 10:11], None, ALU.mult),
                  reads=[sab], writes=[sab])
            sh.op("vector", lambda e, sa=sa: e.tensor_scalar(sa[:, 12:13], sa[:, 2:3], sa[:, 11:12], -1.0, ALU.mult, ALU.mult),
                  reads=[sab], writes=[sab])
            sh.op("scalar", lambda e, xa=xa, sa=sa:
                  e.activation(junkf, xa, AF.Identity, bias=sa[:, 12:13], scale=sa[:, 11:12]),
                  reads=[xab, sab], writes=[junkfb])
            sh.op("vector", lambda e, sa=sa: e.reduce_max(sa[:, 13:14], junkf, axis=AX.X, apply_absolute_value=True),
                  reads=[junkfb], writes=[sab])
            sh.op("vector", lambda e, sa=sa: e.reciprocal(sa[:, 14:15], sa[:, 13:14]),
                  reads=[sab], writes=[sab])
            sh.op("vector", lambda e, qa=qa, sa=sa:
                  e.tensor_scalar(qa, junkf, sa[:, 14:15], 128.0, ALU.mult, ALU.mult),
                  reads=[junkfb, sab], writes=[qab])
            sh.op("vector", lambda e, qa=qa:
                  e.tensor_scalar(qa, qa, CLIP, -CLIP, ALU.min, ALU.max),
                  reads=[qab], writes=[qab])
            sh.op("vector", lambda e, sa=sa, t=t:
                  e.tensor_scalar(sct[:, t:t + 1], sa[:, 13:14], gb_fc[:, 0:1], 0.0078125, ALU.mult, ALU.mult),
                  reads=[sab, gb_fcb], writes=[sctb])
            if t % 4 == 3:
                for c in range(DC):
                    bank, bankb = ptp[c % 2], ptb[c % 2]
                    for j in range(4):
                        src2, src2b = xq4[j]
                        sh.op("tensor",
                              lambda e, bank=bank, j=j, src2=src2, c=c:
                              e.transpose(bank[:, j * 128:(j + 1) * 128],
                                          src2[:, c * 128:(c + 1) * 128], ident),
                              reads=[src2b, ident_b], writes=[bankb])
                    dst, dstb = xhat_fm[c]
                    col0 = (t - 3 - t0h) * 128
                    sh.op("scalar", lambda e, dst=dst, bank=bank, col0=col0:
                          e.copy(dst[:, col0:col0 + 512], bank[:, 0:512]),
                          reads=[bankb], writes=[dstb])

        for o in range(NOB):
            for d in range(DC):
                wa, wab = wtile[d % 2]
                ma, mab = m1t[d % 2]
                ta, tab = tfc[d]
                sh.dma("sync", lambda e, wa=wa, d=d, o=o:
                       e.dma_start(out=wa, in_=wfc[d * 128:(d + 1) * 128, o * 512:(o + 1) * 512]),
                       writes=[wab])
                sh.op("vector", lambda e, ma=ma, wa=wa:
                      e.tensor_scalar(ma, wa, gb_fc[:, 1:2], None, ALU.is_gt),
                      reads=[wab, gb_fcb], writes=[mab])
                sh.op("vector", lambda e, wa=wa:
                      e.tensor_scalar(wa, wa, gb_fc[:, 2:3], None, ALU.is_lt),
                      reads=[wab, gb_fcb], writes=[wab])
                sh.op("vector", lambda e, ta=ta, ma=ma, wa=wa: e.tensor_sub(ta, ma, wa),
                      reads=[mab, wab], writes=[tab])
            for tl in range(HTT):
                t = t0h + tl
                bank, bankb = pb[t % 4], pbb[t % 4]
                for d in range(DC):
                    xa, xab = xhat_fm[d]
                    ta, tab = tfc[d]
                    sh.op("tensor",
                          lambda e, bank=bank, xa=xa, ta=ta, tl=tl, d=d:
                          e.matmul(bank[:], xa[:, tl * 128:(tl + 1) * 128], ta,
                                   start=(d == 0), stop=(d == DC - 1)),
                          reads=[xab, tab], writes=[bankb])
                hs, hsb2 = hstg[t % 4]
                hba, hbab = hbf[t % 4]
                hp, hpb = hps[t % 4]
                sh.op("vector", lambda e, hs=hs, bank=bank, t=t:
                      e.tensor_scalar(hs, bank[:], sct[:, t:t + 1], None, ALU.mult),
                      reads=[bankb, sctb], writes=[hsb2])
                sh.op("scalar", lambda e, hba=hba, hs=hs, hp=hp:
                      e.activation(hba, hs, AF.Gelu, accum_out=hp[:, 0:1]),
                      reads=[hsb2], writes=[hbab, hpb])
                sh.op("scalar", lambda e, hba=hba, hp=hp:
                      e.activation(hjunk, hba, AF.Square, accum_out=hp[:, 1:2]),
                      reads=[hbab], writes=[hjunkb, hpb])
                sh.op("vector", lambda e, hba=hba, hp=hp:
                      e.reduce_max(hp[:, 2:3], hba, axis=AX.X),
                      reads=[hbab], writes=[hpb])
                sh.op("vector", lambda e, hba=hba, hp=hp:
                      e.tensor_reduce(hp[:, 3:4], hba, axis=AX.X, op=ALU.min),
                      reads=[hbab], writes=[hpb])
                if o == 0:
                    for dstc, dstbb, col in ((hsum, hsumb, 0), (hsq, hsqb, 1),
                                             (hmax, hmaxb, 2), (hmin, hminb, 3)):
                        sh.op("vector", lambda e, dstc=dstc, hp=hp, t=t, col=col:
                              e.tensor_scalar(dstc[:, t:t + 1], hp[:, col:col + 1], 1.0, None, ALU.mult),
                              reads=[hpb], writes=[dstbb])
                else:
                    sh.op("vector", lambda e, hp=hp, t=t:
                          e.tensor_add(hsum[:, t:t + 1], hsum[:, t:t + 1], hp[:, 0:1]),
                          reads=[hpb, hsumb], writes=[hsumb])
                    sh.op("vector", lambda e, hp=hp, t=t:
                          e.tensor_add(hsq[:, t:t + 1], hsq[:, t:t + 1], hp[:, 1:2]),
                          reads=[hpb, hsqb], writes=[hsqb])
                    sh.op("vector", lambda e, hp=hp, t=t:
                          e.tensor_max(hmax[:, t:t + 1], hmax[:, t:t + 1], hp[:, 2:3]),
                          reads=[hpb, hmaxb], writes=[hmaxb])
                    sh.op("vector", lambda e, hp=hp, t=t:
                          e.tensor_tensor(hmin[:, t:t + 1], hmin[:, t:t + 1], hp[:, 3:4], ALU.min),
                          reads=[hpb, hminb], writes=[hminb])
                sh.dma("sync", lambda e, hba=hba, t=t, o=o:
                       e.dma_start(out=h_tm[t * 128:(t + 1) * 128, o * 512:(o + 1) * 512], in_=hba),
                       reads=[hbab], writes=[hb])
    sb.pop(); sf.pop()

    sb.push(); sf.push()
    pslab = [sf.alloc(D, f"ps{i}") for i in range(2)]
    pmm = [sf.alloc(D, f"pm{i}") for i in range(2)]
    ptt = [sb.alloc(D, f"ptt{i}") for i in range(2)]
    for i in range(FC):
        wa, wab = pslab[i % 2]
        ma, mab = pmm[i % 2]
        ta, tab = ptt[i % 2]
        sh.dma("sync", lambda e, wa=wa, i=i:
               e.dma_start(out=wa, in_=wproj[i * 128:(i + 1) * 128, :]),
               writes=[wab])
        sh.op("vector", lambda e, ma=ma, wa=wa:
              e.tensor_scalar(ma, wa, gb_pr[:, 1:2], None, ALU.is_gt),
              reads=[wab, gb_prb], writes=[mab])
        sh.op("vector", lambda e, wa=wa:
              e.tensor_scalar(wa, wa, gb_pr[:, 2:3], None, ALU.is_lt),
              reads=[wab, gb_prb], writes=[wab])
        sh.op("vector", lambda e, ta=ta, ma=ma, wa=wa: e.tensor_sub(ta, ma, wa),
              reads=[mab, wab], writes=[tab])
        sh.dma("sync", lambda e, ta=ta, i=i:
               e.dma_start(out=t_proj[i * 128:(i + 1) * 128, :], in_=ta),
               reads=[tab], writes=[tpb])
    sb.pop(); sf.pop()

    sb.push(); sf.push()
    hta = [sb.alloc(FF, f"dht{i}") for i in range(2)]
    ya, yab = sf.alloc(FF, "dy")
    xq2 = [sb.alloc(FF, f"dxq{i}") for i in range(2)]
    dstt = [sf.alloc(16, f"dst{i}") for i in range(2)]
    d2stg = [sb.alloc(512, f"d2s{i}") for i in range(4)]
    for t in range(TT):
        ha, hab = hta[t % 2]
        qa, qab = xq2[t % 2]
        sa, sab = dstt[t % 2]
        sh.dma("sync", lambda e, ha=ha, t=t: e.dma_start(out=ha, in_=h_tm[t * 128:(t + 1) * 128, :]),
               reads=[hb], writes=[hab])
        sh.op("vector", lambda e, sa=sa, t=t: e.tensor_scalar(sa[:, 0:1], hsum[:, t:t + 1], 1.0 / FF, None, ALU.mult),
              reads=[hsumb], writes=[sab])
        sh.op("vector", lambda e, sa=sa: e.tensor_scalar(sa[:, 1:2], sa[:, 0:1], sa[:, 0:1], None, ALU.mult),
              reads=[sab], writes=[sab])
        sh.op("vector", lambda e, sa=sa, t=t: e.tensor_scalar(sa[:, 2:3], hsq[:, t:t + 1], 1.0 / FF, sa[:, 1:2], ALU.mult, ALU.subtract),
              reads=[hsqb, sab], writes=[sab])
        sh.op("vector", lambda e, sa=sa: e.tensor_scalar(sa[:, 2:3], sa[:, 2:3], 1e-6, None, ALU.add),
              reads=[sab], writes=[sab])
        sh.op("scalar", lambda e, sa=sa: e.activation(sa[:, 1:2], sa[:, 2:3], AF.Sqrt),
              reads=[sab], writes=[sab])
        sh.op("vector", lambda e, sa=sa: e.reciprocal(sa[:, 2:3], sa[:, 1:2]),
              reads=[sab], writes=[sab])
        sh.op("vector", lambda e, sa=sa, t=t: e.tensor_scalar(sa[:, 3:4], hmax[:, t:t + 1], sa[:, 0:1], None, ALU.subtract),
              reads=[hmaxb, sab], writes=[sab])
        sh.op("vector", lambda e, sa=sa, t=t: e.tensor_scalar(sa[:, 4:5], hmin[:, t:t + 1], -1.0, sa[:, 0:1], ALU.mult, ALU.add),
              reads=[hminb, sab], writes=[sab])
        sh.op("vector", lambda e, sa=sa: e.tensor_tensor(sa[:, 5:6], sa[:, 3:4], sa[:, 4:5], ALU.max),
              reads=[sab], writes=[sab])
        sh.op("vector", lambda e, sa=sa: e.tensor_scalar(sa[:, 5:6], sa[:, 5:6], sa[:, 2:3], None, ALU.mult),
              reads=[sab], writes=[sab])
        sh.op("vector", lambda e, sa=sa: e.reciprocal(sa[:, 6:7], sa[:, 5:6]),
              reads=[sab], writes=[sab])
        sh.op("vector", lambda e, sa=sa: e.tensor_scalar(sa[:, 6:7], sa[:, 6:7], sa[:, 2:3], 128.0, ALU.mult, ALU.mult),
              reads=[sab], writes=[sab])
        sh.op("vector", lambda e, sa=sa: e.tensor_scalar(sa[:, 7:8], sa[:, 0:1], sa[:, 6:7], -1.0, ALU.mult, ALU.mult),
              reads=[sab], writes=[sab])
        sh.op("vector", lambda e, sa=sa, t=t:
              e.tensor_scalar(sc2[:, t:t + 1], sa[:, 5:6], gb_pr[:, 0:1], 0.0078125, ALU.mult, ALU.mult),
              reads=[sab, gb_prb], writes=[sc2b])
        sh.op("scalar", lambda e, ha=ha, ya=ya, sa=sa:
              e.activation(ya, ha, AF.Identity, bias=sa[:, 7:8], scale=sa[:, 6:7]),
              reads=[hab, sab], writes=[yab])
        sh.op("vector", lambda e, qa=qa, ya=ya:
              e.tensor_scalar(qa, ya, CLIP, -CLIP, ALU.min, ALU.max),
              reads=[yab], writes=[qab])
        for c4 in range(FC // 4):
            bank, bankb = ptp[c4 % 2], ptb[c4 % 2]
            for j in range(4):
                sh.op("tensor", lambda e, bank=bank, qa=qa, c=c4 * 4 + j, j=j:
                      e.transpose(bank[:, j * 128:(j + 1) * 128],
                                  qa[:, c * 128:(c + 1) * 128], ident),
                      reads=[qab, ident_b], writes=[bankb])
            sg, sgb = d2stg[c4 % 4]
            sh.op("scalar", lambda e, sg=sg, bank=bank: e.copy(sg, bank[:, 0:512]),
                  reads=[bankb], writes=[sgb])
            sh.dma("sync", lambda e, sg=sg, c4=c4, t=t:
                   e.dma_start(out=xh2_fm[:, c4 * 4:(c4 + 1) * 4, t * 128:(t + 1) * 128],
                               in_=sg.rearrange("p (c k) -> p c k", k=128)),
                   reads=[sgb], writes=[x2fb])
    sb.pop(); sf.pop()

    sb.push(); sf.push()
    tpsb_shared = Buf("tpsb_g")
    tpsb = []
    for oc in range(FC):
        ap, b = sb.alloc(512, f"tp{oc}", buf=tpsb_shared)
        tpsb.append((ap, tpsb_shared))
    xf = [sb.alloc(FF, f"exf{i}") for i in range(2)]
    ostg = [sf.alloc(512, f"eos{i}") for i in range(4)]
    for dg in range(D // 512):
        for oc in range(FC):
            ta, tab = tpsb[oc]
            sh.dma("sync", lambda e, ta=ta, oc=oc, dg=dg:
                   e.dma_start(out=ta, in_=t_proj[oc * 128:(oc + 1) * 128, dg * 512:(dg + 1) * 512]),
                   reads=[tpb], writes=[tab])
        for t in range(TT):
            xa, xab = xf[t % 2]
            pass
            sh.dma("gpsimd", lambda e, xa=xa, t=t:
                   e.dma_start(out=xa.rearrange("p (c k) -> p c k", k=128), in_=xh2_fm[:, :, t * 128:(t + 1) * 128]),
                   reads=[x2fb], writes=[xab])
            bank, bankb = pb[t % 4], pbb[t % 4]
            for oc in range(FC):
                ta, tab = tpsb[oc]
                sh.op("tensor",
                      lambda e, bank=bank, xa=xa, ta=ta, oc=oc:
                      e.matmul(bank[:], xa[:, oc * 128:(oc + 1) * 128], ta,
                               start=(oc == 0), stop=(oc == FC - 1)),
                      reads=[xab, tab], writes=[bankb])
            sg, sgb = ostg[t % 4]
            sh.op("vector", lambda e, sg=sg, bank=bank, t=t:
                  e.tensor_scalar(sg, bank[:], sc2[:, t:t + 1], None, ALU.mult),
                  reads=[bankb, sc2b], writes=[sgb])
            sh.dma("sync", lambda e, sg=sg, t=t, dg=dg:
                   e.dma_start(out=out_proj[t * 128:(t + 1) * 128, dg * 512:(dg + 1) * 512], in_=sg),
                   reads=[sgb], writes=[opb])
    sb.pop(); sf.pop()

    sb.push(); sf.push()
    ot = [sf.alloc(D, f"fot{i}") for i in range(2)]
    res = [sf.alloc(D, f"fres{i}") for i in range(2)]
    ost = [sf.alloc(8, f"fost{i}") for i in range(2)]
    junk2, junk2b = sb.alloc(D, "fjunk2")
    for t in range(TT):
        oa, oab = ot[t % 2]
        xa2, xab2 = res[t % 2]
        ra, rab = ost[t % 2]
        sh.dma("sync", lambda e, oa=oa, t=t: e.dma_start(out=oa, in_=out_proj[t * 128:(t + 1) * 128, :]),
               reads=[opb], writes=[oab])
        sh.dma("scalar", lambda e, xa2=xa2, t=t: e.dma_start(out=xa2, in_=x_in[t * 128:(t + 1) * 128, :]),
               reads=[xin_b], writes=[xab2])
        emit_ln_stats(sh, oa, oab, D, 1e-5, junk2, junk2b, ra, rab)
        sh.op("vector", lambda e, oa=oa, ra=ra:
              e.tensor_scalar(oa, oa, ra[:, 2:3], ra[:, 3:4], ALU.subtract, ALU.mult),
              reads=[oab, rab], writes=[oab])
        sh.op("vector", lambda e, oa=oa, xa2=xa2: e.tensor_add(oa, oa, xa2),
              reads=[oab, xab2], writes=[oab])
        sh.dma("sync", lambda e, oa=oa, t=t: e.dma_start(out=x_out[t * 128:(t + 1) * 128, :], in_=oa),
               reads=[oab], writes=[xout_b])
    sb.pop(); sf.pop()
    sb.pop(); sf.pop()


def emit_moe(ctx, x_in, xin_b, y_out, yout_b):
    from concourse.bass_core_rust import IndirectOffsetOnAxis
    sh, sb, sf = ctx["sh"], ctx["sb"], ctx["sf"]
    W, dsc, dbuf = ctx["W"], ctx["dsc"], ctx["dbuf"]
    ident, ident_b = ctx["ident"], ctx["ident_b"]
    identf, identf_b = ctx["identf"], ctx["identf_b"]
    onesf, onesf_b = ctx["onesf"], ctx["onesf_b"]
    pb, pbb, ptp, ptb = ctx["pb"], ctx["pbb"], ctx["ptp"], ctx["ptb"]
    u_d, udb = dsc["u_d"], dbuf["u_d"]
    g_d, gdb = dsc["g_d"], dbuf["g_d"]
    acc_d, accb = dsc["acc_d"], dbuf["acc_d"]
    OT = OWN // 128
    NT5 = OWN // 512

    sb.push(); sf.push()
    rsel, rselb = sf.alloc(OT, "rsel")
    probs_all, probsb = sf.alloc(OT * 8, "probs")
    psT, psTb = sf.alloc(OT * 128, "psT")
    xn_d, xn_db = dsc["xn_fm"], Buf("moe_xn")
    bfm = {}
    for nm in ("bup", "bgate"):
        for e in range(E):
            ap, b = sf.alloc(DC, f"{nm}{e}")
            bfm[(nm, e)] = (ap, b)
    for nm in ("bsu", "bsg"):
        ap, b = sf.alloc(DC, nm)
        bfm[nm] = (ap, b)
    bdall, bdallb = sf.alloc(D, "bdall")
    wrs, wrsb = sb.alloc(DC * E, "wrs")
    brg, brgb = sf.alloc(E, "brg")

    nc = ctx["nc"]

    rsel_i = ctx["rsel_t"][:, 0:OT]
    rsel_ib = Buf("rsel_i")
    sh.dma("sync", lambda e: e.dma_start(out=rsel_i, in_=W["rowsel"][:, :]), writes=[rsel_ib])

    def load_bias_fm(src_ap, dst, dstb, tag):
        stg, stgb = sf.alloc(128, f"bstg{tag}")
        sh.dma("sync", lambda e, stg=stg: e.dma_start(out=stg[0:DC, 0:128], in_=src_ap),
               writes=[stgb])
        bank, bankb = pb[0], pbb[0]
        sh.op("tensor", lambda e, bank=bank, stg=stg:
              e.transpose(bank[:, 0:DC], stg[0:DC, 0:128], identf[0:DC, 0:DC]),
              reads=[stgb, identf_b], writes=[bankb])
        sh.op("scalar", lambda e, dst=dst, bank=bank: e.copy(dst, bank[:, 0:DC]),
              reads=[bankb], writes=[dstb])

    for e in range(E):
        ap, b = bfm[("bup", e)]
        load_bias_fm(W["bup"][e], ap, b, f"up{e}")
        ap, b = bfm[("bgate", e)]
        load_bias_fm(W["bgate"][e], ap, b, f"g{e}")
    load_bias_fm(W["bsu"], bfm["bsu"][0], bfm["bsu"][1], "su")
    load_bias_fm(W["bsg"], bfm["bsg"][0], bfm["bsg"][1], "sg")
    sh.dma("sync", lambda e: e.dma_start(out=bdall[0:E, :], in_=W["bdown_r"][:, :]), writes=[bdallb])
    sh.dma("scalar", lambda e: e.dma_start(out=bdall[E:E + 1, :], in_=W["bsd"][:, :]), writes=[bdallb])
    for c in range(DC):
        sh.dma("gpsimd", lambda e, c=c: e.dma_start(out=wrs[:, c * E:(c + 1) * E],
                                                    in_=W["wr"][c * 128:(c + 1) * 128, :]),
               writes=[wrsb])
    brrow, brrowb = sf.alloc(2 * E, "brrow")
    sh.dma("scalar", lambda e: e.dma_start(out=brrow[0:1, 0:E], in_=W["br"][:, :]), writes=[brrowb])
    sh.dma("scalar", lambda e: e.dma_start(out=brrow[0:1, E:2 * E], in_=W["gbias"][:, :]), writes=[brrowb])
    sh.op("vector", lambda e: e.tensor_add(brrow[0:1, 0:E], brrow[0:1, 0:E], brrow[0:1, E:2 * E]),
          reads=[brrowb], writes=[brrowb])
    bank, bankb = pb[1], pbb[1]
    sh.op("tensor", lambda e, bank=bank:
          e.matmul(bank[:, 0:E], onesf[0:1, 0:128], brrow[0:1, 0:E], start=True, stop=True),
          reads=[onesf_b, brrowb], writes=[bankb])
    sh.op("scalar", lambda e, bank=bank: e.copy(brg, bank[:, 0:E]),
          reads=[bankb], writes=[brgb])

    def emit_router_tile(t, tl):
        bank, bankb = pb[2 + t % 2], pbb[2 + t % 2]
        for c in range(DC):
            xa, xab = xnT[c]
            sh.op("tensor",
                  lambda e, bank=bank, xa=xa, c=c, tl=tl:
                  e.matmul(bank[:, 0:E], xa[:, tl * 128:(tl + 1) * 128], wrs[:, c * E:(c + 1) * E],
                           start=(c == 0), stop=(c == DC - 1)),
                  reads=[xab, wrsb], writes=[bankb])
        ra, rab = rt[t % 2]
        g0 = ra[:, 0:E]
        bg = ra[:, 7:7 + E]
        sel = ra[:, 14:14 + E]
        sh.op("vector", lambda e, g0=g0, bank=bank: e.tensor_scalar(g0, bank[:, 0:E], 1.0, None, ALU.mult),
              reads=[bankb], writes=[rab])
        sh.op("vector", lambda e, bg=bg, g0=g0: e.tensor_add(bg, g0, brg),
              reads=[rab, brgb], writes=[rab])
        sh.op("vector", lambda e, ra=ra, bg=bg: e.reduce_max(ra[:, 21:22], bg, axis=AX.X),
              reads=[rab], writes=[rab])
        sh.op("vector", lambda e, ra=ra, sel=sel, bg=bg:
              e.tensor_scalar(sel, bg, ra[:, 21:22], None, ALU.is_ge),
              reads=[rab], writes=[rab])
        sh.op("vector", lambda e, ra=ra, sel=sel, bg=bg:
              e.scalar_tensor_tensor(sel, sel, NEG, bg, ALU.mult, ALU.add),
              reads=[rab], writes=[rab])
        sh.op("vector", lambda e, ra=ra, sel=sel: e.reduce_max(ra[:, 22:23], sel, axis=AX.X),
              reads=[rab], writes=[rab])
        sh.op("vector", lambda e, ra=ra, sel=sel, bg=bg:
              e.tensor_scalar(sel, bg, ra[:, 22:23], None, ALU.is_ge),
              reads=[rab], writes=[rab])
        gm = ra[:, 0:E]
        sh.op("vector", lambda e, sel=sel: e.tensor_scalar(sel, sel, -1.0, None, ALU.add),
              reads=[rab], writes=[rab])
        sh.op("vector", lambda e, gm=gm, sel=sel, g0=g0:
              e.scalar_tensor_tensor(gm, sel, -NEG, g0, ALU.mult, ALU.add),
              reads=[rab], writes=[rab])
        sh.op("vector", lambda e, ra=ra, gm=gm: e.reduce_max(ra[:, 21:22], gm, axis=AX.X),
              reads=[rab], writes=[rab])
        sh.op("vector", lambda e, ra=ra: e.tensor_scalar(ra[:, 22:23], ra[:, 21:22], -1.0, None, ALU.mult),
              reads=[rab], writes=[rab])
        pcol = probs_all[:, t * 8:t * 8 + E]
        sh.op("scalar", lambda e, pcol=pcol, gm=gm, ra=ra:
              e.activation(pcol, gm, AF.Exp, bias=ra[:, 22:23], scale=1.0, accum_out=ra[:, 23:24]),
              reads=[rab], writes=[probsb, rab])
        sh.op("vector", lambda e, ra=ra: e.reciprocal(ra[:, 21:22], ra[:, 23:24]),
              reads=[rab], writes=[rab])
        sh.op("vector", lambda e, pcol=pcol, ra=ra:
              e.tensor_scalar(pcol, pcol, ra[:, 21:22], None, ALU.mult),
              reads=[rab, probsb], writes=[probsb])
        sh.op("vector", lambda e, t=t:
              e.tensor_scalar(probs_all[:, t * 8 + E:t * 8 + 8], onesf[:, 0:8 - E], 1.0, None, ALU.mult),
              reads=[onesf_b], writes=[probsb])
        bank2, bank2b = ptp[0], ptb[0]
        bank3, bank3b = pb[4], pbb[4]
        sh.op("tensor", lambda e, bank3=bank3, t=t:
              e.transpose(bank3[0:8, 0:128], probs_all[:, t * 8:t * 8 + 8], identf),
              reads=[probsb, identf_b], writes=[bank3b])
        sh.op("scalar", lambda e, bank3=bank3, t=t:
              e.copy(psT[0:8, t * 128:(t + 1) * 128], bank3[0:8, 0:128]),
              reads=[bank3b], writes=[psTb])

    sb.push(); sf.push()
    xg = [sf.alloc(D, f"mxg{i}") for i in range(2)]
    xq4 = [sb.alloc(D, f"mxq{i}") for i in range(4)]
    stats = [sf.alloc(8, f"mst{i}") for i in range(4)]
    junkm, junkmb = sb.alloc(D, "mjunk")
    rt = [sf.alloc(24, f"mrt{i}") for i in range(2)]
    mstg = [sb.alloc(512, f"mstg{i}") for i in range(2)]

    for t in range(OT):
        xa, xab = xg[t % 2]
        qa, qab = xq4[t % 4]
        sa, sab = stats[t % 4]
        sh.dma("gpsimd", lambda e, xa=xa, t=t:
               e.indirect_dma_start(xa, None, x_in[:, :],
                                    IndirectOffsetOnAxis(ap=rsel_i[:, t:t + 1], axis=0)),
               reads=[xin_b, rsel_ib], writes=[xab])
        emit_ln_stats(sh, xa, xab, D, 1e-5, junkm, junkmb, sa, sab)
        sh.op("vector", lambda e, qa=qa, xa=xa, sa=sa:
              e.tensor_scalar(qa, xa, sa[:, 2:3], sa[:, 3:4], ALU.subtract, ALU.mult),
              reads=[xab, sab], writes=[qab])
        if t % 4 == 3:
            for c in range(DC):
                bank, bankb = ptp[c % 2], ptb[c % 2]
                for j in range(4):
                    src2, src2b = xq4[j]
                    sh.op("tensor",
                          lambda e, bank=bank, j=j, src2=src2, c=c:
                          e.transpose(bank[:, j * 128:(j + 1) * 128],
                                      src2[:, c * 128:(c + 1) * 128], ident),
                          reads=[src2b, ident_b], writes=[bankb])
                sgx, sgxb = mstg[c % 2]
                sh.op("scalar", lambda e, sgx=sgx, bank=bank:
                      e.copy(sgx, bank[:, 0:512]),
                      reads=[bankb], writes=[sgxb])
                col0 = (t - 3) * 128
                sh.dma("sync", lambda e, sgx=sgx, c=c, col0=col0:
                       e.dma_start(out=xn_d[c, :, col0:col0 + 512], in_=sgx),
                       reads=[sgxb], writes=[xn_db])

    rxn_shared = Buf("rxn_g")
    xnT = []
    for d in range(DC):
        ap, b = sb.alloc(512, f"rxn{d}", buf=rxn_shared)
        xnT.append((ap, rxn_shared))
    for T in range(NT5):
        for d in range(DC):
            xa, xab = xnT[d]
            sh.dma("sync", lambda e, xa=xa, d=d, T=T:
                   e.dma_start(out=xa, in_=xn_d[d, :, T * 512:(T + 1) * 512]),
                   reads=[xn_db], writes=[xab])
        for tl in range(4):
            t = T * 4 + tl
            emit_router_tile(t, tl)

    sb.pop(); sf.pop()

    wex = []
    sb.push(); sf.push()
    wex_shared = Buf("wex_g")
    for d in range(DC):
        ap, b = sb.alloc(D, f"mw{d}", buf=wex_shared)
        wex.append((ap, wex_shared))
    ustg = [sb.alloc(512, f"mus{i}") for i in range(4)]
    sstg = [sb.alloc(512, f"mss{i}") for i in range(4)]
    gstg = [sb.alloc(512, f"mgs{i}") for i in range(4)]
    uslab_shared = [Buf("uslab0"), Buf("uslab1")]
    uslab = [(sb.alloc(DC * 512, f"musl{i}", buf=uslab_shared[i])[0], uslab_shared[i])
             for i in range(2)]
    gcol = [sb.alloc(DC * 128, f"mgc{i}") for i in range(2)]
    astg = [sf.alloc(512, f"mas{i}") for i in range(4)]
    acin = [sf.alloc(512, f"mai{i}") for i in range(4)]

    def load_wexp(wd, e=None):
        for d in range(DC):
            wa, wab = wex[d]
            if e is None:
                sh.dma("gpsimd", lambda e2, wa=wa, d=d, wd=wd:
                       e2.dma_start(out=wa, in_=wd[d * 128:(d + 1) * 128, :]),
                       writes=[wab])
            else:
                sh.dma("gpsimd", lambda e2, wa=wa, d=d, wd=wd, e=e:
                       e2.dma_start(out=wa, in_=wd[e, d * 128:(d + 1) * 128, :]),
                       writes=[wab])

    def expert_pass(eidx):
        shared = eidx == E
        if shared:
            load_wexp(W["wsu"])
            bup_ap, bup_b = bfm["bsu"]
        else:
            load_wexp(W["wup"], eidx)
            bup_ap, bup_b = bfm[("bup", eidx)]
        for T in range(NT5):
            ua_all, ua_allb = uslab[T % 2]
            sh.dma("sync", lambda e2, ua_all=ua_all, T=T:
                   e2.dma_start(out=ua_all.rearrange("p (c k) -> p c k", k=512),
                                in_=xn_d[:, :, T * 512:(T + 1) * 512].rearrange("c p k -> p c k")),
                   reads=[xn_db], writes=[ua_allb])
            for oc in range(DC):
                bank, bankb = pb[(oc * NT5 + T) % 4], pbb[(oc * NT5 + T) % 4]
                for d in range(DC):
                    wa, wab = wex[d]
                    sh.op("tensor",
                          lambda e2, bank=bank, wa=wa, ua_all=ua_all, oc=oc, d=d:
                          e2.matmul(bank[:], wa[:, oc * 128:(oc + 1) * 128],
                                    ua_all[:, d * 512:(d + 1) * 512],
                                    start=(d == 0), stop=(d == DC - 1)),
                          reads=[wab, ua_allb], writes=[bankb])
                sg, sgb = ustg[(oc * NT5 + T) % 4]
                sh.op("scalar", lambda e2, sg=sg, bank=bank, bup_ap=bup_ap, oc=oc:
                      e2.activation(sg, bank[:], AF.Identity, bias=bup_ap[:, oc:oc + 1], scale=1.0),
                      reads=[bankb, bup_b], writes=[sgb])
                sh.dma("sync", lambda e2, sg=sg, oc=oc, T=T:
                       e2.dma_start(out=u_d[:, oc, T * 512:(T + 1) * 512], in_=sg),
                       reads=[sgb], writes=[udb])
        if shared:
            load_wexp(W["wsg"])
            bg_ap, bg_b = bfm["bsg"]
        else:
            load_wexp(W["wgate"], eidx)
            bg_ap, bg_b = bfm[("bgate", eidx)]
        for T in range(NT5):
            ua_all, ua_allb = uslab[T % 2]
            sh.dma("sync", lambda e, ua_all=ua_all, T=T:
                   e.dma_start(out=ua_all.rearrange("p (c k) -> p c k", k=512),
                               in_=u_d[:, :, T * 512:(T + 1) * 512]),
                   reads=[udb], writes=[ua_allb])
            for oc in range(DC):
                bank, bankb = pb[(oc * NT5 + T) % 4], pbb[(oc * NT5 + T) % 4]
                for d in range(DC):
                    wa, wab = wex[d]
                    sh.op("tensor",
                          lambda e, bank=bank, wa=wa, ua_all=ua_all, oc=oc, d=d:
                          e.matmul(bank[:], wa[:, oc * 128:(oc + 1) * 128],
                                   ua_all[:, d * 512:(d + 1) * 512],
                                   start=(d == 0), stop=(d == DC - 1)),
                          reads=[wab, ua_allb], writes=[bankb])
                sg, sgb = sstg[(oc * NT5 + T) % 4]
                sh.op("scalar", lambda e, sg=sg, bank=bank, bg_ap=bg_ap, oc=oc:
                      e.activation(sg, bank[:], AF.Silu, bias=bg_ap[:, oc:oc + 1], scale=1.0),
                      reads=[bankb, bg_b], writes=[sgb])
                ga, gab = gstg[(oc * NT5 + T) % 4]
                sh.op("vector", lambda e, ga=ga, sg=sg, ua_all=ua_all, oc=oc:
                      e.tensor_mul(ga, sg, ua_all[:, oc * 512:(oc + 1) * 512]),
                      reads=[sgb, ua_allb], writes=[gab])
                sh.dma("sync", lambda e, ga=ga, oc=oc, T=T:
                       e.dma_start(out=g_d[:, oc, T * 512:(T + 1) * 512], in_=ga),
                       reads=[gab], writes=[gdb])
        if shared:
            load_wexp(W["wsd"])
        else:
            load_wexp(W["wdown"], eidx)
        for t in range(OT):
            gc, gcb = gcol[t % 2]
            sh.dma("gpsimd", lambda e, gc=gc, t=t:
                   e.dma_start(out=gc.rearrange("p (c k) -> p c k", k=128),
                               in_=g_d[:, :, t * 128:(t + 1) * 128]),
                   reads=[gdb], writes=[gcb])
            for dg in range(D // 512):
                bank, bankb = pb[(t * 4 + dg) % 4], pbb[(t * 4 + dg) % 4]
                for jc in range(DC):
                    wa, wab = wex[jc]
                    sh.op("tensor",
                          lambda e, bank=bank, gc=gc, wa=wa, jc=jc, dg=dg:
                          e.matmul(bank[:], gc[:, jc * 128:(jc + 1) * 128],
                                   wa[:, dg * 512:(dg + 1) * 512],
                                   start=(jc == 0), stop=(jc == DC - 1)),
                          reads=[gcb, wab], writes=[bankb])
                sg, sgb = astg[(t * 4 + dg) % 4]
                if eidx == 0:
                    sh.op("vector", lambda e, sg=sg, bank=bank, t=t:
                          e.tensor_scalar(sg, bank[:], probs_all[:, t * 8:t * 8 + 1], None, ALU.mult),
                          reads=[bankb, probsb], writes=[sgb])
                else:
                    aci, acib = acin[(t * 4 + dg) % 4]
                    sh.dma("scalar", lambda e, aci=aci, t=t, dg=dg:
                           e.dma_start(out=aci, in_=acc_d[t * 128:(t + 1) * 128, dg * 512:(dg + 1) * 512]),
                           reads=[accb], writes=[acib])
                    if shared:
                        sh.op("vector", lambda e, sg=sg, bank=bank, aci=aci:
                              e.scalar_tensor_tensor(sg, bank[:], 1.0, aci, ALU.mult, ALU.add),
                              reads=[bankb, acib], writes=[sgb])
                    else:
                        sh.op("vector", lambda e, sg=sg, bank=bank, aci=aci, t=t, eidx=eidx:
                              e.scalar_tensor_tensor(sg, bank[:], probs_all[:, t * 8 + eidx:t * 8 + eidx + 1],
                                                     aci, ALU.mult, ALU.add),
                              reads=[bankb, acib, probsb], writes=[sgb])
                sh.dma("sync", lambda e, sg=sg, t=t, dg=dg:
                       e.dma_start(out=acc_d[t * 128:(t + 1) * 128, dg * 512:(dg + 1) * 512], in_=sg),
                       reads=[sgb], writes=[accb])

    for eidx in range(E + 1):
        expert_pass(eidx)
    sb.pop(); sf.pop()

    sb.push(); sf.push()
    ot2 = [sf.alloc(D, f"m3o{i}") for i in range(2)]
    xg2 = [sf.alloc(D, f"m3x{i}") for i in range(2)]
    ost = [sf.alloc(8, f"m3s{i}") for i in range(2)]
    junk3, junk3b = sb.alloc(D, "m3junk")
    for t in range(OT):
        oa, oab = ot2[t % 2]
        xa, xab = xg2[t % 2]
        ra, rab = ost[t % 2]
        sh.dma("gpsimd", lambda e, xa=xa, t=t:
               e.indirect_dma_start(xa, None, x_in[:, :],
                                    IndirectOffsetOnAxis(ap=rsel_i[:, t:t + 1], axis=0)),
               reads=[xin_b, rsel_ib], writes=[xab])
        sh.dma("sync", lambda e, oa=oa, t=t: e.dma_start(out=oa, in_=acc_d[t * 128:(t + 1) * 128, :]),
               reads=[accb], writes=[oab])
        for dg in range(D // 512):
            bank, bankb = pb[dg % 2], pbb[dg % 2]
            sh.op("tensor",
                  lambda e, bank=bank, t=t, dg=dg:
                  e.matmul(bank[:], psT[0:8, t * 128:(t + 1) * 128],
                           bdall[0:8, dg * 512:(dg + 1) * 512], start=True, stop=True),
                  reads=[psTb, bdallb], writes=[bankb])
            sh.op("vector", lambda e, oa=oa, bank=bank, dg=dg:
                  e.tensor_add(oa[:, dg * 512:(dg + 1) * 512], oa[:, dg * 512:(dg + 1) * 512], bank[:]),
                  reads=[bankb, oab], writes=[oab])
        emit_ln_stats(sh, oa, oab, D, 1e-5, junk3, junk3b, ra, rab)
        sh.op("vector", lambda e, oa=oa, ra=ra:
              e.tensor_scalar(oa, oa, ra[:, 2:3], ra[:, 3:4], ALU.subtract, ALU.mult),
              reads=[oab, rab], writes=[oab])
        sh.op("vector", lambda e, oa=oa, xa=xa: e.tensor_add(oa, oa, xa),
              reads=[oab, xab], writes=[oab])
        sh.dma("sync", lambda e, oa=oa, t=t: e.dma_start(out=y_out[t * 128:(t + 1) * 128, :], in_=oa),
               reads=[oab], writes=[yout_b])
    sb.pop(); sf.pop()
    sb.pop(); sf.pop()



def make_consts():
    ident = np.eye(128, dtype=ml_dtypes.bfloat16)
    ones = np.ones((128, 128), dtype=ml_dtypes.bfloat16)
    col = np.arange(896, dtype=np.float32)[None, :]
    row = np.arange(128, dtype=np.float32)[:, None]
    mask = np.where(col <= row + 384, 0.0, NEG).astype(np.float32)
    return ident, ones, mask


ZIG0 = np.concatenate([np.arange(1024), np.arange(3072, 4096)]).astype(np.int32)
ZIG1 = np.arange(1024, 3072, dtype=np.int32)
FUSED = True


def _prep_host(inputs):
    ident, ones, mask = make_consts()
    identf = np.eye(128, dtype=np.float32)
    onesf = np.ones((128, 128), dtype=np.float32)
    T = lambda a: np.ascontiguousarray(np.asarray(a).T)
    A = lambda a: np.ascontiguousarray(np.asarray(a))
    shared = {
        "identc": ident, "onesc": ones, "maskc": mask,
        "identfc": identf, "onesfc": onesf,
        "wq": T(inputs["L0_attn_q_proj"]), "wk": T(inputs["L0_attn_k_proj"]),
        "wv": T(inputs["L0_attn_v_proj"]), "wo": T(inputs["L0_attn_o_proj"]),
        "wfc": T(inputs["L0_ffn_w_fc"]), "wproj": T(inputs["L0_ffn_w_proj"]),
        "wqkv": T(inputs["L1_attn_w_qkv"]), "wo1": T(inputs["L1_attn_w_o"]),
        "wup": A(inputs["L1_ffn_Wup"]), "wgate": A(inputs["L1_ffn_Wgate"]),
        "wdown": A(inputs["L1_ffn_Wdown"]),
        "wsu": A(inputs["L1_ffn_Wsu"]), "wsg": A(inputs["L1_ffn_Wsg"]),
        "wsd": A(inputs["L1_ffn_Wsd"]), "wr": A(inputs["L1_ffn_Wr"]),
        "br": A(inputs["L1_ffn_br"]).reshape(1, E),
        "gbias": A(inputs["L1_ffn_gate_bias"]).reshape(1, E),
        "bup": A(inputs["L1_ffn_bup"]).reshape(E, DC, 128),
        "bgate": A(inputs["L1_ffn_bgate"]).reshape(E, DC, 128),
        "bdown": A(inputs["L1_ffn_bdown"]).reshape(E, DC, 128),
        "bdown_r": A(inputs["L1_ffn_bdown"]),
        "bsu": A(inputs["L1_ffn_bsu"]).reshape(DC, 128),
        "bsg": A(inputs["L1_ffn_bsg"]).reshape(DC, 128),
        "bsd": A(inputs["L1_ffn_bsd"]).reshape(1, D),
    }
    x_full = np.asarray(inputs["L0_attn_in_features"])
    per_core = []
    for core in range(8):
        b, sub = core // 2, core % 2
        rows = ZIG0 if sub == 0 else ZIG1
        m = dict(shared)
        m["x0"] = np.ascontiguousarray(x_full[b])
        m["rowsel"] = np.ascontiguousarray(rows.reshape(OWN // 128, 128).T)
        per_core.append(m)
    return per_core


_BLOCK_KEYS = {
    "a0": ["x0", "identc", "onesc", "maskc", "identfc", "onesfc",
           "wq", "wk", "wv", "wo"],
    "f0": ["identc", "onesc", "maskc", "identfc", "onesfc", "wfc", "wproj"],
    "a1": ["identc", "onesc", "maskc", "identfc", "onesfc", "wqkv", "wo1"],
    "moe": ["identc", "onesc", "maskc", "identfc", "onesfc",
            "wup", "wgate", "wdown", "wsu", "wsg", "wsd", "wr", "br", "gbias",
            "bup", "bgate", "bdown", "bdown_r", "bsu", "bsg", "bsd", "rowsel"],
}


def kernel(**inputs):
    per_core = _prep_host(inputs)
    if FUSED:
        nc = build_program(blocks={"a0", "f0", "a1", "moe"}, boundaries={})
        keys = sorted(set(sum(_BLOCK_KEYS.values(), [])))
        in_maps = [{k: per_core[core][k] for k in keys} for core in range(8)]
        res = run_bass_kernel_spmd(nc, in_maps, core_ids=list(range(8)),
                                   trace_cores=[0])
        out = np.empty((4, S, D), dtype=np.float32)
        for core in range(8):
            b, sub = core // 2, core % 2
            rows = ZIG0 if sub == 0 else ZIG1
            out[b, rows, :] = np.asarray(res.results[core]["y"])
        return out
    carry = [dict() for _ in range(8)]
    stage_io = [
        ("a0", {"x1": "out"}, ["x1"]),
        ("f0", {"x1": "in", "x2": "out"}, ["x2"]),
        ("a1", {"x2": "in", "x3": "out"}, ["x3"]),
        ("moe", {"x3": "in"}, ["y"]),
    ]
    for blk, bnd, outs in stage_io:
        nc = build_program(blocks={blk}, boundaries=bnd)
        in_maps = []
        for core in range(8):
            m = {k: per_core[core][k] for k in _BLOCK_KEYS[blk]}
            for k, v in bnd.items():
                if v == "in":
                    m[k] = carry[core][k]
            in_maps.append(m)
        res = run_bass_kernel_spmd(nc, in_maps, core_ids=list(range(8)))
        for core in range(8):
            for k in outs:
                carry[core][k] = np.ascontiguousarray(res.results[core][k])
    out = np.empty((4, S, D), dtype=np.float32)
    for core in range(8):
        b, sub = core // 2, core % 2
        rows = ZIG0 if sub == 0 else ZIG1
        out[b, rows, :] = carry[core]["y"]
    return out


# version 12 (final)
# speedup vs baseline: 1.3691x; 1.3691x over previous
import contextlib
import numpy as np
import ml_dtypes
import concourse.bass as bass
import concourse.mybir as mybir
from concourse.bass_utils import run_bass_kernel_spmd

F32 = mybir.dt.float32
BF16 = mybir.dt.bfloat16
I32 = mybir.dt.int32
AF = mybir.ActivationFunctionType
ALU = mybir.AluOpType
AX = mybir.AxisListType

D = 2048
S = 4096
HEADS = 16
DK = 128
DC = D // 128
TT = S // 128
FF = 8192
FC = FF // 128
E = 7
OWN = 2048
NEG = -1.0e9
ISQ = 0.08838834764831845

ENGS = ("tensor", "vector", "scalar", "sync", "gpsimd")
DMA_ENGS = ("sync", "gpsimd", "scalar")



class Op:
    __slots__ = ("eng", "fn", "deps", "seq", "is_dma", "slot", "slot_ord",
                 "mark_ord", "is_barrier", "dma_snapshot")

    def __init__(self, eng, fn, is_dma, seq):
        self.eng = eng
        self.fn = fn
        self.deps = []
        self.seq = seq
        self.is_dma = is_dma
        self.slot = None
        self.slot_ord = 0
        self.mark_ord = None
        self.is_barrier = False
        self.dma_snapshot = None


class Buf:
    __slots__ = ("w", "r", "name", "dma_cnt", "sem_idx", "qclass")

    def __init__(self, name=""):
        self.w = None
        self.r = []
        self.name = name
        self.dma_cnt = 0
        self.sem_idx = None
        self.qclass = None

    def rd(self):
        return [self.w] if self.w is not None else []

    def wd(self):
        d = list(self.r)
        if self.w is not None:
            d.append(self.w)
        return d


class Sched:

    def __init__(self, nc):
        self.nc = nc
        self.ops = []
        self.slots = []
        self.last_by_eng = {}

    def op(self, eng, fn, reads=(), writes=(), dma=False):
        o = Op(eng, fn, dma, len(self.ops))
        if dma:
            assert len(writes) == 1, "each DMA must write exactly one Buf"
            b = writes[0]
            qc = "sw" if eng == "gpsimd" else "hw"
            if b.qclass is None:
                b.qclass = qc
                self.slots.append(b)
            assert b.qclass == qc, f"buf {b.name} written from both SWDGE and HWDGE queues"
            b.dma_cnt += 1
            o.slot = b
            o.slot_ord = b.dma_cnt
        deps = []
        for b in reads:
            deps += b.rd()
        for b in writes:
            deps += b.wd()
        seen = set()
        for d in deps:
            if d is None or d.seq in seen:
                continue
            seen.add(d.seq)
            if d.eng == "tensor" and eng == "tensor" and not d.is_dma and not dma:
                continue
            o.deps.append(d)
        self.ops.append(o)
        self.last_by_eng[(eng, dma)] = o
        for b in reads:
            b.r.append(o)
        for b in writes:
            b.w = o
            b.r = []
        return o

    def dma(self, eng, fn, reads=(), writes=()):
        return self.op(eng, fn, reads, writes, dma=True)

    def barrier(self):
        lasts = [d for d in self.last_by_eng.values()
                 if d.fn is not None and not d.is_dma]
        snap = [(b, b.dma_cnt) for b in self.slots if b.dma_cnt > 0]
        for e in ENGS:
            o = Op(e, None, False, len(self.ops))
            o.is_barrier = True
            o.deps = [d for d in lasts if d.eng != e]
            o.dma_snapshot = snap
            self.ops.append(o)

    def finalize(self, stack):
        nc = self.nc
        csem = {e: stack.enter_context(nc.semaphore(f"c_{e}")) for e in ENGS}
        barrier_seqs = [o.seq for o in self.ops if o.is_barrier]
        first = {}
        last = {}
        for o in self.ops:
            if o.is_dma:
                b = o.slot
                first.setdefault(id(b), o.seq)
                last[id(b)] = o.seq
            for d in o.deps:
                if d.is_dma:
                    last[id(d.slot)] = o.seq
        import bisect
        slots_sorted = sorted((b for b in self.slots if id(b) in first),
                              key=lambda b: first[id(b)])
        sem_free_at = []
        for b in slots_sorted:
            fs = first[id(b)]
            got = None
            for i, (qc, fa) in enumerate(sem_free_at):
                if qc == b.qclass and fa is not None and fa <= fs:
                    got = i
                    break
            if got is None:
                got = len(sem_free_at)
                sem_free_at.append((b.qclass, None))
            b.sem_idx = got
            ls = last[id(b)]
            j = bisect.bisect_right(barrier_seqs, ls)
            sem_free_at[got] = (b.qclass,
                                barrier_seqs[j] if j < len(barrier_seqs) else None)
        n_sems = len(sem_free_at)
        avail = len(bass.get_kernel_semaphore_range())
        assert n_sems + len(ENGS) <= avail - 2, \
            f"need {n_sems} slot sems + {len(ENGS)}, have {avail}"
        ssem = [stack.enter_context(nc.semaphore(f"s{i}")) for i in range(n_sems)]
        marked = set()
        for o in self.ops:
            for d in o.deps:
                if not d.is_dma:
                    marked.add(d.seq)
        ords = {e: 0 for e in ENGS}
        for o in self.ops:
            if not o.is_dma and not o.is_barrier and o.seq in marked:
                ords[o.eng] += 1
                o.mark_ord = ords[o.eng]
        semcnt = [0] * n_sems
        dma_semord = {}
        bar_semsnap = {}
        for o in self.ops:
            if o.is_dma:
                i = o.slot.sem_idx
                semcnt[i] += 1
                dma_semord[o.seq] = (i, semcnt[i])
            if o.is_barrier:
                bar_semsnap[o.seq] = list(semcnt)
        NCH = len(ENGS) + n_sems
        eng_ch = {e: i for i, e in enumerate(ENGS)}
        ZERO = [0] * NCH
        vc_of = {}
        last_eng_op = {}
        streams = {e: [] for e in ENGS}

        def dep_chan_val(d):
            if d.is_dma:
                i, ordn = dma_semord[d.seq]
                return len(ENGS) + i, 16 * ordn, ssem[i]
            if d.mark_ord is None:
                return None
            return eng_ch[d.eng], d.mark_ord, csem[d.eng]

        for o in self.ops:
            key = (o.eng, o.is_dma)
            prev = last_eng_op.get(key)
            ukey = o.eng
            prev = last_eng_op.get(ukey)
            vc = list(vc_of[prev]) if prev is not None else list(ZERO)
            if prev is not None:
                pass
            waits = []
            for d in o.deps:
                cv = dep_chan_val(d)
                if cv is None:
                    continue
                ch, val, sem = cv
                dvc = vc_of.get(d.seq)
                need = val > vc[ch]
                if need:
                    waits.append((sem, val))
                    vc[ch] = val
                if dvc is not None:
                    for c2 in range(NCH):
                        if dvc[c2] > vc[c2]:
                            vc[c2] = dvc[c2]
            if o.is_barrier:
                snap = bar_semsnap[o.seq]
                for i, cnt in enumerate(snap):
                    if cnt > 0 and 16 * cnt > vc[len(ENGS) + i]:
                        waits.append((ssem[i], 16 * cnt))
                        vc[len(ENGS) + i] = 16 * cnt
            vc_of[o.seq] = vc
            last_eng_op[o.eng] = o.seq
            if not o.is_dma and o.mark_ord is not None:
                pass
            incs = []
            if o.is_dma:
                i, ordn = dma_semord[o.seq]
                incs.append((ssem[i], 16))
            elif o.mark_ord is not None:
                incs.append((csem[o.eng], 1))
            streams[o.eng].append((waits, o.fn, incs))
        tail = []
        sync_last = last_eng_op.get("sync")
        sync_vc = vc_of.get(sync_last, list(ZERO)) if sync_last is not None else list(ZERO)
        for i in range(n_sems):
            if semcnt[i] > 0 and 16 * semcnt[i] > sync_vc[len(ENGS) + i]:
                tail.append((ssem[i], 16 * semcnt[i]))
        streams["sync"].append((tail, None, []))

        with nc.Block() as block:
            def run(engine, seq):
                for waits, fn, incs in seq:
                    for s, v in waits:
                        engine.wait_ge(s, v)
                    if fn is None:
                        continue
                    inst = fn(engine)
                    for s, n in incs:
                        inst.then_inc(s, n)

            @block.tensor
            def _(te):
                run(te, streams["tensor"])

            @block.vector
            def _(v):
                run(v, streams["vector"])

            @block.scalar
            def _(sc):
                run(sc, streams["scalar"])

            @block.sync
            def _(sy):
                run(sy, streams["sync"])

            @block.gpsimd
            def _(g):
                run(g, streams["gpsimd"])


class SbAlloc:

    def __init__(self, nc, stack, name, total_cols, dtype):
        self.t = stack.enter_context(nc.sbuf_tensor(name, [128, total_cols], dtype))
        self.total = total_cols
        self.off = 0
        self.marks = []
        self.hiwater = 0
        self.regions = []

    def alloc(self, cols, name="", buf=None):
        assert self.off + cols <= self.total, \
            f"SBUF pool overflow: {self.off}+{cols}>{self.total} ({name})"
        ap = self.t[:, self.off:self.off + cols]
        b = buf if buf is not None else Buf(name)
        s, e2 = self.off, self.off + cols
        keep = []
        pend = []
        for (rs, re, rb) in self.regions:
            if rs < e2 and re > s:
                pend += rb.r
                if rb.w is not None:
                    pend.append(rb.w)
            else:
                keep.append((rs, re, rb))
        self.regions = keep + [(s, e2, b)]
        seen = set()
        for o in pend:
            if o.seq not in seen:
                seen.add(o.seq)
                b.r.append(o)
        self.off = e2
        self.hiwater = max(self.hiwater, self.off)
        return ap, b

    def push(self):
        self.marks.append(self.off)

    def pop(self):
        self.off = self.marks.pop()



def emit_ln_stats(sh, xt_ap, xt_b, width, eps, junk_ap, junk_b, st_ap, st_b):
    s_sum, s_sq = st_ap[:, 0:1], st_ap[:, 1:2]
    s_m, s_rstd = st_ap[:, 2:3], st_ap[:, 3:4]
    s_tmp, s_var = st_ap[:, 4:5], st_ap[:, 5:6]
    s_negm = st_ap[:, 6:7]
    sh.op("vector", lambda e: e.reduce_sum(s_sum, xt_ap, axis=AX.X),
          reads=[xt_b], writes=[st_b])
    sh.op("scalar", lambda e: e.activation(junk_ap, xt_ap, AF.Square, accum_out=s_sq),
          reads=[xt_b], writes=[junk_b, st_b])
    sh.op("vector", lambda e: e.tensor_scalar(s_m, s_sum, 1.0 / width, None, ALU.mult),
          reads=[st_b], writes=[st_b])
    sh.op("vector", lambda e: e.tensor_scalar(s_negm, s_m, -1.0, None, ALU.mult),
          reads=[st_b], writes=[st_b])
    sh.op("vector", lambda e: e.tensor_scalar(s_tmp, s_m, s_m, None, ALU.mult),
          reads=[st_b], writes=[st_b])
    sh.op("vector", lambda e: e.tensor_scalar(s_var, s_sq, 1.0 / width, s_tmp, ALU.mult, ALU.subtract),
          reads=[st_b], writes=[st_b])
    sh.op("vector", lambda e: e.tensor_scalar(s_var, s_var, float(eps), None, ALU.add),
          reads=[st_b], writes=[st_b])
    sh.op("scalar", lambda e: e.activation(s_tmp, s_var, AF.Sqrt),
          reads=[st_b], writes=[st_b])
    sh.op("vector", lambda e: e.reciprocal(s_rstd, s_tmp),
          reads=[st_b], writes=[st_b])



def build_program(blocks, boundaries):
    nc = bass.Bass("TRN2", target_bir_lowering=False, debug=False,
                   detect_race_conditions=False)
    sh = Sched(nc)

    def stream_tensor(name, shape, dtype=F32):
        if name in boundaries:
            kind = "ExternalInput" if boundaries[name] == "in" else "ExternalOutput"
        else:
            kind = "Internal"
        return nc.dram_tensor(name, list(shape), dtype, kind=kind).ap()

    x0 = nc.dram_tensor("x0", [S, D], F32, kind="ExternalInput").ap() if "a0" in blocks else None
    x1 = stream_tensor("x1", [S, D]) if ("a0" in blocks or "f0" in blocks) else None
    x2 = stream_tensor("x2", [S, D]) if ("f0" in blocks or "a1" in blocks) else None
    x3 = stream_tensor("x3", [S, D]) if ("a1" in blocks or "moe" in blocks) else None
    yout = nc.dram_tensor("y", [OWN, D], F32, kind="ExternalOutput").ap() if "moe" in blocks else None

    ident_d = nc.dram_tensor("identc", [128, 128], BF16, kind="ExternalInput").ap()
    ones_d = nc.dram_tensor("onesc", [128, 128], BF16, kind="ExternalInput").ap()
    mask_d = nc.dram_tensor("maskc", [128, 896], F32, kind="ExternalInput").ap()
    identf_d = nc.dram_tensor("identfc", [128, 128], F32, kind="ExternalInput").ap()
    onesf_d = nc.dram_tensor("onesfc", [128, 128], F32, kind="ExternalInput").ap()

    W = {}

    def win(nm, shape, dt=F32):
        W[nm] = nc.dram_tensor(nm, list(shape), dt, kind="ExternalInput").ap()

    if "a0" in blocks:
        for nm in ("wq", "wk", "wv", "wo"):
            win(nm, [D, D])
    if "f0" in blocks:
        win("wfc", [D, FF])
        win("wproj", [FF, D])
    if "a1" in blocks:
        win("wqkv", [D, 3 * D])
        win("wo1", [D, D])
    if "moe" in blocks:
        win("wup", [E, D, D])
        win("wgate", [E, D, D])
        win("wdown", [E, D, D])
        win("wsu", [D, D])
        win("wsg", [D, D])
        win("wsd", [D, D])
        win("wr", [D, E])
        win("br", [1, E])
        win("gbias", [1, E])
        win("bup", [E, DC, 128])
        win("bgate", [E, DC, 128])
        win("bdown", [E, DC, 128])
        win("bsu", [DC, 128])
        win("bsg", [DC, 128])
        win("bsd", [1, D])
        win("bdown_r", [E, D])
        W["rowsel"] = nc.dram_tensor("rowsel", [128, OWN // 128], I32, kind="ExternalInput").ap()

    dsc = {}
    dsc["xn_fm"] = nc.dram_tensor("xn_fm", [DC, 128, S], BF16, kind="Internal").ap()
    dsc["q_fm"] = nc.dram_tensor("q_fm", [DC, 128, S], BF16, kind="Internal").ap()
    dsc["k_fm"] = nc.dram_tensor("k_fm", [DC, 128, S], BF16, kind="Internal").ap()
    dsc["attn_fm"] = nc.dram_tensor("attn_fm", [DC, 128, S], BF16, kind="Internal").ap()
    dsc["v_img"] = nc.dram_tensor("v_img", [HEADS, 128, TT, 128], BF16, kind="Internal").ap()
    dsc["h_tm"] = nc.dram_tensor("h_tm", [S, FF], BF16, kind="Internal").ap()
    dsc["xh2_fm"] = nc.dram_tensor("xh2_fm", [128, FC, S], BF16, kind="Internal").ap()
    dsc["t_proj"] = nc.dram_tensor("t_proj", [FF, D], BF16, kind="Internal").ap()
    dsc["out_proj"] = nc.dram_tensor("out_proj", [S, D], F32, kind="Internal").ap()
    dsc["u_d"] = nc.dram_tensor("u_d", [128, DC, OWN], BF16, kind="Internal").ap()
    dsc["g_d"] = nc.dram_tensor("g_d", [128, DC, OWN], BF16, kind="Internal").ap()
    dsc["acc_d"] = nc.dram_tensor("acc_d", [OWN, D], F32, kind="Internal").ap()
    dsc["dbg"] = nc.dram_tensor("dbg", [128, 16], F32, kind="Internal").ap()
    dbuf = {nm: Buf(nm) for nm in
            ("xn_fm", "q_fm", "k_fm", "v_img", "attn_fm", "h_tm", "xh2_fm",
             "t_proj", "out_proj", "u_d", "g_d", "acc_d", "x1", "x2", "x3", "y")}

    with contextlib.ExitStack() as stack:
        sb = SbAlloc(nc, stack, "sb_bf", 62000, BF16)
        sf = SbAlloc(nc, stack, "sb_f32", 18600, F32)

        ident, ident_b = sb.alloc(128, "ident")
        ones, ones_b = sb.alloc(128, "ones")
        mask, mask_b = sf.alloc(896, "mask")
        sh.dma("sync", lambda e: e.dma_start(out=ident, in_=ident_d[:, :]), writes=[ident_b])
        sh.dma("sync", lambda e: e.dma_start(out=ones, in_=ones_d[:, :]), writes=[ones_b])
        sh.dma("sync", lambda e: e.dma_start(out=mask, in_=mask_d[:, :]), writes=[mask_b])
        identf, identf_b = sf.alloc(128, "identf")
        onesf, onesf_b = sf.alloc(128, "onesf")
        sh.dma("sync", lambda e: e.dma_start(out=identf, in_=identf_d[:, :]), writes=[identf_b])
        sh.dma("sync", lambda e: e.dma_start(out=onesf, in_=onesf_d[:, :]), writes=[onesf_b])

        pb, pbb = [], []
        for i in range(6):
            t = stack.enter_context(nc.psum_tensor(f"pb{i}", [128, 512], F32))
            pb.append(t)
            pbb.append(Buf(f"pb{i}"))
        ptp, ptb = [], []
        for i in range(2):
            t = stack.enter_context(nc.psum_tensor(f"pt{i}", [128, 1024], BF16))
            ptp.append(t)
            ptb.append(Buf(f"pt{i}"))

        rsel_t = stack.enter_context(nc.sbuf_tensor("rsel_i32", [128, 16], I32))
        ctx = dict(nc=nc, sh=sh, sb=sb, sf=sf, W=W, dsc=dsc, dbuf=dbuf, rsel_t=rsel_t,
                   ident=ident, ident_b=ident_b, ones=ones, ones_b=ones_b,
                   mask=mask, mask_b=mask_b, identf=identf, identf_b=identf_b,
                   onesf=onesf, onesf_b=onesf_b, pb=pb, pbb=pbb, ptp=ptp, ptb=ptb)

        if "a0" in blocks:
            emit_attn(ctx, layer=0, x_in=x0, xin_b=Buf("x0"), x_out=x1, xout_b=dbuf["x1"])
        if "f0" in blocks:
            sh.barrier()
            emit_ffn0(ctx, x_in=x1, xin_b=dbuf["x1"], x_out=x2, xout_b=dbuf["x2"])
        if "a1" in blocks:
            sh.barrier()
            emit_attn(ctx, layer=1, x_in=x2, xin_b=dbuf["x2"], x_out=x3, xout_b=dbuf["x3"])
        if "moe" in blocks:
            sh.barrier()
            emit_moe(ctx, x_in=x3, xin_b=dbuf["x3"], y_out=yout, yout_b=dbuf["y"])

        sh.finalize(stack)
    return nc



def emit_attn(ctx, layer, x_in, xin_b, x_out, xout_b):
    sh, sb, sf = ctx["sh"], ctx["sb"], ctx["sf"]
    W, dsc, dbuf = ctx["W"], ctx["dsc"], ctx["dbuf"]
    ident, ident_b = ctx["ident"], ctx["ident_b"]
    mask, mask_b = ctx["mask"], ctx["mask_b"]
    pb, pbb, ptp, ptb = ctx["pb"], ctx["pbb"], ctx["ptp"], ctx["ptb"]
    EPS = 1e-5
    xn_fm, xnb = dsc["xn_fm"], dbuf["xn_fm"]
    q_fm, qb = dsc["q_fm"], dbuf["q_fm"]
    k_fm, kb = dsc["k_fm"], dbuf["k_fm"]
    v_img, vb = dsc["v_img"], dbuf["v_img"]
    attn_fm, ab = dsc["attn_fm"], dbuf["attn_fm"]

    sb.push(); sf.push()
    xt = [sf.alloc(D, f"xt{i}") for i in range(2)]
    xn_tm4 = [sb.alloc(D, f"xntm{i}") for i in range(4)]
    stats = [sf.alloc(8, f"st{i}") for i in range(4)]
    tstg = [sb.alloc(512, f"tstg{i}") for i in range(2)]

    for t in range(TT):
        xa, xab = xt[t % 2]
        oa, ob = xn_tm4[t % 4]
        sa, sab = stats[t % 4]
        sh.dma("sync", lambda e, xa=xa, t=t: e.dma_start(out=xa, in_=x_in[t * 128:(t + 1) * 128, :]),
               reads=[xin_b], writes=[xab])
        emit_ln_stats(sh, xa, xab, D, EPS, oa, ob, sa, sab)
        sh.op("vector", lambda e, oa=oa, xa=xa, sa=sa:
              e.tensor_scalar(oa, xa, sa[:, 2:3], sa[:, 3:4], ALU.subtract, ALU.mult),
              reads=[xab, sab], writes=[ob])
        if t % 4 == 3:
            for c in range(DC):
                bank, bankb = ptp[c % 2], ptb[c % 2]
                for j in range(4):
                    src, srcb = xn_tm4[j]
                    sh.op("tensor",
                          lambda e, bank=bank, j=j, src=src, c=c:
                          e.transpose(bank[:, j * 128:(j + 1) * 128],
                                      src[:, c * 128:(c + 1) * 128], ident),
                          reads=[srcb, ident_b], writes=[bankb])
                sg, sgb = tstg[c % 2]
                sh.op("scalar", lambda e, sg=sg, bank=bank: e.copy(sg, bank[:, 0:512]),
                      reads=[bankb], writes=[sgb])
                col0 = (t - 3) * 128
                sh.dma("sync",
                       lambda e, sg=sg, c=c, col0=col0:
                       e.dma_start(out=xn_fm[c, :, col0:col0 + 512], in_=sg),
                       reads=[sgb], writes=[xnb])
    sb.pop(); sf.pop()

    sb.push(); sf.push()
    wsb_sh = [Buf("wsb_g0"), Buf("wsb_g1")]
    wsb = []
    for d in range(DC):
        shb = wsb_sh[d // (DC // 2)]
        ap, b = sb.alloc(D, f"wsb{d}", buf=shb)
        wsb.append((ap, shb))
    xnT_shared = Buf("xnT_g")
    xnT_all, _ = sb.alloc(DC * 512, "xnT_all", buf=xnT_shared)
    xnT = [(xnT_all[:, d * 512:(d + 1) * 512], xnT_shared) for d in range(DC)]
    pstg = [sb.alloc(512, f"pstg{i}") for i in range(4)]

    if layer == 0:
        wsrc = {"q": (W["wq"], 0), "k": (W["wk"], 0), "v": (W["wv"], 0),
                "o": (W["wo"], 0)}
    else:
        wsrc = {"q": (W["wqkv"], 0), "k": (W["wqkv"], D), "v": (W["wqkv"], 2 * D),
                "o": (W["wo1"], 0)}

    def load_weights(tag):
        wd, off = wsrc[tag]
        for d in range(DC):
            wa, wab = wsb[d]
            sh.dma("gpsimd",
                   lambda e, wa=wa, d=d, wd=wd, off=off:
                   e.dma_start(out=wa, in_=wd[d * 128:(d + 1) * 128, off:off + D]),
                   writes=[wab])

    def load_xnT(T):
        sh.dma("sync",
               lambda e, T=T:
               e.dma_start(out=xnT_all.rearrange("p (c k) -> p c k", k=512),
                           in_=xn_fm[:, :, T * 512:(T + 1) * 512].rearrange("c p k -> p c k")),
               reads=[xnb], writes=[xnT_shared])

    for tag, dst, dstb in (("q", q_fm, qb), ("k", k_fm, kb)):
        load_weights(tag)
        for T in range(S // 512):
            load_xnT(T)
            for o in range(DC):
                bank, bankb = pb[o % 4], pbb[o % 4]
                for d in range(DC):
                    wa, wab = wsb[d]
                    xa, xab = xnT[d]
                    sh.op("tensor",
                          lambda e, bank=bank, wa=wa, xa=xa, o=o, d=d:
                          e.matmul(bank[:], wa[:, o * 128:(o + 1) * 128], xa,
                                   start=(d == 0), stop=(d == DC - 1)),
                          reads=[wab, xab], writes=[bankb])
                sg, sgb = pstg[o % 4]
                sh.op("scalar", lambda e, sg=sg, bank=bank: e.copy(sg, bank[:]),
                      reads=[bankb], writes=[sgb])
                sh.dma("sync",
                       lambda e, sg=sg, o=o, T=T, dst=dst:
                       e.dma_start(out=dst[o, :, T * 512:(T + 1) * 512], in_=sg),
                       reads=[sgb], writes=[dstb])

    load_weights("v")
    for T in range(S // 512):
        load_xnT(T)
        for tl in range(4):
            t = T * 4 + tl
            for og in range(D // 512):
                bank, bankb = pb[(tl * 4 + og) % 4], pbb[(tl * 4 + og) % 4]
                for d in range(DC):
                    wa, wab = wsb[d]
                    xa, xab = xnT[d]
                    sh.op("tensor",
                          lambda e, bank=bank, xa=xa, wa=wa, tl=tl, og=og, d=d:
                          e.matmul(bank[:], xa[:, tl * 128:(tl + 1) * 128],
                                   wa[:, og * 512:(og + 1) * 512],
                                   start=(d == 0), stop=(d == DC - 1)),
                          reads=[xab, wab], writes=[bankb])
                sg, sgb = pstg[(tl * 4 + og) % 4]
                sh.op("scalar", lambda e, sg=sg, bank=bank: e.copy(sg, bank[:]),
                      reads=[bankb], writes=[sgb])
                for j in range(4):
                    h = og * 4 + j
                    sh.dma("sync",
                           lambda e, sg=sg, h=h, t=t, j=j:
                           e.dma_start(out=v_img[h, :, t, :], in_=sg[:, j * 128:(j + 1) * 128]),
                           reads=[sgb], writes=[vb])
    sb.pop(); sf.pop()

    sb.push(); sf.push()
    qh = [sb.alloc(S, f"qh{i}") for i in range(2)]
    kh = [sb.alloc(S, f"kh{i}") for i in range(2)]
    vh = [sb.alloc(S, f"vh{i}") for i in range(2)]
    psb = [sb.alloc(S, f"p{i}") for i in range(4)]
    pts = [sb.alloc(512, f"pts{i}") for i in range(2)]
    avstg = [sb.alloc(512, f"avs{i}") for i in range(2)]
    scores = [sf.alloc(S, f"sc{i}") for i in range(4)]
    rowst = [sf.alloc(8, f"rs{i}") for i in range(4)]

    NQB = S // 512
    for h in range(HEADS):
        qa, qab = qh[h % 2]
        ka, kab = kh[h % 2]
        va, vab = vh[h % 2]
        sh.dma("sync", lambda e, qa=qa, h=h: e.dma_start(out=qa, in_=q_fm[h, :, :]),
               reads=[qb], writes=[qab])
        sh.dma("sync", lambda e, ka=ka, h=h: e.dma_start(out=ka, in_=k_fm[h, :, :]),
               reads=[kb], writes=[kab])
        sh.dma("sync", lambda e, va=va, h=h: e.dma_start(
            out=va, in_=v_img[h].rearrange("p t d -> p (t d)")),
            reads=[vb], writes=[vab])
        for qblk in range(NQB):
            nkt = qblk + 1
            for kt in range(nkt):
                for qs in range(4):
                    bank, bankb = pb[qs], pbb[qs]
                    sh.op("tensor",
                          lambda e, bank=bank, qa=qa, ka=ka, qblk=qblk, qs=qs, kt=kt:
                          e.matmul(bank[:], qa[:, qblk * 512 + qs * 128:qblk * 512 + (qs + 1) * 128],
                                   ka[:, kt * 512:(kt + 1) * 512], start=True, stop=True),
                          reads=[qab, kab], writes=[bankb])
                    dst, dstb = scores[qs]
                    if kt == qblk:
                        msl = mask[:, 384 - qs * 128: 896 - qs * 128]
                        sh.op("vector",
                              lambda e, dst=dst, bank=bank, msl=msl, kt=kt:
                              e.scalar_tensor_tensor(dst[:, kt * 512:(kt + 1) * 512], bank[:],
                                                     ISQ, msl, ALU.mult, ALU.add),
                              reads=[bankb, mask_b], writes=[dstb])
                    else:
                        sh.op("vector",
                              lambda e, dst=dst, bank=bank, kt=kt:
                              e.tensor_scalar(dst[:, kt * 512:(kt + 1) * 512], bank[:],
                                              ISQ, None, ALU.mult),
                              reads=[bankb], writes=[dstb])
            for qs in range(4):
                sca, scab = scores[qs]
                pa, pab = psb[qs]
                ra, rab = rowst[qs]
                klen = nkt * 512
                sh.op("vector", lambda e, ra=ra, sca=sca, klen=klen:
                      e.reduce_max(ra[:, 0:1], sca[:, 0:klen], axis=AX.X),
                      reads=[scab], writes=[rab])
                sh.op("vector", lambda e, ra=ra:
                      e.tensor_scalar(ra[:, 3:4], ra[:, 0:1], -1.0, None, ALU.mult),
                      reads=[rab], writes=[rab])
                sh.op("scalar", lambda e, pa=pa, sca=sca, ra=ra, klen=klen:
                      e.activation(pa[:, 0:klen], sca[:, 0:klen], AF.Exp,
                                   bias=ra[:, 3:4], scale=1.0, accum_out=ra[:, 1:2]),
                      reads=[scab, rab], writes=[pab, rab])
                sh.op("vector", lambda e, ra=ra: e.reciprocal(ra[:, 2:3], ra[:, 1:2]),
                      reads=[rab], writes=[rab])
                sh.op("vector", lambda e, pa=pa, ra=ra, klen=klen:
                      e.tensor_scalar(pa[:, 0:klen], pa[:, 0:klen], ra[:, 2:3], None, ALU.mult),
                      reads=[rab, pab], writes=[pab])
            avbank, avbankb = pb[4], pbb[4]
            for kc in range(nkt * 4):
                bank, bankb = ptp[kc % 2], ptb[kc % 2]
                for qs in range(4):
                    pa, pab = psb[qs]
                    sh.op("tensor",
                          lambda e, bank=bank, pa=pa, qs=qs, kc=kc:
                          e.transpose(bank[:, qs * 128:(qs + 1) * 128],
                                      pa[:, kc * 128:(kc + 1) * 128], ident),
                          reads=[pab, ident_b], writes=[bankb])
                sg, sgb = pts[kc % 2]
                sh.op("vector", lambda e, sg=sg, bank=bank: e.tensor_copy(sg, bank[:, 0:512]),
                      reads=[bankb], writes=[sgb])
                sh.op("tensor",
                      lambda e, va=va, sg=sg, kc=kc, first=(kc == 0), last=(kc == nkt * 4 - 1),
                      avbank=avbank:
                      e.matmul(avbank[:], va[:, kc * 128:(kc + 1) * 128], sg,
                               start=first, stop=last),
                      reads=[vab, sgb], writes=[avbankb])
            sg, sgb = avstg[qblk % 2]
            sh.op("scalar", lambda e, sg=sg, avbank=avbank: e.copy(sg, avbank[:]),
                  reads=[avbankb], writes=[sgb])
            sh.dma("sync",
                   lambda e, sg=sg, h=h, qblk=qblk:
                   e.dma_start(out=attn_fm[h, :, qblk * 512:(qblk + 1) * 512], in_=sg),
                   reads=[sgb], writes=[ab])
    sb.pop(); sf.pop()

    sb.push(); sf.push()
    wosb_shared = Buf("wosb_g")
    wosb = []
    for d in range(DC):
        ap, b = sb.alloc(D, f"wo{d}", buf=wosb_shared)
        wosb.append((ap, wosb_shared))
        wd, off = wsrc["o"]
        sh.dma("gpsimd",
               lambda e, ap=ap, d=d, wd=wd, off=off:
               e.dma_start(out=ap, in_=wd[d * 128:(d + 1) * 128, off:off + D]),
               writes=[wosb_shared])
    att = [sb.alloc(D, f"att{i}") for i in range(2)]
    junk, junkb = sb.alloc(D, "junk")
    ot = [sf.alloc(D, f"ot{i}") for i in range(2)]
    res = [sf.alloc(D, f"res{i}") for i in range(2)]
    ost = [sf.alloc(8, f"ost{i}") for i in range(2)]

    for t in range(TT):
        aa, aab = att[t % 2]
        sh.dma("sync", lambda e, aa=aa, t=t: e.dma_start(
            out=aa.rearrange("p (c k) -> p c k", k=128),
            in_=attn_fm[:, :, t * 128:(t + 1) * 128].rearrange("c p k -> p c k")),
            reads=[ab], writes=[aab])
        oa, oab = ot[t % 2]
        for og in range(D // 512):
            bank, bankb = pb[og % 4], pbb[og % 4]
            for d in range(DC):
                wa, wab = wosb[d]
                sh.op("tensor",
                      lambda e, bank=bank, aa=aa, wa=wa, d=d, og=og:
                      e.matmul(bank[:], aa[:, d * 128:(d + 1) * 128],
                               wa[:, og * 512:(og + 1) * 512],
                               start=(d == 0), stop=(d == DC - 1)),
                      reads=[aab, wab], writes=[bankb])
            sh.op("scalar", lambda e, oa=oa, og=og, bank=bank:
                  e.copy(oa[:, og * 512:(og + 1) * 512], bank[:]),
                  reads=[bankb], writes=[oab])
        ra, rab = ost[t % 2]
        emit_ln_stats(sh, oa, oab, D, EPS, junk, junkb, ra, rab)
        xa2, xab2 = res[t % 2]
        sh.dma("scalar", lambda e, xa2=xa2, t=t:
               e.dma_start(out=xa2, in_=x_in[t * 128:(t + 1) * 128, :]),
               reads=[xin_b], writes=[xab2])
        sh.op("vector", lambda e, oa=oa, ra=ra:
              e.tensor_scalar(oa, oa, ra[:, 2:3], ra[:, 3:4], ALU.subtract, ALU.mult),
              reads=[oab, rab], writes=[oab])
        sh.op("vector", lambda e, oa=oa, xa2=xa2: e.tensor_add(oa, oa, xa2),
              reads=[oab, xab2], writes=[oab])
        sh.dma("sync", lambda e, oa=oa, t=t: e.dma_start(out=x_out[t * 128:(t + 1) * 128, :], in_=oa),
               reads=[oab], writes=[xout_b])
    sb.pop(); sf.pop()


def emit_ffn0(ctx, x_in, xin_b, x_out, xout_b):
    sh, sb, sf = ctx["sh"], ctx["sb"], ctx["sf"]
    W, dsc, dbuf = ctx["W"], ctx["dsc"], ctx["dbuf"]
    ident, ident_b = ctx["ident"], ctx["ident_b"]
    identf, identf_b = ctx["identf"], ctx["identf_b"]
    onesf, onesf_b = ctx["onesf"], ctx["onesf_b"]
    pb, pbb, ptp, ptb = ctx["pb"], ctx["pbb"], ctx["ptp"], ctx["ptb"]
    wfc, wproj = W["wfc"], W["wproj"]
    h_tm, hb = dsc["h_tm"], dbuf["h_tm"]
    xh2_fm, x2fb = dsc["xh2_fm"], dbuf["xh2_fm"]
    t_proj, tpb = dsc["t_proj"], dbuf["t_proj"]
    out_proj, opb = dsc["out_proj"], dbuf["out_proj"]
    CLIP = 128.0 - 1e-6

    sb.push(); sf.push()
    gb_fc, gb_fcb = sf.alloc(4, "gb_fc")
    gb_pr, gb_prb = sf.alloc(4, "gb_pr")
    hsum, hsumb = sf.alloc(TT, "hsum")
    hsq, hsqb = sf.alloc(TT, "hsq")
    hmax, hmaxb = sf.alloc(TT, "hmax")
    hmin, hminb = sf.alloc(TT, "hmin")
    sct, sctb = sf.alloc(TT, "sct")
    sc2, sc2b = sf.alloc(TT, "sc2")
    ln3, ln3b = sf.alloc(3 * TT, "ln3")

    sb.push(); sf.push()
    gsc, gscb = sf.alloc(16, "gsc")
    grow, growb = sf.alloc(16, "grow")
    rowx, rowxb = sf.alloc(128, "rowx")
    slab = [sf.alloc(FF, f"gslab{i}") for i in range(2)]
    acc_fc = gsc[:, 0:1]; acc_pr = gsc[:, 1:2]; tmp0 = gsc[:, 2:3]

    def gamma_pass(wd, n_slabs, width, acc_col):
        for i in range(n_slabs):
            sa, sab = slab[i % 2]
            sh.dma("sync", lambda e, sa=sa, i=i, wd=wd, width=width:
                   e.dma_start(out=sa[:, 0:width], in_=wd[i * 128:(i + 1) * 128, :]),
                   writes=[sab])
            sh.op("vector", lambda e, sa=sa, width=width:
                  e.reduce_sum(tmp0, sa[:, 0:width], axis=AX.X, apply_absolute_value=True),
                  reads=[sab], writes=[gscb])
            if i == 0:
                sh.op("vector", lambda e, acc_col=acc_col:
                      e.tensor_scalar(acc_col, tmp0, 1.0, None, ALU.mult),
                      reads=[gscb], writes=[gscb])
            else:
                sh.op("vector", lambda e, acc_col=acc_col:
                      e.tensor_add(acc_col, acc_col, tmp0),
                      reads=[gscb], writes=[gscb])

    gamma_pass(wfc, DC, FF, acc_fc)
    gamma_pass(wproj, FC, D, acc_pr)


    for idx, (acc_col, nelem, gb, gbb) in enumerate(
            ((acc_fc, D * FF, gb_fc, gb_fcb), (acc_pr, FF * D, gb_pr, gb_prb))):
        bank, bankb = pb[idx % 2], pbb[idx % 2]
        sh.op("tensor", lambda e, bank=bank, acc_col=acc_col:
              e.transpose(bank[0:1, 0:128], acc_col, identf),
              reads=[gscb, identf_b], writes=[bankb])
        r0 = grow[0:1, idx * 4:idx * 4 + 1]
        r1 = grow[0:1, idx * 4 + 1:idx * 4 + 2]
        r2 = grow[0:1, idx * 4 + 2:idx * 4 + 3]
        sh.op("vector", lambda e, bank=bank, r0=r0:
              e.reduce_sum(r0, bank[0:1, 0:128], axis=AX.X),
              reads=[bankb], writes=[growb])
        sh.op("vector", lambda e, r0=r0, r1=r1, nelem=nelem:
              e.tensor_scalar(r1, r0, 1.0 / nelem, None, ALU.mult),
              reads=[growb], writes=[growb])
        sh.op("vector", lambda e, r1=r1, r2=r2:
              e.tensor_scalar(r2, r1, 1e-6, 0.5, ALU.add, ALU.mult),
              reads=[growb], writes=[growb])
        bank2, bank2b = pb[2 + idx], pbb[2 + idx]
        sh.op("tensor", lambda e, bank2=bank2, idx=idx:
              e.matmul(bank2[:, 0:2], onesf[0:1, 0:128], grow[0:1, idx * 4 + 1:idx * 4 + 3],
                       start=True, stop=True),
              reads=[onesf_b, growb], writes=[bank2b])
        sh.op("scalar", lambda e, gb=gb, bank2=bank2: e.copy(gb[:, 0:2], bank2[:, 0:2]),
              reads=[bank2b], writes=[gbb])
        sh.op("vector", lambda e, gb=gb:
              e.tensor_scalar(gb[:, 2:3], gb[:, 1:2], -1.0, None, ALU.mult),
              reads=[gbb], writes=[gbb])
    sb.pop(); sf.pop()
    sh.barrier()

    sb.push(); sf.push()
    xhat_fm = []
    for c in range(DC):
        ap, bb = sb.alloc(S // 2, f"xhfm{c}")
        xhat_fm.append((ap, bb))
    xt = [sf.alloc(D, f"fxt{i}") for i in range(2)]
    xq4 = [sb.alloc(D, f"fxq{i}") for i in range(4)]
    stats = [sf.alloc(16, f"fst{i}") for i in range(4)]
    junkf, junkfb = sf.alloc(D, "fjunk")
    tfc2 = []
    for par in range(2):
        row = []
        for d in range(DC):
            ap, bb = sb.alloc(512, f"tfc{par}_{d}")
            row.append((ap, bb))
        tfc2.append(row)
    wtile = [sf.alloc(512, f"fwt{i}") for i in range(4)]
    m1t = [sf.alloc(512, f"fm1{i}") for i in range(4)]
    hstg = [sf.alloc(512, f"fhs{i}") for i in range(4)]
    hbf = [sb.alloc(512, f"fhb{i}") for i in range(4)]
    hjunk, hjunkb = sf.alloc(512, "fhjunk")
    hps = [sf.alloc(4, f"fhp{i}") for i in range(4)]
    NOB = FF // 512
    HTT = TT // 2

    for half in range(2):
        t0h = half * HTT
        for tl in range(HTT):
            t = t0h + tl
            xa, xab = xt[t % 2]
            qa, qab = xq4[t % 4]
            sa, sab = stats[t % 4]
            sh.dma("scalar", lambda e, xa=xa, t=t: e.dma_start(out=xa, in_=x_in[t * 128:(t + 1) * 128, :]),
                   reads=[xin_b], writes=[xab])
            emit_ln_stats(sh, xa, xab, D, 1e-5, junkf, junkfb, sa, sab)
            sh.op("vector", lambda e, sa=sa: e.tensor_scalar(sa[:, 7:8], sa[:, 5:6], -1e-5, None, ALU.add),
                  reads=[sab], writes=[sab])
            sh.op("vector", lambda e, sa=sa: e.tensor_scalar(sa[:, 8:9], sa[:, 3:4], sa[:, 3:4], None, ALU.mult),
                  reads=[sab], writes=[sab])
            sh.op("vector", lambda e, sa=sa: e.tensor_scalar(sa[:, 8:9], sa[:, 8:9], sa[:, 7:8], None, ALU.mult),
                  reads=[sab], writes=[sab])
            sh.op("vector", lambda e, sa=sa: e.tensor_scalar(sa[:, 8:9], sa[:, 8:9], 1e-6, None, ALU.add),
                  reads=[sab], writes=[sab])
            sh.op("scalar", lambda e, sa=sa: e.activation(sa[:, 9:10], sa[:, 8:9], AF.Sqrt),
                  reads=[sab], writes=[sab])
            sh.op("vector", lambda e, sa=sa: e.reciprocal(sa[:, 10:11], sa[:, 9:10]),
                  reads=[sab], writes=[sab])
            sh.op("vector", lambda e, sa=sa: e.tensor_scalar(sa[:, 11:12], sa[:, 3:4], sa[:, 10:11], None, ALU.mult),
                  reads=[sab], writes=[sab])
            sh.op("vector", lambda e, sa=sa: e.tensor_scalar(sa[:, 12:13], sa[:, 2:3], sa[:, 11:12], -1.0, ALU.mult, ALU.mult),
                  reads=[sab], writes=[sab])
            sh.op("scalar", lambda e, xa=xa, sa=sa:
                  e.activation(junkf, xa, AF.Identity, bias=sa[:, 12:13], scale=sa[:, 11:12]),
                  reads=[xab, sab], writes=[junkfb])
            sh.op("vector", lambda e, sa=sa: e.reduce_max(sa[:, 13:14], junkf, axis=AX.X, apply_absolute_value=True),
                  reads=[junkfb], writes=[sab])
            sh.op("vector", lambda e, sa=sa: e.reciprocal(sa[:, 14:15], sa[:, 13:14]),
                  reads=[sab], writes=[sab])
            sh.op("vector", lambda e, qa=qa, sa=sa:
                  e.tensor_scalar(qa, junkf, sa[:, 14:15], 128.0, ALU.mult, ALU.mult),
                  reads=[junkfb, sab], writes=[qab])
            sh.op("vector", lambda e, qa=qa:
                  e.tensor_scalar(qa, qa, CLIP, -CLIP, ALU.min, ALU.max),
                  reads=[qab], writes=[qab])
            sh.op("vector", lambda e, sa=sa, t=t:
                  e.tensor_scalar(sct[:, t:t + 1], sa[:, 13:14], gb_fc[:, 0:1], 0.0078125, ALU.mult, ALU.mult),
                  reads=[sab, gb_fcb], writes=[sctb])
            if t % 4 == 3:
                for c in range(DC):
                    bank, bankb = ptp[c % 2], ptb[c % 2]
                    for j in range(4):
                        src2, src2b = xq4[j]
                        sh.op("tensor",
                              lambda e, bank=bank, j=j, src2=src2, c=c:
                              e.transpose(bank[:, j * 128:(j + 1) * 128],
                                          src2[:, c * 128:(c + 1) * 128], ident),
                              reads=[src2b, ident_b], writes=[bankb])
                    dst, dstb = xhat_fm[c]
                    col0 = (t - 3 - t0h) * 128
                    sh.op("scalar", lambda e, dst=dst, bank=bank, col0=col0:
                          e.copy(dst[:, col0:col0 + 512], bank[:, 0:512]),
                          reads=[bankb], writes=[dstb])

        for o in range(NOB):
            tfc = tfc2[o % 2]
            for d in range(DC):
                wa, wab = wtile[d % 4]
                ma, mab = m1t[d % 4]
                ta, tab = tfc[d]
                sh.dma("sync", lambda e, wa=wa, d=d, o=o:
                       e.dma_start(out=wa, in_=wfc[d * 128:(d + 1) * 128, o * 512:(o + 1) * 512]),
                       writes=[wab])
                sh.op("vector", lambda e, ma=ma, wa=wa:
                      e.tensor_scalar(ma, wa, gb_fc[:, 1:2], None, ALU.is_gt),
                      reads=[wab, gb_fcb], writes=[mab])
                sh.op("vector", lambda e, wa=wa:
                      e.tensor_scalar(wa, wa, gb_fc[:, 2:3], None, ALU.is_lt),
                      reads=[wab, gb_fcb], writes=[wab])
                sh.op("vector", lambda e, ta=ta, ma=ma, wa=wa: e.tensor_sub(ta, ma, wa),
                      reads=[mab, wab], writes=[tab])
            for tl in range(HTT):
                t = t0h + tl
                bank, bankb = pb[t % 4], pbb[t % 4]
                for d in range(DC):
                    xa, xab = xhat_fm[d]
                    ta, tab = tfc[d]
                    sh.op("tensor",
                          lambda e, bank=bank, xa=xa, ta=ta, tl=tl, d=d:
                          e.matmul(bank[:], xa[:, tl * 128:(tl + 1) * 128], ta,
                                   start=(d == 0), stop=(d == DC - 1)),
                          reads=[xab, tab], writes=[bankb])
                hs, hsb2 = hstg[t % 4]
                hba, hbab = hbf[t % 4]
                hp, hpb = hps[t % 4]
                sh.op("vector", lambda e, hs=hs, bank=bank, t=t:
                      e.tensor_scalar(hs, bank[:], sct[:, t:t + 1], None, ALU.mult),
                      reads=[bankb, sctb], writes=[hsb2])
                sh.op("scalar", lambda e, hba=hba, hs=hs, hp=hp:
                      e.activation(hba, hs, AF.Gelu, accum_out=hp[:, 0:1]),
                      reads=[hsb2], writes=[hbab, hpb])
                sh.op("scalar", lambda e, hba=hba, hp=hp:
                      e.activation(hjunk, hba, AF.Square, accum_out=hp[:, 1:2]),
                      reads=[hbab], writes=[hjunkb, hpb])
                sh.op("vector", lambda e, hba=hba, hp=hp:
                      e.reduce_max(hp[:, 2:3], hba, axis=AX.X),
                      reads=[hbab], writes=[hpb])
                sh.op("vector", lambda e, hba=hba, hp=hp:
                      e.tensor_reduce(hp[:, 3:4], hba, axis=AX.X, op=ALU.min),
                      reads=[hbab], writes=[hpb])
                if o == 0:
                    for dstc, dstbb, col in ((hsum, hsumb, 0), (hsq, hsqb, 1),
                                             (hmax, hmaxb, 2), (hmin, hminb, 3)):
                        sh.op("vector", lambda e, dstc=dstc, hp=hp, t=t, col=col:
                              e.tensor_scalar(dstc[:, t:t + 1], hp[:, col:col + 1], 1.0, None, ALU.mult),
                              reads=[hpb], writes=[dstbb])
                else:
                    sh.op("vector", lambda e, hp=hp, t=t:
                          e.tensor_add(hsum[:, t:t + 1], hsum[:, t:t + 1], hp[:, 0:1]),
                          reads=[hpb, hsumb], writes=[hsumb])
                    sh.op("vector", lambda e, hp=hp, t=t:
                          e.tensor_add(hsq[:, t:t + 1], hsq[:, t:t + 1], hp[:, 1:2]),
                          reads=[hpb, hsqb], writes=[hsqb])
                    sh.op("vector", lambda e, hp=hp, t=t:
                          e.tensor_max(hmax[:, t:t + 1], hmax[:, t:t + 1], hp[:, 2:3]),
                          reads=[hpb, hmaxb], writes=[hmaxb])
                    sh.op("vector", lambda e, hp=hp, t=t:
                          e.tensor_tensor(hmin[:, t:t + 1], hmin[:, t:t + 1], hp[:, 3:4], ALU.min),
                          reads=[hpb, hminb], writes=[hminb])
                sh.dma("sync", lambda e, hba=hba, t=t, o=o:
                       e.dma_start(out=h_tm[t * 128:(t + 1) * 128, o * 512:(o + 1) * 512], in_=hba),
                       reads=[hbab], writes=[hb])
    sb.pop(); sf.pop()

    sb.push(); sf.push()
    pslab = [sf.alloc(D, f"ps{i}") for i in range(2)]
    pmm = [sf.alloc(D, f"pm{i}") for i in range(2)]
    ptt = [sb.alloc(D, f"ptt{i}") for i in range(2)]
    for i in range(FC):
        wa, wab = pslab[i % 2]
        ma, mab = pmm[i % 2]
        ta, tab = ptt[i % 2]
        sh.dma("sync", lambda e, wa=wa, i=i:
               e.dma_start(out=wa, in_=wproj[i * 128:(i + 1) * 128, :]),
               writes=[wab])
        sh.op("vector", lambda e, ma=ma, wa=wa:
              e.tensor_scalar(ma, wa, gb_pr[:, 1:2], None, ALU.is_gt),
              reads=[wab, gb_prb], writes=[mab])
        sh.op("vector", lambda e, wa=wa:
              e.tensor_scalar(wa, wa, gb_pr[:, 2:3], None, ALU.is_lt),
              reads=[wab, gb_prb], writes=[wab])
        sh.op("vector", lambda e, ta=ta, ma=ma, wa=wa: e.tensor_sub(ta, ma, wa),
              reads=[mab, wab], writes=[tab])
        sh.dma("sync", lambda e, ta=ta, i=i:
               e.dma_start(out=t_proj[i * 128:(i + 1) * 128, :], in_=ta),
               reads=[tab], writes=[tpb])
    sb.pop(); sf.pop()

    sb.push(); sf.push()
    hta = [sb.alloc(FF, f"dht{i}") for i in range(2)]
    ya, yab = sf.alloc(FF, "dy")
    xq2 = [sb.alloc(FF, f"dxq{i}") for i in range(2)]
    dstt = [sf.alloc(16, f"dst{i}") for i in range(2)]
    d2stg = [sb.alloc(512, f"d2s{i}") for i in range(4)]
    for t in range(TT):
        ha, hab = hta[t % 2]
        qa, qab = xq2[t % 2]
        sa, sab = dstt[t % 2]
        sh.dma("sync", lambda e, ha=ha, t=t: e.dma_start(out=ha, in_=h_tm[t * 128:(t + 1) * 128, :]),
               reads=[hb], writes=[hab])
        sh.op("vector", lambda e, sa=sa, t=t: e.tensor_scalar(sa[:, 0:1], hsum[:, t:t + 1], 1.0 / FF, None, ALU.mult),
              reads=[hsumb], writes=[sab])
        sh.op("vector", lambda e, sa=sa: e.tensor_scalar(sa[:, 1:2], sa[:, 0:1], sa[:, 0:1], None, ALU.mult),
              reads=[sab], writes=[sab])
        sh.op("vector", lambda e, sa=sa, t=t: e.tensor_scalar(sa[:, 2:3], hsq[:, t:t + 1], 1.0 / FF, sa[:, 1:2], ALU.mult, ALU.subtract),
              reads=[hsqb, sab], writes=[sab])
        sh.op("vector", lambda e, sa=sa: e.tensor_scalar(sa[:, 2:3], sa[:, 2:3], 1e-6, None, ALU.add),
              reads=[sab], writes=[sab])
        sh.op("scalar", lambda e, sa=sa: e.activation(sa[:, 1:2], sa[:, 2:3], AF.Sqrt),
              reads=[sab], writes=[sab])
        sh.op("vector", lambda e, sa=sa: e.reciprocal(sa[:, 2:3], sa[:, 1:2]),
              reads=[sab], writes=[sab])
        sh.op("vector", lambda e, sa=sa, t=t: e.tensor_scalar(sa[:, 3:4], hmax[:, t:t + 1], sa[:, 0:1], None, ALU.subtract),
              reads=[hmaxb, sab], writes=[sab])
        sh.op("vector", lambda e, sa=sa, t=t: e.tensor_scalar(sa[:, 4:5], hmin[:, t:t + 1], -1.0, sa[:, 0:1], ALU.mult, ALU.add),
              reads=[hminb, sab], writes=[sab])
        sh.op("vector", lambda e, sa=sa: e.tensor_tensor(sa[:, 5:6], sa[:, 3:4], sa[:, 4:5], ALU.max),
              reads=[sab], writes=[sab])
        sh.op("vector", lambda e, sa=sa: e.tensor_scalar(sa[:, 5:6], sa[:, 5:6], sa[:, 2:3], None, ALU.mult),
              reads=[sab], writes=[sab])
        sh.op("vector", lambda e, sa=sa: e.reciprocal(sa[:, 6:7], sa[:, 5:6]),
              reads=[sab], writes=[sab])
        sh.op("vector", lambda e, sa=sa: e.tensor_scalar(sa[:, 6:7], sa[:, 6:7], sa[:, 2:3], 128.0, ALU.mult, ALU.mult),
              reads=[sab], writes=[sab])
        sh.op("vector", lambda e, sa=sa: e.tensor_scalar(sa[:, 7:8], sa[:, 0:1], sa[:, 6:7], -1.0, ALU.mult, ALU.mult),
              reads=[sab], writes=[sab])
        sh.op("vector", lambda e, sa=sa, t=t:
              e.tensor_scalar(sc2[:, t:t + 1], sa[:, 5:6], gb_pr[:, 0:1], 0.0078125, ALU.mult, ALU.mult),
              reads=[sab, gb_prb], writes=[sc2b])
        sh.op("scalar", lambda e, ha=ha, ya=ya, sa=sa:
              e.activation(ya, ha, AF.Identity, bias=sa[:, 7:8], scale=sa[:, 6:7]),
              reads=[hab, sab], writes=[yab])
        sh.op("vector", lambda e, qa=qa, ya=ya:
              e.tensor_scalar(qa, ya, CLIP, -CLIP, ALU.min, ALU.max),
              reads=[yab], writes=[qab])
        for c4 in range(FC // 4):
            bank, bankb = ptp[c4 % 2], ptb[c4 % 2]
            for j in range(4):
                sh.op("tensor", lambda e, bank=bank, qa=qa, c=c4 * 4 + j, j=j:
                      e.transpose(bank[:, j * 128:(j + 1) * 128],
                                  qa[:, c * 128:(c + 1) * 128], ident),
                      reads=[qab, ident_b], writes=[bankb])
            sg, sgb = d2stg[c4 % 4]
            sh.op("scalar", lambda e, sg=sg, bank=bank: e.copy(sg, bank[:, 0:512]),
                  reads=[bankb], writes=[sgb])
            sh.dma("sync", lambda e, sg=sg, c4=c4, t=t:
                   e.dma_start(out=xh2_fm[:, c4 * 4:(c4 + 1) * 4, t * 128:(t + 1) * 128],
                               in_=sg.rearrange("p (c k) -> p c k", k=128)),
                   reads=[sgb], writes=[x2fb])
    sb.pop(); sf.pop()

    sb.push(); sf.push()
    tpsb_shared = Buf("tpsb_g")
    tpsb = []
    for oc in range(FC):
        ap, b = sb.alloc(512, f"tp{oc}", buf=tpsb_shared)
        tpsb.append((ap, tpsb_shared))
    xf = [sb.alloc(FF, f"exf{i}") for i in range(2)]
    ostg = [sf.alloc(512, f"eos{i}") for i in range(4)]
    for dg in range(D // 512):
        for oc in range(FC):
            ta, tab = tpsb[oc]
            sh.dma("sync", lambda e, ta=ta, oc=oc, dg=dg:
                   e.dma_start(out=ta, in_=t_proj[oc * 128:(oc + 1) * 128, dg * 512:(dg + 1) * 512]),
                   reads=[tpb], writes=[tab])
        for t in range(TT):
            xa, xab = xf[t % 2]
            pass
            sh.dma("gpsimd", lambda e, xa=xa, t=t:
                   e.dma_start(out=xa.rearrange("p (c k) -> p c k", k=128), in_=xh2_fm[:, :, t * 128:(t + 1) * 128]),
                   reads=[x2fb], writes=[xab])
            bank, bankb = pb[t % 4], pbb[t % 4]
            for oc in range(FC):
                ta, tab = tpsb[oc]
                sh.op("tensor",
                      lambda e, bank=bank, xa=xa, ta=ta, oc=oc:
                      e.matmul(bank[:], xa[:, oc * 128:(oc + 1) * 128], ta,
                               start=(oc == 0), stop=(oc == FC - 1)),
                      reads=[xab, tab], writes=[bankb])
            sg, sgb = ostg[t % 4]
            sh.op("vector", lambda e, sg=sg, bank=bank, t=t:
                  e.tensor_scalar(sg, bank[:], sc2[:, t:t + 1], None, ALU.mult),
                  reads=[bankb, sc2b], writes=[sgb])
            sh.dma("sync", lambda e, sg=sg, t=t, dg=dg:
                   e.dma_start(out=out_proj[t * 128:(t + 1) * 128, dg * 512:(dg + 1) * 512], in_=sg),
                   reads=[sgb], writes=[opb])
    sb.pop(); sf.pop()

    sb.push(); sf.push()
    ot = [sf.alloc(D, f"fot{i}") for i in range(2)]
    res = [sf.alloc(D, f"fres{i}") for i in range(2)]
    ost = [sf.alloc(8, f"fost{i}") for i in range(2)]
    junk2, junk2b = sb.alloc(D, "fjunk2")
    for t in range(TT):
        oa, oab = ot[t % 2]
        xa2, xab2 = res[t % 2]
        ra, rab = ost[t % 2]
        sh.dma("sync", lambda e, oa=oa, t=t: e.dma_start(out=oa, in_=out_proj[t * 128:(t + 1) * 128, :]),
               reads=[opb], writes=[oab])
        sh.dma("scalar", lambda e, xa2=xa2, t=t: e.dma_start(out=xa2, in_=x_in[t * 128:(t + 1) * 128, :]),
               reads=[xin_b], writes=[xab2])
        emit_ln_stats(sh, oa, oab, D, 1e-5, junk2, junk2b, ra, rab)
        sh.op("vector", lambda e, oa=oa, ra=ra:
              e.tensor_scalar(oa, oa, ra[:, 2:3], ra[:, 3:4], ALU.subtract, ALU.mult),
              reads=[oab, rab], writes=[oab])
        sh.op("vector", lambda e, oa=oa, xa2=xa2: e.tensor_add(oa, oa, xa2),
              reads=[oab, xab2], writes=[oab])
        sh.dma("sync", lambda e, oa=oa, t=t: e.dma_start(out=x_out[t * 128:(t + 1) * 128, :], in_=oa),
               reads=[oab], writes=[xout_b])
    sb.pop(); sf.pop()
    sb.pop(); sf.pop()


def emit_moe(ctx, x_in, xin_b, y_out, yout_b):
    from concourse.bass_core_rust import IndirectOffsetOnAxis
    sh, sb, sf = ctx["sh"], ctx["sb"], ctx["sf"]
    W, dsc, dbuf = ctx["W"], ctx["dsc"], ctx["dbuf"]
    ident, ident_b = ctx["ident"], ctx["ident_b"]
    identf, identf_b = ctx["identf"], ctx["identf_b"]
    onesf, onesf_b = ctx["onesf"], ctx["onesf_b"]
    pb, pbb, ptp, ptb = ctx["pb"], ctx["pbb"], ctx["ptp"], ctx["ptb"]
    u_d, udb = dsc["u_d"], dbuf["u_d"]
    g_d, gdb = dsc["g_d"], dbuf["g_d"]
    acc_d, accb = dsc["acc_d"], dbuf["acc_d"]
    OT = OWN // 128
    NT5 = OWN // 512

    sb.push(); sf.push()
    rsel, rselb = sf.alloc(OT, "rsel")
    probs_all, probsb = sf.alloc(OT * 8, "probs")
    psT, psTb = sf.alloc(OT * 128, "psT")
    xn_d, xn_db = dsc["xn_fm"], Buf("moe_xn")
    bfm = {}
    for nm in ("bup", "bgate"):
        for e in range(E):
            ap, b = sf.alloc(DC, f"{nm}{e}")
            bfm[(nm, e)] = (ap, b)
    for nm in ("bsu", "bsg"):
        ap, b = sf.alloc(DC, nm)
        bfm[nm] = (ap, b)
    bdall, bdallb = sf.alloc(D, "bdall")
    wrs, wrsb = sb.alloc(DC * E, "wrs")
    brg, brgb = sf.alloc(E, "brg")

    nc = ctx["nc"]

    rsel_i = ctx["rsel_t"][:, 0:OT]
    rsel_ib = Buf("rsel_i")
    sh.dma("sync", lambda e: e.dma_start(out=rsel_i, in_=W["rowsel"][:, :]), writes=[rsel_ib])

    def load_bias_fm(src_ap, dst, dstb, tag):
        stg, stgb = sf.alloc(128, f"bstg{tag}")
        sh.dma("sync", lambda e, stg=stg: e.dma_start(out=stg[0:DC, 0:128], in_=src_ap),
               writes=[stgb])
        bank, bankb = pb[0], pbb[0]
        sh.op("tensor", lambda e, bank=bank, stg=stg:
              e.transpose(bank[:, 0:DC], stg[0:DC, 0:128], identf[0:DC, 0:DC]),
              reads=[stgb, identf_b], writes=[bankb])
        sh.op("scalar", lambda e, dst=dst, bank=bank: e.copy(dst, bank[:, 0:DC]),
              reads=[bankb], writes=[dstb])

    for e in range(E):
        ap, b = bfm[("bup", e)]
        load_bias_fm(W["bup"][e], ap, b, f"up{e}")
        ap, b = bfm[("bgate", e)]
        load_bias_fm(W["bgate"][e], ap, b, f"g{e}")
    load_bias_fm(W["bsu"], bfm["bsu"][0], bfm["bsu"][1], "su")
    load_bias_fm(W["bsg"], bfm["bsg"][0], bfm["bsg"][1], "sg")
    sh.dma("sync", lambda e: e.dma_start(out=bdall[0:E, :], in_=W["bdown_r"][:, :]), writes=[bdallb])
    sh.dma("scalar", lambda e: e.dma_start(out=bdall[E:E + 1, :], in_=W["bsd"][:, :]), writes=[bdallb])
    for c in range(DC):
        sh.dma("gpsimd", lambda e, c=c: e.dma_start(out=wrs[:, c * E:(c + 1) * E],
                                                    in_=W["wr"][c * 128:(c + 1) * 128, :]),
               writes=[wrsb])
    brrow, brrowb = sf.alloc(2 * E, "brrow")
    sh.dma("scalar", lambda e: e.dma_start(out=brrow[0:1, 0:E], in_=W["br"][:, :]), writes=[brrowb])
    sh.dma("scalar", lambda e: e.dma_start(out=brrow[0:1, E:2 * E], in_=W["gbias"][:, :]), writes=[brrowb])
    sh.op("vector", lambda e: e.tensor_add(brrow[0:1, 0:E], brrow[0:1, 0:E], brrow[0:1, E:2 * E]),
          reads=[brrowb], writes=[brrowb])
    bank, bankb = pb[1], pbb[1]
    sh.op("tensor", lambda e, bank=bank:
          e.matmul(bank[:, 0:E], onesf[0:1, 0:128], brrow[0:1, 0:E], start=True, stop=True),
          reads=[onesf_b, brrowb], writes=[bankb])
    sh.op("scalar", lambda e, bank=bank: e.copy(brg, bank[:, 0:E]),
          reads=[bankb], writes=[brgb])

    def emit_router_tile(t, tl):
        bank, bankb = pb[2 + t % 2], pbb[2 + t % 2]
        for c in range(DC):
            xa, xab = xnT[c]
            sh.op("tensor",
                  lambda e, bank=bank, xa=xa, c=c, tl=tl:
                  e.matmul(bank[:, 0:E], xa[:, tl * 128:(tl + 1) * 128], wrs[:, c * E:(c + 1) * E],
                           start=(c == 0), stop=(c == DC - 1)),
                  reads=[xab, wrsb], writes=[bankb])
        ra, rab = rt[t % 2]
        g0 = ra[:, 0:E]
        bg = ra[:, 7:7 + E]
        sel = ra[:, 14:14 + E]
        sh.op("vector", lambda e, g0=g0, bank=bank: e.tensor_scalar(g0, bank[:, 0:E], 1.0, None, ALU.mult),
              reads=[bankb], writes=[rab])
        sh.op("vector", lambda e, bg=bg, g0=g0: e.tensor_add(bg, g0, brg),
              reads=[rab, brgb], writes=[rab])
        sh.op("vector", lambda e, ra=ra, bg=bg: e.reduce_max(ra[:, 21:22], bg, axis=AX.X),
              reads=[rab], writes=[rab])
        sh.op("vector", lambda e, ra=ra, sel=sel, bg=bg:
              e.tensor_scalar(sel, bg, ra[:, 21:22], None, ALU.is_ge),
              reads=[rab], writes=[rab])
        sh.op("vector", lambda e, ra=ra, sel=sel, bg=bg:
              e.scalar_tensor_tensor(sel, sel, NEG, bg, ALU.mult, ALU.add),
              reads=[rab], writes=[rab])
        sh.op("vector", lambda e, ra=ra, sel=sel: e.reduce_max(ra[:, 22:23], sel, axis=AX.X),
              reads=[rab], writes=[rab])
        sh.op("vector", lambda e, ra=ra, sel=sel, bg=bg:
              e.tensor_scalar(sel, bg, ra[:, 22:23], None, ALU.is_ge),
              reads=[rab], writes=[rab])
        gm = ra[:, 0:E]
        sh.op("vector", lambda e, sel=sel: e.tensor_scalar(sel, sel, -1.0, None, ALU.add),
              reads=[rab], writes=[rab])
        sh.op("vector", lambda e, gm=gm, sel=sel, g0=g0:
              e.scalar_tensor_tensor(gm, sel, -NEG, g0, ALU.mult, ALU.add),
              reads=[rab], writes=[rab])
        sh.op("vector", lambda e, ra=ra, gm=gm: e.reduce_max(ra[:, 21:22], gm, axis=AX.X),
              reads=[rab], writes=[rab])
        sh.op("vector", lambda e, ra=ra: e.tensor_scalar(ra[:, 22:23], ra[:, 21:22], -1.0, None, ALU.mult),
              reads=[rab], writes=[rab])
        pcol = probs_all[:, t * 8:t * 8 + E]
        sh.op("scalar", lambda e, pcol=pcol, gm=gm, ra=ra:
              e.activation(pcol, gm, AF.Exp, bias=ra[:, 22:23], scale=1.0, accum_out=ra[:, 23:24]),
              reads=[rab], writes=[probsb, rab])
        sh.op("vector", lambda e, ra=ra: e.reciprocal(ra[:, 21:22], ra[:, 23:24]),
              reads=[rab], writes=[rab])
        sh.op("vector", lambda e, pcol=pcol, ra=ra:
              e.tensor_scalar(pcol, pcol, ra[:, 21:22], None, ALU.mult),
              reads=[rab, probsb], writes=[probsb])
        sh.op("vector", lambda e, t=t:
              e.tensor_scalar(probs_all[:, t * 8 + E:t * 8 + 8], onesf[:, 0:8 - E], 1.0, None, ALU.mult),
              reads=[onesf_b], writes=[probsb])
        bank2, bank2b = ptp[0], ptb[0]
        bank3, bank3b = pb[4], pbb[4]
        sh.op("tensor", lambda e, bank3=bank3, t=t:
              e.transpose(bank3[0:8, 0:128], probs_all[:, t * 8:t * 8 + 8], identf),
              reads=[probsb, identf_b], writes=[bank3b])
        sh.op("scalar", lambda e, bank3=bank3, t=t:
              e.copy(psT[0:8, t * 128:(t + 1) * 128], bank3[0:8, 0:128]),
              reads=[bank3b], writes=[psTb])

    sb.push(); sf.push()
    xg = [sf.alloc(D, f"mxg{i}") for i in range(2)]
    xq4 = [sb.alloc(D, f"mxq{i}") for i in range(4)]
    stats = [sf.alloc(8, f"mst{i}") for i in range(4)]
    junkm, junkmb = sb.alloc(D, "mjunk")
    rt = [sf.alloc(24, f"mrt{i}") for i in range(2)]
    mstg = [sb.alloc(512, f"mstg{i}") for i in range(2)]

    for t in range(OT):
        xa, xab = xg[t % 2]
        qa, qab = xq4[t % 4]
        sa, sab = stats[t % 4]
        sh.dma("gpsimd", lambda e, xa=xa, t=t:
               e.indirect_dma_start(xa, None, x_in[:, :],
                                    IndirectOffsetOnAxis(ap=rsel_i[:, t:t + 1], axis=0)),
               reads=[xin_b, rsel_ib], writes=[xab])
        emit_ln_stats(sh, xa, xab, D, 1e-5, junkm, junkmb, sa, sab)
        sh.op("vector", lambda e, qa=qa, xa=xa, sa=sa:
              e.tensor_scalar(qa, xa, sa[:, 2:3], sa[:, 3:4], ALU.subtract, ALU.mult),
              reads=[xab, sab], writes=[qab])
        if t % 4 == 3:
            for c in range(DC):
                bank, bankb = ptp[c % 2], ptb[c % 2]
                for j in range(4):
                    src2, src2b = xq4[j]
                    sh.op("tensor",
                          lambda e, bank=bank, j=j, src2=src2, c=c:
                          e.transpose(bank[:, j * 128:(j + 1) * 128],
                                      src2[:, c * 128:(c + 1) * 128], ident),
                          reads=[src2b, ident_b], writes=[bankb])
                sgx, sgxb = mstg[c % 2]
                sh.op("scalar", lambda e, sgx=sgx, bank=bank:
                      e.copy(sgx, bank[:, 0:512]),
                      reads=[bankb], writes=[sgxb])
                col0 = (t - 3) * 128
                sh.dma("sync", lambda e, sgx=sgx, c=c, col0=col0:
                       e.dma_start(out=xn_d[c, :, col0:col0 + 512], in_=sgx),
                       reads=[sgxb], writes=[xn_db])

    rxn_shared = Buf("rxn_g")
    xnT = []
    for d in range(DC):
        ap, b = sb.alloc(512, f"rxn{d}", buf=rxn_shared)
        xnT.append((ap, rxn_shared))
    for T in range(NT5):
        for d in range(DC):
            xa, xab = xnT[d]
            sh.dma("sync", lambda e, xa=xa, d=d, T=T:
                   e.dma_start(out=xa, in_=xn_d[d, :, T * 512:(T + 1) * 512]),
                   reads=[xn_db], writes=[xab])
        for tl in range(4):
            t = T * 4 + tl
            emit_router_tile(t, tl)

    sb.pop(); sf.pop()

    wex = []
    sb.push(); sf.push()
    wex_shared = Buf("wex_g")
    for d in range(DC):
        ap, b = sb.alloc(D, f"mw{d}", buf=wex_shared)
        wex.append((ap, wex_shared))
    ustg = [sb.alloc(512, f"mus{i}") for i in range(4)]
    sstg = [sb.alloc(512, f"mss{i}") for i in range(4)]
    gstg = [sb.alloc(512, f"mgs{i}") for i in range(4)]
    uslab_shared = [Buf("uslab0"), Buf("uslab1")]
    uslab = [(sb.alloc(DC * 512, f"musl{i}", buf=uslab_shared[i])[0], uslab_shared[i])
             for i in range(2)]
    gcol = [sb.alloc(DC * 128, f"mgc{i}") for i in range(2)]
    astg = [sf.alloc(512, f"mas{i}") for i in range(4)]
    acin = [sf.alloc(512, f"mai{i}") for i in range(4)]

    def load_wexp(wd, e=None):
        for d in range(DC):
            wa, wab = wex[d]
            if e is None:
                sh.dma("gpsimd", lambda e2, wa=wa, d=d, wd=wd:
                       e2.dma_start(out=wa, in_=wd[d * 128:(d + 1) * 128, :]),
                       writes=[wab])
            else:
                sh.dma("gpsimd", lambda e2, wa=wa, d=d, wd=wd, e=e:
                       e2.dma_start(out=wa, in_=wd[e, d * 128:(d + 1) * 128, :]),
                       writes=[wab])

    def expert_pass(eidx):
        shared = eidx == E
        if shared:
            load_wexp(W["wsu"])
            bup_ap, bup_b = bfm["bsu"]
        else:
            load_wexp(W["wup"], eidx)
            bup_ap, bup_b = bfm[("bup", eidx)]
        for T in range(NT5):
            ua_all, ua_allb = uslab[T % 2]
            sh.dma("sync", lambda e2, ua_all=ua_all, T=T:
                   e2.dma_start(out=ua_all.rearrange("p (c k) -> p c k", k=512),
                                in_=xn_d[:, :, T * 512:(T + 1) * 512].rearrange("c p k -> p c k")),
                   reads=[xn_db], writes=[ua_allb])
            for oc in range(DC):
                bank, bankb = pb[(oc * NT5 + T) % 4], pbb[(oc * NT5 + T) % 4]
                for d in range(DC):
                    wa, wab = wex[d]
                    sh.op("tensor",
                          lambda e2, bank=bank, wa=wa, ua_all=ua_all, oc=oc, d=d:
                          e2.matmul(bank[:], wa[:, oc * 128:(oc + 1) * 128],
                                    ua_all[:, d * 512:(d + 1) * 512],
                                    start=(d == 0), stop=(d == DC - 1)),
                          reads=[wab, ua_allb], writes=[bankb])
                sg, sgb = ustg[(oc * NT5 + T) % 4]
                sh.op("scalar", lambda e2, sg=sg, bank=bank, bup_ap=bup_ap, oc=oc:
                      e2.activation(sg, bank[:], AF.Identity, bias=bup_ap[:, oc:oc + 1], scale=1.0),
                      reads=[bankb, bup_b], writes=[sgb])
                sh.dma("sync", lambda e2, sg=sg, oc=oc, T=T:
                       e2.dma_start(out=u_d[:, oc, T * 512:(T + 1) * 512], in_=sg),
                       reads=[sgb], writes=[udb])
        if shared:
            load_wexp(W["wsg"])
            bg_ap, bg_b = bfm["bsg"]
        else:
            load_wexp(W["wgate"], eidx)
            bg_ap, bg_b = bfm[("bgate", eidx)]
        for T in range(NT5):
            ua_all, ua_allb = uslab[T % 2]
            sh.dma("sync", lambda e, ua_all=ua_all, T=T:
                   e.dma_start(out=ua_all.rearrange("p (c k) -> p c k", k=512),
                               in_=u_d[:, :, T * 512:(T + 1) * 512]),
                   reads=[udb], writes=[ua_allb])
            for oc in range(DC):
                bank, bankb = pb[(oc * NT5 + T) % 4], pbb[(oc * NT5 + T) % 4]
                for d in range(DC):
                    wa, wab = wex[d]
                    sh.op("tensor",
                          lambda e, bank=bank, wa=wa, ua_all=ua_all, oc=oc, d=d:
                          e.matmul(bank[:], wa[:, oc * 128:(oc + 1) * 128],
                                   ua_all[:, d * 512:(d + 1) * 512],
                                   start=(d == 0), stop=(d == DC - 1)),
                          reads=[wab, ua_allb], writes=[bankb])
                sg, sgb = sstg[(oc * NT5 + T) % 4]
                sh.op("scalar", lambda e, sg=sg, bank=bank, bg_ap=bg_ap, oc=oc:
                      e.activation(sg, bank[:], AF.Silu, bias=bg_ap[:, oc:oc + 1], scale=1.0),
                      reads=[bankb, bg_b], writes=[sgb])
                ga, gab = gstg[(oc * NT5 + T) % 4]
                sh.op("vector", lambda e, ga=ga, sg=sg, ua_all=ua_all, oc=oc:
                      e.tensor_mul(ga, sg, ua_all[:, oc * 512:(oc + 1) * 512]),
                      reads=[sgb, ua_allb], writes=[gab])
                sh.dma("sync", lambda e, ga=ga, oc=oc, T=T:
                       e.dma_start(out=g_d[:, oc, T * 512:(T + 1) * 512], in_=ga),
                       reads=[gab], writes=[gdb])
        if shared:
            load_wexp(W["wsd"])
        else:
            load_wexp(W["wdown"], eidx)
        for t in range(OT):
            gc, gcb = gcol[t % 2]
            sh.dma("gpsimd", lambda e, gc=gc, t=t:
                   e.dma_start(out=gc.rearrange("p (c k) -> p c k", k=128),
                               in_=g_d[:, :, t * 128:(t + 1) * 128]),
                   reads=[gdb], writes=[gcb])
            for dg in range(D // 512):
                bank, bankb = pb[(t * 4 + dg) % 4], pbb[(t * 4 + dg) % 4]
                for jc in range(DC):
                    wa, wab = wex[jc]
                    sh.op("tensor",
                          lambda e, bank=bank, gc=gc, wa=wa, jc=jc, dg=dg:
                          e.matmul(bank[:], gc[:, jc * 128:(jc + 1) * 128],
                                   wa[:, dg * 512:(dg + 1) * 512],
                                   start=(jc == 0), stop=(jc == DC - 1)),
                          reads=[gcb, wab], writes=[bankb])
                sg, sgb = astg[(t * 4 + dg) % 4]
                if eidx == 0:
                    sh.op("vector", lambda e, sg=sg, bank=bank, t=t:
                          e.tensor_scalar(sg, bank[:], probs_all[:, t * 8:t * 8 + 1], None, ALU.mult),
                          reads=[bankb, probsb], writes=[sgb])
                else:
                    aci, acib = acin[(t * 4 + dg) % 4]
                    sh.dma("scalar", lambda e, aci=aci, t=t, dg=dg:
                           e.dma_start(out=aci, in_=acc_d[t * 128:(t + 1) * 128, dg * 512:(dg + 1) * 512]),
                           reads=[accb], writes=[acib])
                    if shared:
                        sh.op("vector", lambda e, sg=sg, bank=bank, aci=aci:
                              e.scalar_tensor_tensor(sg, bank[:], 1.0, aci, ALU.mult, ALU.add),
                              reads=[bankb, acib], writes=[sgb])
                    else:
                        sh.op("vector", lambda e, sg=sg, bank=bank, aci=aci, t=t, eidx=eidx:
                              e.scalar_tensor_tensor(sg, bank[:], probs_all[:, t * 8 + eidx:t * 8 + eidx + 1],
                                                     aci, ALU.mult, ALU.add),
                              reads=[bankb, acib, probsb], writes=[sgb])
                sh.dma("sync", lambda e, sg=sg, t=t, dg=dg:
                       e.dma_start(out=acc_d[t * 128:(t + 1) * 128, dg * 512:(dg + 1) * 512], in_=sg),
                       reads=[sgb], writes=[accb])

    for eidx in range(E + 1):
        expert_pass(eidx)
    sb.pop(); sf.pop()

    sb.push(); sf.push()
    ot2 = [sf.alloc(D, f"m3o{i}") for i in range(2)]
    xg2 = [sf.alloc(D, f"m3x{i}") for i in range(2)]
    ost = [sf.alloc(8, f"m3s{i}") for i in range(2)]
    junk3, junk3b = sb.alloc(D, "m3junk")
    for t in range(OT):
        oa, oab = ot2[t % 2]
        xa, xab = xg2[t % 2]
        ra, rab = ost[t % 2]
        sh.dma("gpsimd", lambda e, xa=xa, t=t:
               e.indirect_dma_start(xa, None, x_in[:, :],
                                    IndirectOffsetOnAxis(ap=rsel_i[:, t:t + 1], axis=0)),
               reads=[xin_b, rsel_ib], writes=[xab])
        sh.dma("sync", lambda e, oa=oa, t=t: e.dma_start(out=oa, in_=acc_d[t * 128:(t + 1) * 128, :]),
               reads=[accb], writes=[oab])
        for dg in range(D // 512):
            bank, bankb = pb[dg % 2], pbb[dg % 2]
            sh.op("tensor",
                  lambda e, bank=bank, t=t, dg=dg:
                  e.matmul(bank[:], psT[0:8, t * 128:(t + 1) * 128],
                           bdall[0:8, dg * 512:(dg + 1) * 512], start=True, stop=True),
                  reads=[psTb, bdallb], writes=[bankb])
            sh.op("vector", lambda e, oa=oa, bank=bank, dg=dg:
                  e.tensor_add(oa[:, dg * 512:(dg + 1) * 512], oa[:, dg * 512:(dg + 1) * 512], bank[:]),
                  reads=[bankb, oab], writes=[oab])
        emit_ln_stats(sh, oa, oab, D, 1e-5, junk3, junk3b, ra, rab)
        sh.op("vector", lambda e, oa=oa, ra=ra:
              e.tensor_scalar(oa, oa, ra[:, 2:3], ra[:, 3:4], ALU.subtract, ALU.mult),
              reads=[oab, rab], writes=[oab])
        sh.op("vector", lambda e, oa=oa, xa=xa: e.tensor_add(oa, oa, xa),
              reads=[oab, xab], writes=[oab])
        sh.dma("sync", lambda e, oa=oa, t=t: e.dma_start(out=y_out[t * 128:(t + 1) * 128, :], in_=oa),
               reads=[oab], writes=[yout_b])
    sb.pop(); sf.pop()
    sb.pop(); sf.pop()



def make_consts():
    ident = np.eye(128, dtype=ml_dtypes.bfloat16)
    ones = np.ones((128, 128), dtype=ml_dtypes.bfloat16)
    col = np.arange(896, dtype=np.float32)[None, :]
    row = np.arange(128, dtype=np.float32)[:, None]
    mask = np.where(col <= row + 384, 0.0, NEG).astype(np.float32)
    return ident, ones, mask


ZIG0 = np.concatenate([np.arange(1024), np.arange(3072, 4096)]).astype(np.int32)
ZIG1 = np.arange(1024, 3072, dtype=np.int32)
FUSED = True


def _prep_host(inputs):
    ident, ones, mask = make_consts()
    identf = np.eye(128, dtype=np.float32)
    onesf = np.ones((128, 128), dtype=np.float32)
    T = lambda a: np.ascontiguousarray(np.asarray(a).T)
    A = lambda a: np.ascontiguousarray(np.asarray(a))
    shared = {
        "identc": ident, "onesc": ones, "maskc": mask,
        "identfc": identf, "onesfc": onesf,
        "wq": T(inputs["L0_attn_q_proj"]), "wk": T(inputs["L0_attn_k_proj"]),
        "wv": T(inputs["L0_attn_v_proj"]), "wo": T(inputs["L0_attn_o_proj"]),
        "wfc": T(inputs["L0_ffn_w_fc"]), "wproj": T(inputs["L0_ffn_w_proj"]),
        "wqkv": T(inputs["L1_attn_w_qkv"]), "wo1": T(inputs["L1_attn_w_o"]),
        "wup": A(inputs["L1_ffn_Wup"]), "wgate": A(inputs["L1_ffn_Wgate"]),
        "wdown": A(inputs["L1_ffn_Wdown"]),
        "wsu": A(inputs["L1_ffn_Wsu"]), "wsg": A(inputs["L1_ffn_Wsg"]),
        "wsd": A(inputs["L1_ffn_Wsd"]), "wr": A(inputs["L1_ffn_Wr"]),
        "br": A(inputs["L1_ffn_br"]).reshape(1, E),
        "gbias": A(inputs["L1_ffn_gate_bias"]).reshape(1, E),
        "bup": A(inputs["L1_ffn_bup"]).reshape(E, DC, 128),
        "bgate": A(inputs["L1_ffn_bgate"]).reshape(E, DC, 128),
        "bdown": A(inputs["L1_ffn_bdown"]).reshape(E, DC, 128),
        "bdown_r": A(inputs["L1_ffn_bdown"]),
        "bsu": A(inputs["L1_ffn_bsu"]).reshape(DC, 128),
        "bsg": A(inputs["L1_ffn_bsg"]).reshape(DC, 128),
        "bsd": A(inputs["L1_ffn_bsd"]).reshape(1, D),
    }
    x_full = np.asarray(inputs["L0_attn_in_features"])
    per_core = []
    for core in range(8):
        b, sub = core // 2, core % 2
        rows = ZIG0 if sub == 0 else ZIG1
        m = dict(shared)
        m["x0"] = np.ascontiguousarray(x_full[b])
        m["rowsel"] = np.ascontiguousarray(rows.reshape(OWN // 128, 128).T)
        per_core.append(m)
    return per_core


_BLOCK_KEYS = {
    "a0": ["x0", "identc", "onesc", "maskc", "identfc", "onesfc",
           "wq", "wk", "wv", "wo"],
    "f0": ["identc", "onesc", "maskc", "identfc", "onesfc", "wfc", "wproj"],
    "a1": ["identc", "onesc", "maskc", "identfc", "onesfc", "wqkv", "wo1"],
    "moe": ["identc", "onesc", "maskc", "identfc", "onesfc",
            "wup", "wgate", "wdown", "wsu", "wsg", "wsd", "wr", "br", "gbias",
            "bup", "bgate", "bdown", "bdown_r", "bsu", "bsg", "bsd", "rowsel"],
}


def kernel(**inputs):
    per_core = _prep_host(inputs)
    if FUSED:
        nc = build_program(blocks={"a0", "f0", "a1", "moe"}, boundaries={})
        keys = sorted(set(sum(_BLOCK_KEYS.values(), [])))
        in_maps = [{k: per_core[core][k] for k in keys} for core in range(8)]
        res = run_bass_kernel_spmd(nc, in_maps, core_ids=list(range(8)),
                                   trace_cores=[0])
        out = np.empty((4, S, D), dtype=np.float32)
        for core in range(8):
            b, sub = core // 2, core % 2
            rows = ZIG0 if sub == 0 else ZIG1
            out[b, rows, :] = np.asarray(res.results[core]["y"])
        return out
    carry = [dict() for _ in range(8)]
    stage_io = [
        ("a0", {"x1": "out"}, ["x1"]),
        ("f0", {"x1": "in", "x2": "out"}, ["x2"]),
        ("a1", {"x2": "in", "x3": "out"}, ["x3"]),
        ("moe", {"x3": "in"}, ["y"]),
    ]
    for blk, bnd, outs in stage_io:
        nc = build_program(blocks={blk}, boundaries=bnd)
        in_maps = []
        for core in range(8):
            m = {k: per_core[core][k] for k in _BLOCK_KEYS[blk]}
            for k, v in bnd.items():
                if v == "in":
                    m[k] = carry[core][k]
            in_maps.append(m)
        res = run_bass_kernel_spmd(nc, in_maps, core_ids=list(range(8)))
        for core in range(8):
            for k in outs:
                carry[core][k] = np.ascontiguousarray(res.results[core][k])
    out = np.empty((4, S, D), dtype=np.float32)
    for core in range(8):
        b, sub = core // 2, core % 2
        rows = ZIG0 if sub == 0 else ZIG1
        out[b, rows, :] = carry[core]["y"]
    return out


# version 13
# speedup vs baseline: 1.3813x; 1.0089x over previous
import contextlib
import numpy as np
import ml_dtypes
import concourse.bass as bass
import concourse.mybir as mybir
from concourse.bass_utils import run_bass_kernel_spmd

F32 = mybir.dt.float32
BF16 = mybir.dt.bfloat16
I32 = mybir.dt.int32
AF = mybir.ActivationFunctionType
ALU = mybir.AluOpType
AX = mybir.AxisListType

D = 2048
S = 4096
HEADS = 16
DK = 128
DC = D // 128
TT = S // 128
FF = 8192
FC = FF // 128
E = 7
OWN = 2048
NEG = -1.0e9
ISQ = 0.08838834764831845

ENGS = ("tensor", "vector", "scalar", "sync", "gpsimd")
DMA_ENGS = ("sync", "gpsimd", "scalar")



class Op:
    __slots__ = ("eng", "fn", "deps", "seq", "is_dma", "slot", "slot_ord",
                 "mark_ord", "is_barrier", "dma_snapshot")

    def __init__(self, eng, fn, is_dma, seq):
        self.eng = eng
        self.fn = fn
        self.deps = []
        self.seq = seq
        self.is_dma = is_dma
        self.slot = None
        self.slot_ord = 0
        self.mark_ord = None
        self.is_barrier = False
        self.dma_snapshot = None


class Buf:
    __slots__ = ("w", "r", "name", "dma_cnt", "sem_idx", "qclass")

    def __init__(self, name=""):
        self.w = None
        self.r = []
        self.name = name
        self.dma_cnt = 0
        self.sem_idx = None
        self.qclass = None

    def rd(self):
        return [self.w] if self.w is not None else []

    def wd(self):
        d = list(self.r)
        if self.w is not None:
            d.append(self.w)
        return d


class Sched:

    def __init__(self, nc):
        self.nc = nc
        self.ops = []
        self.slots = []
        self.last_by_eng = {}

    def op(self, eng, fn, reads=(), writes=(), dma=False):
        o = Op(eng, fn, dma, len(self.ops))
        if dma:
            assert len(writes) == 1, "each DMA must write exactly one Buf"
            b = writes[0]
            qc = "sw" if eng == "gpsimd" else "hw"
            if b.qclass is None:
                b.qclass = qc
                self.slots.append(b)
            assert b.qclass == qc, f"buf {b.name} written from both SWDGE and HWDGE queues"
            b.dma_cnt += 1
            o.slot = b
            o.slot_ord = b.dma_cnt
        deps = []
        for b in reads:
            deps += b.rd()
        for b in writes:
            deps += b.wd()
        seen = set()
        for d in deps:
            if d is None or d.seq in seen:
                continue
            seen.add(d.seq)
            if d.eng == "tensor" and eng == "tensor" and not d.is_dma and not dma:
                continue
            o.deps.append(d)
        self.ops.append(o)
        self.last_by_eng[(eng, dma)] = o
        for b in reads:
            b.r.append(o)
        for b in writes:
            b.w = o
            b.r = []
        return o

    def dma(self, eng, fn, reads=(), writes=()):
        return self.op(eng, fn, reads, writes, dma=True)

    def barrier(self):
        lasts = [d for d in self.last_by_eng.values()
                 if d.fn is not None and not d.is_dma]
        snap = [(b, b.dma_cnt) for b in self.slots if b.dma_cnt > 0]
        for e in ENGS:
            o = Op(e, None, False, len(self.ops))
            o.is_barrier = True
            o.deps = [d for d in lasts if d.eng != e]
            o.dma_snapshot = snap
            self.ops.append(o)

    def finalize(self, stack):
        nc = self.nc
        csem = {e: stack.enter_context(nc.semaphore(f"c_{e}")) for e in ENGS}
        barrier_seqs = [o.seq for o in self.ops if o.is_barrier]
        first = {}
        last = {}
        for o in self.ops:
            if o.is_dma:
                b = o.slot
                first.setdefault(id(b), o.seq)
                last[id(b)] = o.seq
            for d in o.deps:
                if d.is_dma:
                    last[id(d.slot)] = o.seq
        import bisect
        slots_sorted = sorted((b for b in self.slots if id(b) in first),
                              key=lambda b: first[id(b)])
        sem_free_at = []
        for b in slots_sorted:
            fs = first[id(b)]
            got = None
            for i, (qc, fa) in enumerate(sem_free_at):
                if qc == b.qclass and fa is not None and fa <= fs:
                    got = i
                    break
            if got is None:
                got = len(sem_free_at)
                sem_free_at.append((b.qclass, None))
            b.sem_idx = got
            ls = last[id(b)]
            j = bisect.bisect_right(barrier_seqs, ls)
            sem_free_at[got] = (b.qclass,
                                barrier_seqs[j] if j < len(barrier_seqs) else None)
        n_sems = len(sem_free_at)
        avail = len(bass.get_kernel_semaphore_range())
        assert n_sems + len(ENGS) <= avail - 2, \
            f"need {n_sems} slot sems + {len(ENGS)}, have {avail}"
        ssem = [stack.enter_context(nc.semaphore(f"s{i}")) for i in range(n_sems)]
        marked = set()
        for o in self.ops:
            for d in o.deps:
                if not d.is_dma:
                    marked.add(d.seq)
        ords = {e: 0 for e in ENGS}
        for o in self.ops:
            if not o.is_dma and not o.is_barrier and o.seq in marked:
                ords[o.eng] += 1
                o.mark_ord = ords[o.eng]
        semcnt = [0] * n_sems
        dma_semord = {}
        bar_semsnap = {}
        for o in self.ops:
            if o.is_dma:
                i = o.slot.sem_idx
                semcnt[i] += 1
                dma_semord[o.seq] = (i, semcnt[i])
            if o.is_barrier:
                bar_semsnap[o.seq] = list(semcnt)
        NCH = len(ENGS) + n_sems
        eng_ch = {e: i for i, e in enumerate(ENGS)}
        ZERO = [0] * NCH
        vc_of = {}
        last_eng_op = {}
        streams = {e: [] for e in ENGS}

        def dep_chan_val(d):
            if d.is_dma:
                i, ordn = dma_semord[d.seq]
                return len(ENGS) + i, 16 * ordn, ssem[i]
            if d.mark_ord is None:
                return None
            return eng_ch[d.eng], d.mark_ord, csem[d.eng]

        for o in self.ops:
            key = (o.eng, o.is_dma)
            prev = last_eng_op.get(key)
            ukey = o.eng
            prev = last_eng_op.get(ukey)
            vc = list(vc_of[prev]) if prev is not None else list(ZERO)
            if prev is not None:
                pass
            waits = []
            for d in o.deps:
                cv = dep_chan_val(d)
                if cv is None:
                    continue
                ch, val, sem = cv
                dvc = vc_of.get(d.seq)
                need = val > vc[ch]
                if need:
                    waits.append((sem, val))
                    vc[ch] = val
                if dvc is not None:
                    for c2 in range(NCH):
                        if dvc[c2] > vc[c2]:
                            vc[c2] = dvc[c2]
            if o.is_barrier:
                snap = bar_semsnap[o.seq]
                for i, cnt in enumerate(snap):
                    if cnt > 0 and 16 * cnt > vc[len(ENGS) + i]:
                        waits.append((ssem[i], 16 * cnt))
                        vc[len(ENGS) + i] = 16 * cnt
            vc_of[o.seq] = vc
            last_eng_op[o.eng] = o.seq
            if not o.is_dma and o.mark_ord is not None:
                pass
            incs = []
            if o.is_dma:
                i, ordn = dma_semord[o.seq]
                incs.append((ssem[i], 16))
            elif o.mark_ord is not None:
                incs.append((csem[o.eng], 1))
            streams[o.eng].append((waits, o.fn, incs))
        tail = []
        sync_last = last_eng_op.get("sync")
        sync_vc = vc_of.get(sync_last, list(ZERO)) if sync_last is not None else list(ZERO)
        for i in range(n_sems):
            if semcnt[i] > 0 and 16 * semcnt[i] > sync_vc[len(ENGS) + i]:
                tail.append((ssem[i], 16 * semcnt[i]))
        streams["sync"].append((tail, None, []))

        with nc.Block() as block:
            def run(engine, seq):
                for waits, fn, incs in seq:
                    for s, v in waits:
                        engine.wait_ge(s, v)
                    if fn is None:
                        continue
                    inst = fn(engine)
                    for s, n in incs:
                        inst.then_inc(s, n)

            @block.tensor
            def _(te):
                run(te, streams["tensor"])

            @block.vector
            def _(v):
                run(v, streams["vector"])

            @block.scalar
            def _(sc):
                run(sc, streams["scalar"])

            @block.sync
            def _(sy):
                run(sy, streams["sync"])

            @block.gpsimd
            def _(g):
                run(g, streams["gpsimd"])


class SbAlloc:

    def __init__(self, nc, stack, name, total_cols, dtype):
        self.t = stack.enter_context(nc.sbuf_tensor(name, [128, total_cols], dtype))
        self.total = total_cols
        self.off = 0
        self.marks = []
        self.hiwater = 0
        self.regions = []

    def alloc(self, cols, name="", buf=None):
        assert self.off + cols <= self.total, \
            f"SBUF pool overflow: {self.off}+{cols}>{self.total} ({name})"
        ap = self.t[:, self.off:self.off + cols]
        b = buf if buf is not None else Buf(name)
        s, e2 = self.off, self.off + cols
        keep = []
        pend = []
        for (rs, re, rb) in self.regions:
            if rs < e2 and re > s:
                pend += rb.r
                if rb.w is not None:
                    pend.append(rb.w)
            else:
                keep.append((rs, re, rb))
        self.regions = keep + [(s, e2, b)]
        seen = set()
        for o in pend:
            if o.seq not in seen:
                seen.add(o.seq)
                b.r.append(o)
        self.off = e2
        self.hiwater = max(self.hiwater, self.off)
        return ap, b

    def push(self):
        self.marks.append(self.off)

    def pop(self):
        self.off = self.marks.pop()



def emit_ln_stats(sh, xt_ap, xt_b, width, eps, junk_ap, junk_b, st_ap, st_b):
    s_sum, s_sq = st_ap[:, 0:1], st_ap[:, 1:2]
    s_m, s_rstd = st_ap[:, 2:3], st_ap[:, 3:4]
    s_tmp, s_var = st_ap[:, 4:5], st_ap[:, 5:6]
    s_negm = st_ap[:, 6:7]
    sh.op("vector", lambda e: e.reduce_sum(s_sum, xt_ap, axis=AX.X),
          reads=[xt_b], writes=[st_b])
    sh.op("scalar", lambda e: e.activation(junk_ap, xt_ap, AF.Square, accum_out=s_sq),
          reads=[xt_b], writes=[junk_b, st_b])
    sh.op("vector", lambda e: e.tensor_scalar(s_m, s_sum, 1.0 / width, None, ALU.mult),
          reads=[st_b], writes=[st_b])
    sh.op("vector", lambda e: e.tensor_scalar(s_negm, s_m, -1.0, None, ALU.mult),
          reads=[st_b], writes=[st_b])
    sh.op("vector", lambda e: e.tensor_scalar(s_tmp, s_m, s_m, None, ALU.mult),
          reads=[st_b], writes=[st_b])
    sh.op("vector", lambda e: e.tensor_scalar(s_var, s_sq, 1.0 / width, s_tmp, ALU.mult, ALU.subtract),
          reads=[st_b], writes=[st_b])
    sh.op("vector", lambda e: e.tensor_scalar(s_var, s_var, float(eps), None, ALU.add),
          reads=[st_b], writes=[st_b])
    sh.op("scalar", lambda e: e.activation(s_tmp, s_var, AF.Sqrt),
          reads=[st_b], writes=[st_b])
    sh.op("vector", lambda e: e.reciprocal(s_rstd, s_tmp),
          reads=[st_b], writes=[st_b])



def build_program(blocks, boundaries):
    nc = bass.Bass("TRN2", target_bir_lowering=False, debug=False,
                   detect_race_conditions=False)
    sh = Sched(nc)

    def stream_tensor(name, shape, dtype=F32):
        if name in boundaries:
            kind = "ExternalInput" if boundaries[name] == "in" else "ExternalOutput"
        else:
            kind = "Internal"
        return nc.dram_tensor(name, list(shape), dtype, kind=kind).ap()

    x0 = nc.dram_tensor("x0", [S, D], F32, kind="ExternalInput").ap() if "a0" in blocks else None
    x1 = stream_tensor("x1", [S, D]) if ("a0" in blocks or "f0" in blocks) else None
    x2 = stream_tensor("x2", [S, D]) if ("f0" in blocks or "a1" in blocks) else None
    x3 = stream_tensor("x3", [S, D]) if ("a1" in blocks or "moe" in blocks) else None
    yout = nc.dram_tensor("y", [OWN, D], F32, kind="ExternalOutput").ap() if "moe" in blocks else None

    ident_d = nc.dram_tensor("identc", [128, 128], BF16, kind="ExternalInput").ap()
    ones_d = nc.dram_tensor("onesc", [128, 128], BF16, kind="ExternalInput").ap()
    mask_d = nc.dram_tensor("maskc", [128, 896], F32, kind="ExternalInput").ap()
    identf_d = nc.dram_tensor("identfc", [128, 128], F32, kind="ExternalInput").ap()
    onesf_d = nc.dram_tensor("onesfc", [128, 128], F32, kind="ExternalInput").ap()

    W = {}

    def win(nm, shape, dt=F32):
        W[nm] = nc.dram_tensor(nm, list(shape), dt, kind="ExternalInput").ap()

    if "a0" in blocks:
        for nm in ("wq", "wk", "wv", "wo"):
            win(nm, [D, D])
    if "f0" in blocks:
        win("wfc", [D, FF])
        win("wproj", [FF, D])
    if "a1" in blocks:
        win("wqkv", [D, 3 * D])
        win("wo1", [D, D])
    if "moe" in blocks:
        win("wup", [E, D, D])
        win("wgate", [E, D, D])
        win("wdown", [E, D, D])
        win("wsu", [D, D])
        win("wsg", [D, D])
        win("wsd", [D, D])
        win("wr", [D, E])
        win("br", [1, E])
        win("gbias", [1, E])
        win("bup", [E, DC, 128])
        win("bgate", [E, DC, 128])
        win("bdown", [E, DC, 128])
        win("bsu", [DC, 128])
        win("bsg", [DC, 128])
        win("bsd", [1, D])
        win("bdown_r", [E, D])
        W["rowsel"] = nc.dram_tensor("rowsel", [128, OWN // 128], I32, kind="ExternalInput").ap()

    dsc = {}
    dsc["xn_fm"] = nc.dram_tensor("xn_fm", [DC, 128, S], BF16, kind="Internal").ap()
    dsc["q_fm"] = nc.dram_tensor("q_fm", [DC, 128, S], BF16, kind="Internal").ap()
    dsc["k_fm"] = nc.dram_tensor("k_fm", [DC, 128, S], BF16, kind="Internal").ap()
    dsc["attn_fm"] = nc.dram_tensor("attn_fm", [DC, 128, S], BF16, kind="Internal").ap()
    dsc["v_img"] = nc.dram_tensor("v_img", [HEADS, 128, TT, 128], BF16, kind="Internal").ap()
    dsc["h_tm"] = nc.dram_tensor("h_tm", [S, FF], BF16, kind="Internal").ap()
    dsc["xh2_fm"] = nc.dram_tensor("xh2_fm", [128, FC, S], BF16, kind="Internal").ap()
    dsc["t_proj"] = nc.dram_tensor("t_proj", [FF, D], BF16, kind="Internal").ap()
    dsc["out_proj"] = nc.dram_tensor("out_proj", [S, D], F32, kind="Internal").ap()
    dsc["u_d"] = nc.dram_tensor("u_d", [128, DC, OWN], BF16, kind="Internal").ap()
    dsc["g_d"] = nc.dram_tensor("g_d", [128, DC, OWN], BF16, kind="Internal").ap()
    dsc["acc_d"] = nc.dram_tensor("acc_d", [OWN, D], F32, kind="Internal").ap()
    dsc["dbg"] = nc.dram_tensor("dbg", [128, 16], F32, kind="Internal").ap()
    dbuf = {nm: Buf(nm) for nm in
            ("xn_fm", "q_fm", "k_fm", "v_img", "attn_fm", "h_tm", "xh2_fm",
             "t_proj", "out_proj", "u_d", "g_d", "acc_d", "x1", "x2", "x3", "y")}

    with contextlib.ExitStack() as stack:
        sb = SbAlloc(nc, stack, "sb_bf", 62000, BF16)
        sf = SbAlloc(nc, stack, "sb_f32", 18600, F32)

        ident, ident_b = sb.alloc(128, "ident")
        ones, ones_b = sb.alloc(128, "ones")
        mask, mask_b = sf.alloc(896, "mask")
        sh.dma("sync", lambda e: e.dma_start(out=ident, in_=ident_d[:, :]), writes=[ident_b])
        sh.dma("sync", lambda e: e.dma_start(out=ones, in_=ones_d[:, :]), writes=[ones_b])
        sh.dma("sync", lambda e: e.dma_start(out=mask, in_=mask_d[:, :]), writes=[mask_b])
        identf, identf_b = sf.alloc(128, "identf")
        onesf, onesf_b = sf.alloc(128, "onesf")
        sh.dma("sync", lambda e: e.dma_start(out=identf, in_=identf_d[:, :]), writes=[identf_b])
        sh.dma("sync", lambda e: e.dma_start(out=onesf, in_=onesf_d[:, :]), writes=[onesf_b])

        pb, pbb = [], []
        for i in range(6):
            t = stack.enter_context(nc.psum_tensor(f"pb{i}", [128, 512], F32))
            pb.append(t)
            pbb.append(Buf(f"pb{i}"))
        ptp, ptb = [], []
        for i in range(2):
            t = stack.enter_context(nc.psum_tensor(f"pt{i}", [128, 1024], BF16))
            ptp.append(t)
            ptb.append(Buf(f"pt{i}"))

        rsel_t = stack.enter_context(nc.sbuf_tensor("rsel_i32", [128, 16], I32))
        ctx = dict(nc=nc, sh=sh, sb=sb, sf=sf, W=W, dsc=dsc, dbuf=dbuf, rsel_t=rsel_t,
                   ident=ident, ident_b=ident_b, ones=ones, ones_b=ones_b,
                   mask=mask, mask_b=mask_b, identf=identf, identf_b=identf_b,
                   onesf=onesf, onesf_b=onesf_b, pb=pb, pbb=pbb, ptp=ptp, ptb=ptb)

        if "a0" in blocks:
            emit_attn(ctx, layer=0, x_in=x0, xin_b=Buf("x0"), x_out=x1, xout_b=dbuf["x1"])
        if "f0" in blocks:
            sh.barrier()
            emit_ffn0(ctx, x_in=x1, xin_b=dbuf["x1"], x_out=x2, xout_b=dbuf["x2"])
        if "a1" in blocks:
            sh.barrier()
            emit_attn(ctx, layer=1, x_in=x2, xin_b=dbuf["x2"], x_out=x3, xout_b=dbuf["x3"])
        if "moe" in blocks:
            sh.barrier()
            emit_moe(ctx, x_in=x3, xin_b=dbuf["x3"], y_out=yout, yout_b=dbuf["y"])

        sh.finalize(stack)
    return nc



def emit_attn(ctx, layer, x_in, xin_b, x_out, xout_b):
    sh, sb, sf = ctx["sh"], ctx["sb"], ctx["sf"]
    W, dsc, dbuf = ctx["W"], ctx["dsc"], ctx["dbuf"]
    ident, ident_b = ctx["ident"], ctx["ident_b"]
    mask, mask_b = ctx["mask"], ctx["mask_b"]
    pb, pbb, ptp, ptb = ctx["pb"], ctx["pbb"], ctx["ptp"], ctx["ptb"]
    EPS = 1e-5
    xn_fm, xnb = dsc["xn_fm"], dbuf["xn_fm"]
    q_fm, qb = dsc["q_fm"], dbuf["q_fm"]
    k_fm, kb = dsc["k_fm"], dbuf["k_fm"]
    v_img, vb = dsc["v_img"], dbuf["v_img"]
    attn_fm, ab = dsc["attn_fm"], dbuf["attn_fm"]

    sb.push(); sf.push()
    xt = [sf.alloc(D, f"xt{i}") for i in range(2)]
    xn_tm4 = [sb.alloc(D, f"xntm{i}") for i in range(4)]
    stats = [sf.alloc(8, f"st{i}") for i in range(4)]
    tstg = [sb.alloc(512, f"tstg{i}") for i in range(2)]

    for t in range(TT):
        xa, xab = xt[t % 2]
        oa, ob = xn_tm4[t % 4]
        sa, sab = stats[t % 4]
        sh.dma("sync", lambda e, xa=xa, t=t: e.dma_start(out=xa, in_=x_in[t * 128:(t + 1) * 128, :]),
               reads=[xin_b], writes=[xab])
        emit_ln_stats(sh, xa, xab, D, EPS, oa, ob, sa, sab)
        sh.op("vector", lambda e, oa=oa, xa=xa, sa=sa:
              e.tensor_scalar(oa, xa, sa[:, 2:3], sa[:, 3:4], ALU.subtract, ALU.mult),
              reads=[xab, sab], writes=[ob])
        if t % 4 == 3:
            for c in range(DC):
                bank, bankb = ptp[c % 2], ptb[c % 2]
                for j in range(4):
                    src, srcb = xn_tm4[j]
                    sh.op("tensor",
                          lambda e, bank=bank, j=j, src=src, c=c:
                          e.transpose(bank[:, j * 128:(j + 1) * 128],
                                      src[:, c * 128:(c + 1) * 128], ident),
                          reads=[srcb, ident_b], writes=[bankb])
                sg, sgb = tstg[c % 2]
                sh.op("scalar", lambda e, sg=sg, bank=bank: e.copy(sg, bank[:, 0:512]),
                      reads=[bankb], writes=[sgb])
                col0 = (t - 3) * 128
                sh.dma("sync",
                       lambda e, sg=sg, c=c, col0=col0:
                       e.dma_start(out=xn_fm[c, :, col0:col0 + 512], in_=sg),
                       reads=[sgb], writes=[xnb])
    sb.pop(); sf.pop()

    sb.push(); sf.push()
    wsb_sh = [Buf("wsb_g0"), Buf("wsb_g1")]
    wsb = []
    for d in range(DC):
        shb = wsb_sh[d // (DC // 2)]
        ap, b = sb.alloc(D, f"wsb{d}", buf=shb)
        wsb.append((ap, shb))
    xnT_shared = Buf("xnT_g")
    xnT_all, _ = sb.alloc(DC * 512, "xnT_all", buf=xnT_shared)
    xnT = [(xnT_all[:, d * 512:(d + 1) * 512], xnT_shared) for d in range(DC)]
    pstg = [sb.alloc(512, f"pstg{i}") for i in range(4)]

    if layer == 0:
        wsrc = {"q": (W["wq"], 0), "k": (W["wk"], 0), "v": (W["wv"], 0),
                "o": (W["wo"], 0)}
    else:
        wsrc = {"q": (W["wqkv"], 0), "k": (W["wqkv"], D), "v": (W["wqkv"], 2 * D),
                "o": (W["wo1"], 0)}

    def load_weights(tag):
        wd, off = wsrc[tag]
        for d in range(DC):
            wa, wab = wsb[d]
            sh.dma("gpsimd",
                   lambda e, wa=wa, d=d, wd=wd, off=off:
                   e.dma_start(out=wa, in_=wd[d * 128:(d + 1) * 128, off:off + D]),
                   writes=[wab])

    def load_xnT(T):
        sh.dma("sync",
               lambda e, T=T:
               e.dma_start(out=xnT_all.rearrange("p (c k) -> p c k", k=512),
                           in_=xn_fm[:, :, T * 512:(T + 1) * 512].rearrange("c p k -> p c k")),
               reads=[xnb], writes=[xnT_shared])

    for tag, dst, dstb in (("q", q_fm, qb), ("k", k_fm, kb)):
        load_weights(tag)
        for T in range(S // 512):
            load_xnT(T)
            for o in range(DC):
                bank, bankb = pb[o % 4], pbb[o % 4]
                for d in range(DC):
                    wa, wab = wsb[d]
                    xa, xab = xnT[d]
                    sh.op("tensor",
                          lambda e, bank=bank, wa=wa, xa=xa, o=o, d=d:
                          e.matmul(bank[:], wa[:, o * 128:(o + 1) * 128], xa,
                                   start=(d == 0), stop=(d == DC - 1)),
                          reads=[wab, xab], writes=[bankb])
                sg, sgb = pstg[o % 4]
                sh.op("scalar", lambda e, sg=sg, bank=bank: e.copy(sg, bank[:]),
                      reads=[bankb], writes=[sgb])
                sh.dma("sync",
                       lambda e, sg=sg, o=o, T=T, dst=dst:
                       e.dma_start(out=dst[o, :, T * 512:(T + 1) * 512], in_=sg),
                       reads=[sgb], writes=[dstb])

    load_weights("v")
    for T in range(S // 512):
        load_xnT(T)
        for tl in range(4):
            t = T * 4 + tl
            for og in range(D // 512):
                bank, bankb = pb[(tl * 4 + og) % 4], pbb[(tl * 4 + og) % 4]
                for d in range(DC):
                    wa, wab = wsb[d]
                    xa, xab = xnT[d]
                    sh.op("tensor",
                          lambda e, bank=bank, xa=xa, wa=wa, tl=tl, og=og, d=d:
                          e.matmul(bank[:], xa[:, tl * 128:(tl + 1) * 128],
                                   wa[:, og * 512:(og + 1) * 512],
                                   start=(d == 0), stop=(d == DC - 1)),
                          reads=[xab, wab], writes=[bankb])
                sg, sgb = pstg[(tl * 4 + og) % 4]
                sh.op("scalar", lambda e, sg=sg, bank=bank: e.copy(sg, bank[:]),
                      reads=[bankb], writes=[sgb])
                for j in range(4):
                    h = og * 4 + j
                    sh.dma("sync",
                           lambda e, sg=sg, h=h, t=t, j=j:
                           e.dma_start(out=v_img[h, :, t, :], in_=sg[:, j * 128:(j + 1) * 128]),
                           reads=[sgb], writes=[vb])
    sb.pop(); sf.pop()

    sb.push(); sf.push()
    qh = [sb.alloc(S, f"qh{i}") for i in range(2)]
    kh = [sb.alloc(S, f"kh{i}") for i in range(2)]
    vh = [sb.alloc(S, f"vh{i}") for i in range(2)]
    psb = [sb.alloc(S, f"p{i}") for i in range(4)]
    pts = [sb.alloc(512, f"pts{i}") for i in range(2)]
    avstg = [sb.alloc(512, f"avs{i}") for i in range(2)]
    scores = [sf.alloc(S, f"sc{i}") for i in range(4)]
    rowst = [sf.alloc(8, f"rs{i}") for i in range(4)]

    NQB = S // 512
    for h in range(HEADS):
        qa, qab = qh[h % 2]
        ka, kab = kh[h % 2]
        va, vab = vh[h % 2]
        sh.dma("sync", lambda e, qa=qa, h=h: e.dma_start(out=qa, in_=q_fm[h, :, :]),
               reads=[qb], writes=[qab])
        sh.dma("sync", lambda e, ka=ka, h=h: e.dma_start(out=ka, in_=k_fm[h, :, :]),
               reads=[kb], writes=[kab])
        sh.dma("sync", lambda e, va=va, h=h: e.dma_start(
            out=va, in_=v_img[h].rearrange("p t d -> p (t d)")),
            reads=[vb], writes=[vab])
        for qblk in range(NQB):
            nkt = qblk + 1
            for kt in range(nkt):
                for qs in range(4):
                    bank, bankb = pb[qs], pbb[qs]
                    sh.op("tensor",
                          lambda e, bank=bank, qa=qa, ka=ka, qblk=qblk, qs=qs, kt=kt:
                          e.matmul(bank[:], qa[:, qblk * 512 + qs * 128:qblk * 512 + (qs + 1) * 128],
                                   ka[:, kt * 512:(kt + 1) * 512], start=True, stop=True),
                          reads=[qab, kab], writes=[bankb])
                    dst, dstb = scores[qs]
                    if kt == qblk:
                        msl = mask[:, 384 - qs * 128: 896 - qs * 128]
                        sh.op("vector",
                              lambda e, dst=dst, bank=bank, msl=msl, kt=kt:
                              e.scalar_tensor_tensor(dst[:, kt * 512:(kt + 1) * 512], bank[:],
                                                     ISQ, msl, ALU.mult, ALU.add),
                              reads=[bankb, mask_b], writes=[dstb])
                    elif qs < 2:
                        sh.op("vector",
                              lambda e, dst=dst, bank=bank, kt=kt:
                              e.tensor_scalar(dst[:, kt * 512:(kt + 1) * 512], bank[:],
                                              ISQ, None, ALU.mult),
                              reads=[bankb], writes=[dstb])
                    else:
                        sh.op("scalar",
                              lambda e, dst=dst, bank=bank, kt=kt:
                              e.mul(dst[:, kt * 512:(kt + 1) * 512], bank[:], ISQ),
                              reads=[bankb], writes=[dstb])
            for qs in range(4):
                sca, scab = scores[qs]
                pa, pab = psb[qs]
                ra, rab = rowst[qs]
                klen = nkt * 512
                sh.op("vector", lambda e, ra=ra, sca=sca, klen=klen:
                      e.reduce_max(ra[:, 0:1], sca[:, 0:klen], axis=AX.X),
                      reads=[scab], writes=[rab])
                sh.op("vector", lambda e, ra=ra:
                      e.tensor_scalar(ra[:, 3:4], ra[:, 0:1], -1.0, None, ALU.mult),
                      reads=[rab], writes=[rab])
                sh.op("scalar", lambda e, pa=pa, sca=sca, ra=ra, klen=klen:
                      e.activation(pa[:, 0:klen], sca[:, 0:klen], AF.Exp,
                                   bias=ra[:, 3:4], scale=1.0, accum_out=ra[:, 1:2]),
                      reads=[scab, rab], writes=[pab, rab])
                sh.op("vector", lambda e, ra=ra: e.reciprocal(ra[:, 2:3], ra[:, 1:2]),
                      reads=[rab], writes=[rab])
                sh.op("vector", lambda e, pa=pa, ra=ra, klen=klen:
                      e.tensor_scalar(pa[:, 0:klen], pa[:, 0:klen], ra[:, 2:3], None, ALU.mult),
                      reads=[rab, pab], writes=[pab])
            avbank, avbankb = pb[4], pbb[4]
            for kc in range(nkt * 4):
                bank, bankb = ptp[kc % 2], ptb[kc % 2]
                for qs in range(4):
                    pa, pab = psb[qs]
                    sh.op("tensor",
                          lambda e, bank=bank, pa=pa, qs=qs, kc=kc:
                          e.transpose(bank[:, qs * 128:(qs + 1) * 128],
                                      pa[:, kc * 128:(kc + 1) * 128], ident),
                          reads=[pab, ident_b], writes=[bankb])
                sg, sgb = pts[kc % 2]
                sh.op("vector", lambda e, sg=sg, bank=bank: e.tensor_copy(sg, bank[:, 0:512]),
                      reads=[bankb], writes=[sgb])
                sh.op("tensor",
                      lambda e, va=va, sg=sg, kc=kc, first=(kc == 0), last=(kc == nkt * 4 - 1),
                      avbank=avbank:
                      e.matmul(avbank[:], va[:, kc * 128:(kc + 1) * 128], sg,
                               start=first, stop=last),
                      reads=[vab, sgb], writes=[avbankb])
            sg, sgb = avstg[qblk % 2]
            sh.op("scalar", lambda e, sg=sg, avbank=avbank: e.copy(sg, avbank[:]),
                  reads=[avbankb], writes=[sgb])
            sh.dma("sync",
                   lambda e, sg=sg, h=h, qblk=qblk:
                   e.dma_start(out=attn_fm[h, :, qblk * 512:(qblk + 1) * 512], in_=sg),
                   reads=[sgb], writes=[ab])
    sb.pop(); sf.pop()

    sb.push(); sf.push()
    wosb_shared = Buf("wosb_g")
    wosb = []
    for d in range(DC):
        ap, b = sb.alloc(D, f"wo{d}", buf=wosb_shared)
        wosb.append((ap, wosb_shared))
        wd, off = wsrc["o"]
        sh.dma("gpsimd",
               lambda e, ap=ap, d=d, wd=wd, off=off:
               e.dma_start(out=ap, in_=wd[d * 128:(d + 1) * 128, off:off + D]),
               writes=[wosb_shared])
    att = [sb.alloc(D, f"att{i}") for i in range(2)]
    junk, junkb = sb.alloc(D, "junk")
    ot = [sf.alloc(D, f"ot{i}") for i in range(2)]
    res = [sf.alloc(D, f"res{i}") for i in range(2)]
    ost = [sf.alloc(8, f"ost{i}") for i in range(2)]

    for t in range(TT):
        aa, aab = att[t % 2]
        sh.dma("sync", lambda e, aa=aa, t=t: e.dma_start(
            out=aa.rearrange("p (c k) -> p c k", k=128),
            in_=attn_fm[:, :, t * 128:(t + 1) * 128].rearrange("c p k -> p c k")),
            reads=[ab], writes=[aab])
        oa, oab = ot[t % 2]
        for og in range(D // 512):
            bank, bankb = pb[og % 4], pbb[og % 4]
            for d in range(DC):
                wa, wab = wosb[d]
                sh.op("tensor",
                      lambda e, bank=bank, aa=aa, wa=wa, d=d, og=og:
                      e.matmul(bank[:], aa[:, d * 128:(d + 1) * 128],
                               wa[:, og * 512:(og + 1) * 512],
                               start=(d == 0), stop=(d == DC - 1)),
                      reads=[aab, wab], writes=[bankb])
            sh.op("scalar", lambda e, oa=oa, og=og, bank=bank:
                  e.copy(oa[:, og * 512:(og + 1) * 512], bank[:]),
                  reads=[bankb], writes=[oab])
        ra, rab = ost[t % 2]
        emit_ln_stats(sh, oa, oab, D, EPS, junk, junkb, ra, rab)
        xa2, xab2 = res[t % 2]
        sh.dma("scalar", lambda e, xa2=xa2, t=t:
               e.dma_start(out=xa2, in_=x_in[t * 128:(t + 1) * 128, :]),
               reads=[xin_b], writes=[xab2])
        sh.op("vector", lambda e, oa=oa, ra=ra:
              e.tensor_scalar(oa, oa, ra[:, 2:3], ra[:, 3:4], ALU.subtract, ALU.mult),
              reads=[oab, rab], writes=[oab])
        sh.op("vector", lambda e, oa=oa, xa2=xa2: e.tensor_add(oa, oa, xa2),
              reads=[oab, xab2], writes=[oab])
        sh.dma("sync", lambda e, oa=oa, t=t: e.dma_start(out=x_out[t * 128:(t + 1) * 128, :], in_=oa),
               reads=[oab], writes=[xout_b])
    sb.pop(); sf.pop()


def emit_ffn0(ctx, x_in, xin_b, x_out, xout_b):
    sh, sb, sf = ctx["sh"], ctx["sb"], ctx["sf"]
    W, dsc, dbuf = ctx["W"], ctx["dsc"], ctx["dbuf"]
    ident, ident_b = ctx["ident"], ctx["ident_b"]
    identf, identf_b = ctx["identf"], ctx["identf_b"]
    onesf, onesf_b = ctx["onesf"], ctx["onesf_b"]
    pb, pbb, ptp, ptb = ctx["pb"], ctx["pbb"], ctx["ptp"], ctx["ptb"]
    wfc, wproj = W["wfc"], W["wproj"]
    h_tm, hb = dsc["h_tm"], dbuf["h_tm"]
    xh2_fm, x2fb = dsc["xh2_fm"], dbuf["xh2_fm"]
    t_proj, tpb = dsc["t_proj"], dbuf["t_proj"]
    out_proj, opb = dsc["out_proj"], dbuf["out_proj"]
    CLIP = 128.0 - 1e-6

    sb.push(); sf.push()
    gb_fc, gb_fcb = sf.alloc(4, "gb_fc")
    gb_pr, gb_prb = sf.alloc(4, "gb_pr")
    hsum, hsumb = sf.alloc(TT, "hsum")
    hsq, hsqb = sf.alloc(TT, "hsq")
    hmax, hmaxb = sf.alloc(TT, "hmax")
    hmin, hminb = sf.alloc(TT, "hmin")
    sct, sctb = sf.alloc(TT, "sct")
    sc2, sc2b = sf.alloc(TT, "sc2")
    ln3, ln3b = sf.alloc(3 * TT, "ln3")

    sb.push(); sf.push()
    gsc, gscb = sf.alloc(16, "gsc")
    grow, growb = sf.alloc(16, "grow")
    rowx, rowxb = sf.alloc(128, "rowx")
    slab = [sf.alloc(FF, f"gslab{i}") for i in range(2)]
    acc_fc = gsc[:, 0:1]; acc_pr = gsc[:, 1:2]; tmp0 = gsc[:, 2:3]

    def gamma_pass(wd, n_slabs, width, acc_col):
        for i in range(n_slabs):
            sa, sab = slab[i % 2]
            sh.dma("sync", lambda e, sa=sa, i=i, wd=wd, width=width:
                   e.dma_start(out=sa[:, 0:width], in_=wd[i * 128:(i + 1) * 128, :]),
                   writes=[sab])
            sh.op("vector", lambda e, sa=sa, width=width:
                  e.reduce_sum(tmp0, sa[:, 0:width], axis=AX.X, apply_absolute_value=True),
                  reads=[sab], writes=[gscb])
            if i == 0:
                sh.op("vector", lambda e, acc_col=acc_col:
                      e.tensor_scalar(acc_col, tmp0, 1.0, None, ALU.mult),
                      reads=[gscb], writes=[gscb])
            else:
                sh.op("vector", lambda e, acc_col=acc_col:
                      e.tensor_add(acc_col, acc_col, tmp0),
                      reads=[gscb], writes=[gscb])

    gamma_pass(wfc, DC, FF, acc_fc)
    gamma_pass(wproj, FC, D, acc_pr)


    for idx, (acc_col, nelem, gb, gbb) in enumerate(
            ((acc_fc, D * FF, gb_fc, gb_fcb), (acc_pr, FF * D, gb_pr, gb_prb))):
        bank, bankb = pb[idx % 2], pbb[idx % 2]
        sh.op("tensor", lambda e, bank=bank, acc_col=acc_col:
              e.transpose(bank[0:1, 0:128], acc_col, identf),
              reads=[gscb, identf_b], writes=[bankb])
        r0 = grow[0:1, idx * 4:idx * 4 + 1]
        r1 = grow[0:1, idx * 4 + 1:idx * 4 + 2]
        r2 = grow[0:1, idx * 4 + 2:idx * 4 + 3]
        sh.op("vector", lambda e, bank=bank, r0=r0:
              e.reduce_sum(r0, bank[0:1, 0:128], axis=AX.X),
              reads=[bankb], writes=[growb])
        sh.op("vector", lambda e, r0=r0, r1=r1, nelem=nelem:
              e.tensor_scalar(r1, r0, 1.0 / nelem, None, ALU.mult),
              reads=[growb], writes=[growb])
        sh.op("vector", lambda e, r1=r1, r2=r2:
              e.tensor_scalar(r2, r1, 1e-6, 0.5, ALU.add, ALU.mult),
              reads=[growb], writes=[growb])
        bank2, bank2b = pb[2 + idx], pbb[2 + idx]
        sh.op("tensor", lambda e, bank2=bank2, idx=idx:
              e.matmul(bank2[:, 0:2], onesf[0:1, 0:128], grow[0:1, idx * 4 + 1:idx * 4 + 3],
                       start=True, stop=True),
              reads=[onesf_b, growb], writes=[bank2b])
        sh.op("scalar", lambda e, gb=gb, bank2=bank2: e.copy(gb[:, 0:2], bank2[:, 0:2]),
              reads=[bank2b], writes=[gbb])
        sh.op("vector", lambda e, gb=gb:
              e.tensor_scalar(gb[:, 2:3], gb[:, 1:2], -1.0, None, ALU.mult),
              reads=[gbb], writes=[gbb])
    sb.pop(); sf.pop()
    sh.barrier()

    sb.push(); sf.push()
    xhat_fm = []
    for c in range(DC):
        ap, bb = sb.alloc(S // 2, f"xhfm{c}")
        xhat_fm.append((ap, bb))
    xt = [sf.alloc(D, f"fxt{i}") for i in range(2)]
    xq4 = [sb.alloc(D, f"fxq{i}") for i in range(4)]
    stats = [sf.alloc(16, f"fst{i}") for i in range(4)]
    junkf, junkfb = sf.alloc(D, "fjunk")
    tfc2 = []
    for par in range(2):
        row = []
        for d in range(DC):
            ap, bb = sb.alloc(512, f"tfc{par}_{d}")
            row.append((ap, bb))
        tfc2.append(row)
    wtile = [sf.alloc(512, f"fwt{i}") for i in range(4)]
    m1t = [sf.alloc(512, f"fm1{i}") for i in range(4)]
    hstg = [sf.alloc(512, f"fhs{i}") for i in range(4)]
    hbf = [sb.alloc(512, f"fhb{i}") for i in range(4)]
    hjunk, hjunkb = sf.alloc(512, "fhjunk")
    hps = [sf.alloc(4, f"fhp{i}") for i in range(4)]
    NOB = FF // 512
    HTT = TT // 2

    for half in range(2):
        t0h = half * HTT
        for tl in range(HTT):
            t = t0h + tl
            xa, xab = xt[t % 2]
            qa, qab = xq4[t % 4]
            sa, sab = stats[t % 4]
            sh.dma("scalar", lambda e, xa=xa, t=t: e.dma_start(out=xa, in_=x_in[t * 128:(t + 1) * 128, :]),
                   reads=[xin_b], writes=[xab])
            emit_ln_stats(sh, xa, xab, D, 1e-5, junkf, junkfb, sa, sab)
            sh.op("vector", lambda e, sa=sa: e.tensor_scalar(sa[:, 7:8], sa[:, 5:6], -1e-5, None, ALU.add),
                  reads=[sab], writes=[sab])
            sh.op("vector", lambda e, sa=sa: e.tensor_scalar(sa[:, 8:9], sa[:, 3:4], sa[:, 3:4], None, ALU.mult),
                  reads=[sab], writes=[sab])
            sh.op("vector", lambda e, sa=sa: e.tensor_scalar(sa[:, 8:9], sa[:, 8:9], sa[:, 7:8], None, ALU.mult),
                  reads=[sab], writes=[sab])
            sh.op("vector", lambda e, sa=sa: e.tensor_scalar(sa[:, 8:9], sa[:, 8:9], 1e-6, None, ALU.add),
                  reads=[sab], writes=[sab])
            sh.op("scalar", lambda e, sa=sa: e.activation(sa[:, 9:10], sa[:, 8:9], AF.Sqrt),
                  reads=[sab], writes=[sab])
            sh.op("vector", lambda e, sa=sa: e.reciprocal(sa[:, 10:11], sa[:, 9:10]),
                  reads=[sab], writes=[sab])
            sh.op("vector", lambda e, sa=sa: e.tensor_scalar(sa[:, 11:12], sa[:, 3:4], sa[:, 10:11], None, ALU.mult),
                  reads=[sab], writes=[sab])
            sh.op("vector", lambda e, sa=sa: e.tensor_scalar(sa[:, 12:13], sa[:, 2:3], sa[:, 11:12], -1.0, ALU.mult, ALU.mult),
                  reads=[sab], writes=[sab])
            sh.op("scalar", lambda e, xa=xa, sa=sa:
                  e.activation(junkf, xa, AF.Identity, bias=sa[:, 12:13], scale=sa[:, 11:12]),
                  reads=[xab, sab], writes=[junkfb])
            sh.op("vector", lambda e, sa=sa: e.reduce_max(sa[:, 13:14], junkf, axis=AX.X, apply_absolute_value=True),
                  reads=[junkfb], writes=[sab])
            sh.op("vector", lambda e, sa=sa: e.reciprocal(sa[:, 14:15], sa[:, 13:14]),
                  reads=[sab], writes=[sab])
            sh.op("vector", lambda e, qa=qa, sa=sa:
                  e.tensor_scalar(qa, junkf, sa[:, 14:15], 128.0, ALU.mult, ALU.mult),
                  reads=[junkfb, sab], writes=[qab])
            sh.op("vector", lambda e, qa=qa:
                  e.tensor_scalar(qa, qa, CLIP, -CLIP, ALU.min, ALU.max),
                  reads=[qab], writes=[qab])
            sh.op("vector", lambda e, sa=sa, t=t:
                  e.tensor_scalar(sct[:, t:t + 1], sa[:, 13:14], gb_fc[:, 0:1], 0.0078125, ALU.mult, ALU.mult),
                  reads=[sab, gb_fcb], writes=[sctb])
            if t % 4 == 3:
                for c in range(DC):
                    bank, bankb = ptp[c % 2], ptb[c % 2]
                    for j in range(4):
                        src2, src2b = xq4[j]
                        sh.op("tensor",
                              lambda e, bank=bank, j=j, src2=src2, c=c:
                              e.transpose(bank[:, j * 128:(j + 1) * 128],
                                          src2[:, c * 128:(c + 1) * 128], ident),
                              reads=[src2b, ident_b], writes=[bankb])
                    dst, dstb = xhat_fm[c]
                    col0 = (t - 3 - t0h) * 128
                    sh.op("scalar", lambda e, dst=dst, bank=bank, col0=col0:
                          e.copy(dst[:, col0:col0 + 512], bank[:, 0:512]),
                          reads=[bankb], writes=[dstb])

        for o in range(NOB):
            tfc = tfc2[o % 2]
            for d in range(DC):
                wa, wab = wtile[d % 4]
                ma, mab = m1t[d % 4]
                ta, tab = tfc[d]
                sh.dma("sync", lambda e, wa=wa, d=d, o=o:
                       e.dma_start(out=wa, in_=wfc[d * 128:(d + 1) * 128, o * 512:(o + 1) * 512]),
                       writes=[wab])
                sh.op("vector", lambda e, ma=ma, wa=wa:
                      e.tensor_scalar(ma, wa, gb_fc[:, 1:2], None, ALU.is_gt),
                      reads=[wab, gb_fcb], writes=[mab])
                sh.op("vector", lambda e, wa=wa:
                      e.tensor_scalar(wa, wa, gb_fc[:, 2:3], None, ALU.is_lt),
                      reads=[wab, gb_fcb], writes=[wab])
                sh.op("vector", lambda e, ta=ta, ma=ma, wa=wa: e.tensor_sub(ta, ma, wa),
                      reads=[mab, wab], writes=[tab])
            for tl in range(HTT):
                t = t0h + tl
                bank, bankb = pb[t % 4], pbb[t % 4]
                for d in range(DC):
                    xa, xab = xhat_fm[d]
                    ta, tab = tfc[d]
                    sh.op("tensor",
                          lambda e, bank=bank, xa=xa, ta=ta, tl=tl, d=d:
                          e.matmul(bank[:], xa[:, tl * 128:(tl + 1) * 128], ta,
                                   start=(d == 0), stop=(d == DC - 1)),
                          reads=[xab, tab], writes=[bankb])
                hs, hsb2 = hstg[t % 4]
                hba, hbab = hbf[t % 4]
                hp, hpb = hps[t % 4]
                sh.op("vector", lambda e, hs=hs, bank=bank, t=t:
                      e.tensor_scalar(hs, bank[:], sct[:, t:t + 1], None, ALU.mult),
                      reads=[bankb, sctb], writes=[hsb2])
                sh.op("scalar", lambda e, hba=hba, hs=hs, hp=hp:
                      e.activation(hba, hs, AF.Gelu, accum_out=hp[:, 0:1]),
                      reads=[hsb2], writes=[hbab, hpb])
                sh.op("scalar", lambda e, hba=hba, hp=hp:
                      e.activation(hjunk, hba, AF.Square, accum_out=hp[:, 1:2]),
                      reads=[hbab], writes=[hjunkb, hpb])
                sh.op("vector", lambda e, hba=hba, hp=hp:
                      e.reduce_max(hp[:, 2:3], hba, axis=AX.X),
                      reads=[hbab], writes=[hpb])
                sh.op("vector", lambda e, hba=hba, hp=hp:
                      e.tensor_reduce(hp[:, 3:4], hba, axis=AX.X, op=ALU.min),
                      reads=[hbab], writes=[hpb])
                if o == 0:
                    for dstc, dstbb, col in ((hsum, hsumb, 0), (hsq, hsqb, 1),
                                             (hmax, hmaxb, 2), (hmin, hminb, 3)):
                        sh.op("vector", lambda e, dstc=dstc, hp=hp, t=t, col=col:
                              e.tensor_scalar(dstc[:, t:t + 1], hp[:, col:col + 1], 1.0, None, ALU.mult),
                              reads=[hpb], writes=[dstbb])
                else:
                    sh.op("vector", lambda e, hp=hp, t=t:
                          e.tensor_add(hsum[:, t:t + 1], hsum[:, t:t + 1], hp[:, 0:1]),
                          reads=[hpb, hsumb], writes=[hsumb])
                    sh.op("vector", lambda e, hp=hp, t=t:
                          e.tensor_add(hsq[:, t:t + 1], hsq[:, t:t + 1], hp[:, 1:2]),
                          reads=[hpb, hsqb], writes=[hsqb])
                    sh.op("vector", lambda e, hp=hp, t=t:
                          e.tensor_max(hmax[:, t:t + 1], hmax[:, t:t + 1], hp[:, 2:3]),
                          reads=[hpb, hmaxb], writes=[hmaxb])
                    sh.op("vector", lambda e, hp=hp, t=t:
                          e.tensor_tensor(hmin[:, t:t + 1], hmin[:, t:t + 1], hp[:, 3:4], ALU.min),
                          reads=[hpb, hminb], writes=[hminb])
                sh.dma("sync", lambda e, hba=hba, t=t, o=o:
                       e.dma_start(out=h_tm[t * 128:(t + 1) * 128, o * 512:(o + 1) * 512], in_=hba),
                       reads=[hbab], writes=[hb])
    sb.pop(); sf.pop()

    sb.push(); sf.push()
    pslab = [sf.alloc(D, f"ps{i}") for i in range(2)]
    pmm = [sf.alloc(D, f"pm{i}") for i in range(2)]
    ptt = [sb.alloc(D, f"ptt{i}") for i in range(2)]
    for i in range(FC):
        wa, wab = pslab[i % 2]
        ma, mab = pmm[i % 2]
        ta, tab = ptt[i % 2]
        sh.dma("sync", lambda e, wa=wa, i=i:
               e.dma_start(out=wa, in_=wproj[i * 128:(i + 1) * 128, :]),
               writes=[wab])
        sh.op("vector", lambda e, ma=ma, wa=wa:
              e.tensor_scalar(ma, wa, gb_pr[:, 1:2], None, ALU.is_gt),
              reads=[wab, gb_prb], writes=[mab])
        sh.op("vector", lambda e, wa=wa:
              e.tensor_scalar(wa, wa, gb_pr[:, 2:3], None, ALU.is_lt),
              reads=[wab, gb_prb], writes=[wab])
        sh.op("vector", lambda e, ta=ta, ma=ma, wa=wa: e.tensor_sub(ta, ma, wa),
              reads=[mab, wab], writes=[tab])
        sh.dma("sync", lambda e, ta=ta, i=i:
               e.dma_start(out=t_proj[i * 128:(i + 1) * 128, :], in_=ta),
               reads=[tab], writes=[tpb])
    sb.pop(); sf.pop()

    sb.push(); sf.push()
    hta = [sb.alloc(FF, f"dht{i}") for i in range(2)]
    ya, yab = sf.alloc(FF, "dy")
    xq2 = [sb.alloc(FF, f"dxq{i}") for i in range(2)]
    dstt = [sf.alloc(16, f"dst{i}") for i in range(2)]
    d2stg = [sb.alloc(512, f"d2s{i}") for i in range(4)]
    for t in range(TT):
        ha, hab = hta[t % 2]
        qa, qab = xq2[t % 2]
        sa, sab = dstt[t % 2]
        sh.dma("sync", lambda e, ha=ha, t=t: e.dma_start(out=ha, in_=h_tm[t * 128:(t + 1) * 128, :]),
               reads=[hb], writes=[hab])
        sh.op("vector", lambda e, sa=sa, t=t: e.tensor_scalar(sa[:, 0:1], hsum[:, t:t + 1], 1.0 / FF, None, ALU.mult),
              reads=[hsumb], writes=[sab])
        sh.op("vector", lambda e, sa=sa: e.tensor_scalar(sa[:, 1:2], sa[:, 0:1], sa[:, 0:1], None, ALU.mult),
              reads=[sab], writes=[sab])
        sh.op("vector", lambda e, sa=sa, t=t: e.tensor_scalar(sa[:, 2:3], hsq[:, t:t + 1], 1.0 / FF, sa[:, 1:2], ALU.mult, ALU.subtract),
              reads=[hsqb, sab], writes=[sab])
        sh.op("vector", lambda e, sa=sa: e.tensor_scalar(sa[:, 2:3], sa[:, 2:3], 1e-6, None, ALU.add),
              reads=[sab], writes=[sab])
        sh.op("scalar", lambda e, sa=sa: e.activation(sa[:, 1:2], sa[:, 2:3], AF.Sqrt),
              reads=[sab], writes=[sab])
        sh.op("vector", lambda e, sa=sa: e.reciprocal(sa[:, 2:3], sa[:, 1:2]),
              reads=[sab], writes=[sab])
        sh.op("vector", lambda e, sa=sa, t=t: e.tensor_scalar(sa[:, 3:4], hmax[:, t:t + 1], sa[:, 0:1], None, ALU.subtract),
              reads=[hmaxb, sab], writes=[sab])
        sh.op("vector", lambda e, sa=sa, t=t: e.tensor_scalar(sa[:, 4:5], hmin[:, t:t + 1], -1.0, sa[:, 0:1], ALU.mult, ALU.add),
              reads=[hminb, sab], writes=[sab])
        sh.op("vector", lambda e, sa=sa: e.tensor_tensor(sa[:, 5:6], sa[:, 3:4], sa[:, 4:5], ALU.max),
              reads=[sab], writes=[sab])
        sh.op("vector", lambda e, sa=sa: e.tensor_scalar(sa[:, 5:6], sa[:, 5:6], sa[:, 2:3], None, ALU.mult),
              reads=[sab], writes=[sab])
        sh.op("vector", lambda e, sa=sa: e.reciprocal(sa[:, 6:7], sa[:, 5:6]),
              reads=[sab], writes=[sab])
        sh.op("vector", lambda e, sa=sa: e.tensor_scalar(sa[:, 6:7], sa[:, 6:7], sa[:, 2:3], 128.0, ALU.mult, ALU.mult),
              reads=[sab], writes=[sab])
        sh.op("vector", lambda e, sa=sa: e.tensor_scalar(sa[:, 7:8], sa[:, 0:1], sa[:, 6:7], -1.0, ALU.mult, ALU.mult),
              reads=[sab], writes=[sab])
        sh.op("vector", lambda e, sa=sa, t=t:
              e.tensor_scalar(sc2[:, t:t + 1], sa[:, 5:6], gb_pr[:, 0:1], 0.0078125, ALU.mult, ALU.mult),
              reads=[sab, gb_prb], writes=[sc2b])
        sh.op("scalar", lambda e, ha=ha, ya=ya, sa=sa:
              e.activation(ya, ha, AF.Identity, bias=sa[:, 7:8], scale=sa[:, 6:7]),
              reads=[hab, sab], writes=[yab])
        sh.op("vector", lambda e, qa=qa, ya=ya:
              e.tensor_scalar(qa, ya, CLIP, -CLIP, ALU.min, ALU.max),
              reads=[yab], writes=[qab])
        for c4 in range(FC // 4):
            bank, bankb = ptp[c4 % 2], ptb[c4 % 2]
            for j in range(4):
                sh.op("tensor", lambda e, bank=bank, qa=qa, c=c4 * 4 + j, j=j:
                      e.transpose(bank[:, j * 128:(j + 1) * 128],
                                  qa[:, c * 128:(c + 1) * 128], ident),
                      reads=[qab, ident_b], writes=[bankb])
            sg, sgb = d2stg[c4 % 4]
            sh.op("scalar", lambda e, sg=sg, bank=bank: e.copy(sg, bank[:, 0:512]),
                  reads=[bankb], writes=[sgb])
            sh.dma("sync", lambda e, sg=sg, c4=c4, t=t:
                   e.dma_start(out=xh2_fm[:, c4 * 4:(c4 + 1) * 4, t * 128:(t + 1) * 128],
                               in_=sg.rearrange("p (c k) -> p c k", k=128)),
                   reads=[sgb], writes=[x2fb])
    sb.pop(); sf.pop()

    sb.push(); sf.push()
    tpsb_shared = Buf("tpsb_g")
    tpsb = []
    for oc in range(FC):
        ap, b = sb.alloc(512, f"tp{oc}", buf=tpsb_shared)
        tpsb.append((ap, tpsb_shared))
    xf = [sb.alloc(FF, f"exf{i}") for i in range(2)]
    ostg = [sf.alloc(512, f"eos{i}") for i in range(4)]
    for dg in range(D // 512):
        for oc in range(FC):
            ta, tab = tpsb[oc]
            sh.dma("sync", lambda e, ta=ta, oc=oc, dg=dg:
                   e.dma_start(out=ta, in_=t_proj[oc * 128:(oc + 1) * 128, dg * 512:(dg + 1) * 512]),
                   reads=[tpb], writes=[tab])
        for t in range(TT):
            xa, xab = xf[t % 2]
            pass
            sh.dma("gpsimd", lambda e, xa=xa, t=t:
                   e.dma_start(out=xa.rearrange("p (c k) -> p c k", k=128), in_=xh2_fm[:, :, t * 128:(t + 1) * 128]),
                   reads=[x2fb], writes=[xab])
            bank, bankb = pb[t % 4], pbb[t % 4]
            for oc in range(FC):
                ta, tab = tpsb[oc]
                sh.op("tensor",
                      lambda e, bank=bank, xa=xa, ta=ta, oc=oc:
                      e.matmul(bank[:], xa[:, oc * 128:(oc + 1) * 128], ta,
                               start=(oc == 0), stop=(oc == FC - 1)),
                      reads=[xab, tab], writes=[bankb])
            sg, sgb = ostg[t % 4]
            sh.op("vector", lambda e, sg=sg, bank=bank, t=t:
                  e.tensor_scalar(sg, bank[:], sc2[:, t:t + 1], None, ALU.mult),
                  reads=[bankb, sc2b], writes=[sgb])
            sh.dma("sync", lambda e, sg=sg, t=t, dg=dg:
                   e.dma_start(out=out_proj[t * 128:(t + 1) * 128, dg * 512:(dg + 1) * 512], in_=sg),
                   reads=[sgb], writes=[opb])
    sb.pop(); sf.pop()

    sb.push(); sf.push()
    ot = [sf.alloc(D, f"fot{i}") for i in range(2)]
    res = [sf.alloc(D, f"fres{i}") for i in range(2)]
    ost = [sf.alloc(8, f"fost{i}") for i in range(2)]
    junk2, junk2b = sb.alloc(D, "fjunk2")
    for t in range(TT):
        oa, oab = ot[t % 2]
        xa2, xab2 = res[t % 2]
        ra, rab = ost[t % 2]
        sh.dma("sync", lambda e, oa=oa, t=t: e.dma_start(out=oa, in_=out_proj[t * 128:(t + 1) * 128, :]),
               reads=[opb], writes=[oab])
        sh.dma("scalar", lambda e, xa2=xa2, t=t: e.dma_start(out=xa2, in_=x_in[t * 128:(t + 1) * 128, :]),
               reads=[xin_b], writes=[xab2])
        emit_ln_stats(sh, oa, oab, D, 1e-5, junk2, junk2b, ra, rab)
        sh.op("vector", lambda e, oa=oa, ra=ra:
              e.tensor_scalar(oa, oa, ra[:, 2:3], ra[:, 3:4], ALU.subtract, ALU.mult),
              reads=[oab, rab], writes=[oab])
        sh.op("vector", lambda e, oa=oa, xa2=xa2: e.tensor_add(oa, oa, xa2),
              reads=[oab, xab2], writes=[oab])
        sh.dma("sync", lambda e, oa=oa, t=t: e.dma_start(out=x_out[t * 128:(t + 1) * 128, :], in_=oa),
               reads=[oab], writes=[xout_b])
    sb.pop(); sf.pop()
    sb.pop(); sf.pop()


def emit_moe(ctx, x_in, xin_b, y_out, yout_b):
    from concourse.bass_core_rust import IndirectOffsetOnAxis
    sh, sb, sf = ctx["sh"], ctx["sb"], ctx["sf"]
    W, dsc, dbuf = ctx["W"], ctx["dsc"], ctx["dbuf"]
    ident, ident_b = ctx["ident"], ctx["ident_b"]
    identf, identf_b = ctx["identf"], ctx["identf_b"]
    onesf, onesf_b = ctx["onesf"], ctx["onesf_b"]
    pb, pbb, ptp, ptb = ctx["pb"], ctx["pbb"], ctx["ptp"], ctx["ptb"]
    u_d, udb = dsc["u_d"], dbuf["u_d"]
    g_d, gdb = dsc["g_d"], dbuf["g_d"]
    acc_d, accb = dsc["acc_d"], dbuf["acc_d"]
    OT = OWN // 128
    NT5 = OWN // 512

    sb.push(); sf.push()
    rsel, rselb = sf.alloc(OT, "rsel")
    probs_all, probsb = sf.alloc(OT * 8, "probs")
    psT, psTb = sf.alloc(OT * 128, "psT")
    xn_d, xn_db = dsc["xn_fm"], Buf("moe_xn")
    bfm = {}
    for nm in ("bup", "bgate"):
        for e in range(E):
            ap, b = sf.alloc(DC, f"{nm}{e}")
            bfm[(nm, e)] = (ap, b)
    for nm in ("bsu", "bsg"):
        ap, b = sf.alloc(DC, nm)
        bfm[nm] = (ap, b)
    bdall, bdallb = sf.alloc(D, "bdall")
    wrs, wrsb = sb.alloc(DC * E, "wrs")
    brg, brgb = sf.alloc(E, "brg")

    nc = ctx["nc"]

    rsel_i = ctx["rsel_t"][:, 0:OT]
    rsel_ib = Buf("rsel_i")
    sh.dma("sync", lambda e: e.dma_start(out=rsel_i, in_=W["rowsel"][:, :]), writes=[rsel_ib])

    def load_bias_fm(src_ap, dst, dstb, tag):
        stg, stgb = sf.alloc(128, f"bstg{tag}")
        sh.dma("sync", lambda e, stg=stg: e.dma_start(out=stg[0:DC, 0:128], in_=src_ap),
               writes=[stgb])
        bank, bankb = pb[0], pbb[0]
        sh.op("tensor", lambda e, bank=bank, stg=stg:
              e.transpose(bank[:, 0:DC], stg[0:DC, 0:128], identf[0:DC, 0:DC]),
              reads=[stgb, identf_b], writes=[bankb])
        sh.op("scalar", lambda e, dst=dst, bank=bank: e.copy(dst, bank[:, 0:DC]),
              reads=[bankb], writes=[dstb])

    for e in range(E):
        ap, b = bfm[("bup", e)]
        load_bias_fm(W["bup"][e], ap, b, f"up{e}")
        ap, b = bfm[("bgate", e)]
        load_bias_fm(W["bgate"][e], ap, b, f"g{e}")
    load_bias_fm(W["bsu"], bfm["bsu"][0], bfm["bsu"][1], "su")
    load_bias_fm(W["bsg"], bfm["bsg"][0], bfm["bsg"][1], "sg")
    sh.dma("sync", lambda e: e.dma_start(out=bdall[0:E, :], in_=W["bdown_r"][:, :]), writes=[bdallb])
    sh.dma("scalar", lambda e: e.dma_start(out=bdall[E:E + 1, :], in_=W["bsd"][:, :]), writes=[bdallb])
    for c in range(DC):
        sh.dma("gpsimd", lambda e, c=c: e.dma_start(out=wrs[:, c * E:(c + 1) * E],
                                                    in_=W["wr"][c * 128:(c + 1) * 128, :]),
               writes=[wrsb])
    brrow, brrowb = sf.alloc(2 * E, "brrow")
    sh.dma("scalar", lambda e: e.dma_start(out=brrow[0:1, 0:E], in_=W["br"][:, :]), writes=[brrowb])
    sh.dma("scalar", lambda e: e.dma_start(out=brrow[0:1, E:2 * E], in_=W["gbias"][:, :]), writes=[brrowb])
    sh.op("vector", lambda e: e.tensor_add(brrow[0:1, 0:E], brrow[0:1, 0:E], brrow[0:1, E:2 * E]),
          reads=[brrowb], writes=[brrowb])
    bank, bankb = pb[1], pbb[1]
    sh.op("tensor", lambda e, bank=bank:
          e.matmul(bank[:, 0:E], onesf[0:1, 0:128], brrow[0:1, 0:E], start=True, stop=True),
          reads=[onesf_b, brrowb], writes=[bankb])
    sh.op("scalar", lambda e, bank=bank: e.copy(brg, bank[:, 0:E]),
          reads=[bankb], writes=[brgb])

    def emit_router_tile(t, tl):
        bank, bankb = pb[2 + t % 2], pbb[2 + t % 2]
        for c in range(DC):
            xa, xab = xnT[c]
            sh.op("tensor",
                  lambda e, bank=bank, xa=xa, c=c, tl=tl:
                  e.matmul(bank[:, 0:E], xa[:, tl * 128:(tl + 1) * 128], wrs[:, c * E:(c + 1) * E],
                           start=(c == 0), stop=(c == DC - 1)),
                  reads=[xab, wrsb], writes=[bankb])
        ra, rab = rt[t % 2]
        g0 = ra[:, 0:E]
        bg = ra[:, 7:7 + E]
        sel = ra[:, 14:14 + E]
        sh.op("vector", lambda e, g0=g0, bank=bank: e.tensor_scalar(g0, bank[:, 0:E], 1.0, None, ALU.mult),
              reads=[bankb], writes=[rab])
        sh.op("vector", lambda e, bg=bg, g0=g0: e.tensor_add(bg, g0, brg),
              reads=[rab, brgb], writes=[rab])
        sh.op("vector", lambda e, ra=ra, bg=bg: e.reduce_max(ra[:, 21:22], bg, axis=AX.X),
              reads=[rab], writes=[rab])
        sh.op("vector", lambda e, ra=ra, sel=sel, bg=bg:
              e.tensor_scalar(sel, bg, ra[:, 21:22], None, ALU.is_ge),
              reads=[rab], writes=[rab])
        sh.op("vector", lambda e, ra=ra, sel=sel, bg=bg:
              e.scalar_tensor_tensor(sel, sel, NEG, bg, ALU.mult, ALU.add),
              reads=[rab], writes=[rab])
        sh.op("vector", lambda e, ra=ra, sel=sel: e.reduce_max(ra[:, 22:23], sel, axis=AX.X),
              reads=[rab], writes=[rab])
        sh.op("vector", lambda e, ra=ra, sel=sel, bg=bg:
              e.tensor_scalar(sel, bg, ra[:, 22:23], None, ALU.is_ge),
              reads=[rab], writes=[rab])
        gm = ra[:, 0:E]
        sh.op("vector", lambda e, sel=sel: e.tensor_scalar(sel, sel, -1.0, None, ALU.add),
              reads=[rab], writes=[rab])
        sh.op("vector", lambda e, gm=gm, sel=sel, g0=g0:
              e.scalar_tensor_tensor(gm, sel, -NEG, g0, ALU.mult, ALU.add),
              reads=[rab], writes=[rab])
        sh.op("vector", lambda e, ra=ra, gm=gm: e.reduce_max(ra[:, 21:22], gm, axis=AX.X),
              reads=[rab], writes=[rab])
        sh.op("vector", lambda e, ra=ra: e.tensor_scalar(ra[:, 22:23], ra[:, 21:22], -1.0, None, ALU.mult),
              reads=[rab], writes=[rab])
        pcol = probs_all[:, t * 8:t * 8 + E]
        sh.op("scalar", lambda e, pcol=pcol, gm=gm, ra=ra:
              e.activation(pcol, gm, AF.Exp, bias=ra[:, 22:23], scale=1.0, accum_out=ra[:, 23:24]),
              reads=[rab], writes=[probsb, rab])
        sh.op("vector", lambda e, ra=ra: e.reciprocal(ra[:, 21:22], ra[:, 23:24]),
              reads=[rab], writes=[rab])
        sh.op("vector", lambda e, pcol=pcol, ra=ra:
              e.tensor_scalar(pcol, pcol, ra[:, 21:22], None, ALU.mult),
              reads=[rab, probsb], writes=[probsb])
        sh.op("vector", lambda e, t=t:
              e.tensor_scalar(probs_all[:, t * 8 + E:t * 8 + 8], onesf[:, 0:8 - E], 1.0, None, ALU.mult),
              reads=[onesf_b], writes=[probsb])
        bank2, bank2b = ptp[0], ptb[0]
        bank3, bank3b = pb[4], pbb[4]
        sh.op("tensor", lambda e, bank3=bank3, t=t:
              e.transpose(bank3[0:8, 0:128], probs_all[:, t * 8:t * 8 + 8], identf),
              reads=[probsb, identf_b], writes=[bank3b])
        sh.op("scalar", lambda e, bank3=bank3, t=t:
              e.copy(psT[0:8, t * 128:(t + 1) * 128], bank3[0:8, 0:128]),
              reads=[bank3b], writes=[psTb])

    sb.push(); sf.push()
    xg = [sf.alloc(D, f"mxg{i}") for i in range(2)]
    xq4 = [sb.alloc(D, f"mxq{i}") for i in range(4)]
    stats = [sf.alloc(8, f"mst{i}") for i in range(4)]
    junkm, junkmb = sb.alloc(D, "mjunk")
    rt = [sf.alloc(24, f"mrt{i}") for i in range(2)]
    mstg = [sb.alloc(512, f"mstg{i}") for i in range(2)]

    for t in range(OT):
        xa, xab = xg[t % 2]
        qa, qab = xq4[t % 4]
        sa, sab = stats[t % 4]
        sh.dma("gpsimd", lambda e, xa=xa, t=t:
               e.indirect_dma_start(xa, None, x_in[:, :],
                                    IndirectOffsetOnAxis(ap=rsel_i[:, t:t + 1], axis=0)),
               reads=[xin_b, rsel_ib], writes=[xab])
        emit_ln_stats(sh, xa, xab, D, 1e-5, junkm, junkmb, sa, sab)
        sh.op("vector", lambda e, qa=qa, xa=xa, sa=sa:
              e.tensor_scalar(qa, xa, sa[:, 2:3], sa[:, 3:4], ALU.subtract, ALU.mult),
              reads=[xab, sab], writes=[qab])
        if t % 4 == 3:
            for c in range(DC):
                bank, bankb = ptp[c % 2], ptb[c % 2]
                for j in range(4):
                    src2, src2b = xq4[j]
                    sh.op("tensor",
                          lambda e, bank=bank, j=j, src2=src2, c=c:
                          e.transpose(bank[:, j * 128:(j + 1) * 128],
                                      src2[:, c * 128:(c + 1) * 128], ident),
                          reads=[src2b, ident_b], writes=[bankb])
                sgx, sgxb = mstg[c % 2]
                sh.op("scalar", lambda e, sgx=sgx, bank=bank:
                      e.copy(sgx, bank[:, 0:512]),
                      reads=[bankb], writes=[sgxb])
                col0 = (t - 3) * 128
                sh.dma("sync", lambda e, sgx=sgx, c=c, col0=col0:
                       e.dma_start(out=xn_d[c, :, col0:col0 + 512], in_=sgx),
                       reads=[sgxb], writes=[xn_db])

    rxn_shared = Buf("rxn_g")
    xnT = []
    for d in range(DC):
        ap, b = sb.alloc(512, f"rxn{d}", buf=rxn_shared)
        xnT.append((ap, rxn_shared))
    for T in range(NT5):
        for d in range(DC):
            xa, xab = xnT[d]
            sh.dma("sync", lambda e, xa=xa, d=d, T=T:
                   e.dma_start(out=xa, in_=xn_d[d, :, T * 512:(T + 1) * 512]),
                   reads=[xn_db], writes=[xab])
        for tl in range(4):
            t = T * 4 + tl
            emit_router_tile(t, tl)

    sb.pop(); sf.pop()

    wex = []
    sb.push(); sf.push()
    wex_shared = Buf("wex_g")
    for d in range(DC):
        ap, b = sb.alloc(D, f"mw{d}", buf=wex_shared)
        wex.append((ap, wex_shared))
    ustg = [sb.alloc(512, f"mus{i}") for i in range(4)]
    sstg = [sb.alloc(512, f"mss{i}") for i in range(4)]
    gstg = [sb.alloc(512, f"mgs{i}") for i in range(4)]
    uslab_shared = [Buf("uslab0"), Buf("uslab1")]
    uslab = [(sb.alloc(DC * 512, f"musl{i}", buf=uslab_shared[i])[0], uslab_shared[i])
             for i in range(2)]
    gcol = [sb.alloc(DC * 128, f"mgc{i}") for i in range(2)]
    astg = [sf.alloc(512, f"mas{i}") for i in range(4)]
    acin = [sf.alloc(512, f"mai{i}") for i in range(4)]

    def load_wexp(wd, e=None):
        for d in range(DC):
            wa, wab = wex[d]
            if e is None:
                sh.dma("gpsimd", lambda e2, wa=wa, d=d, wd=wd:
                       e2.dma_start(out=wa, in_=wd[d * 128:(d + 1) * 128, :]),
                       writes=[wab])
            else:
                sh.dma("gpsimd", lambda e2, wa=wa, d=d, wd=wd, e=e:
                       e2.dma_start(out=wa, in_=wd[e, d * 128:(d + 1) * 128, :]),
                       writes=[wab])

    def expert_pass(eidx):
        shared = eidx == E
        if shared:
            load_wexp(W["wsu"])
            bup_ap, bup_b = bfm["bsu"]
        else:
            load_wexp(W["wup"], eidx)
            bup_ap, bup_b = bfm[("bup", eidx)]
        for T in range(NT5):
            ua_all, ua_allb = uslab[T % 2]
            sh.dma("sync", lambda e2, ua_all=ua_all, T=T:
                   e2.dma_start(out=ua_all.rearrange("p (c k) -> p c k", k=512),
                                in_=xn_d[:, :, T * 512:(T + 1) * 512].rearrange("c p k -> p c k")),
                   reads=[xn_db], writes=[ua_allb])
            for oc in range(DC):
                bank, bankb = pb[(oc * NT5 + T) % 4], pbb[(oc * NT5 + T) % 4]
                for d in range(DC):
                    wa, wab = wex[d]
                    sh.op("tensor",
                          lambda e2, bank=bank, wa=wa, ua_all=ua_all, oc=oc, d=d:
                          e2.matmul(bank[:], wa[:, oc * 128:(oc + 1) * 128],
                                    ua_all[:, d * 512:(d + 1) * 512],
                                    start=(d == 0), stop=(d == DC - 1)),
                          reads=[wab, ua_allb], writes=[bankb])
                sg, sgb = ustg[(oc * NT5 + T) % 4]
                sh.op("scalar", lambda e2, sg=sg, bank=bank, bup_ap=bup_ap, oc=oc:
                      e2.activation(sg, bank[:], AF.Identity, bias=bup_ap[:, oc:oc + 1], scale=1.0),
                      reads=[bankb, bup_b], writes=[sgb])
                sh.dma("sync", lambda e2, sg=sg, oc=oc, T=T:
                       e2.dma_start(out=u_d[:, oc, T * 512:(T + 1) * 512], in_=sg),
                       reads=[sgb], writes=[udb])
        if shared:
            load_wexp(W["wsg"])
            bg_ap, bg_b = bfm["bsg"]
        else:
            load_wexp(W["wgate"], eidx)
            bg_ap, bg_b = bfm[("bgate", eidx)]
        for T in range(NT5):
            ua_all, ua_allb = uslab[T % 2]
            sh.dma("sync", lambda e, ua_all=ua_all, T=T:
                   e.dma_start(out=ua_all.rearrange("p (c k) -> p c k", k=512),
                               in_=u_d[:, :, T * 512:(T + 1) * 512]),
                   reads=[udb], writes=[ua_allb])
            for oc in range(DC):
                bank, bankb = pb[(oc * NT5 + T) % 4], pbb[(oc * NT5 + T) % 4]
                for d in range(DC):
                    wa, wab = wex[d]
                    sh.op("tensor",
                          lambda e, bank=bank, wa=wa, ua_all=ua_all, oc=oc, d=d:
                          e.matmul(bank[:], wa[:, oc * 128:(oc + 1) * 128],
                                   ua_all[:, d * 512:(d + 1) * 512],
                                   start=(d == 0), stop=(d == DC - 1)),
                          reads=[wab, ua_allb], writes=[bankb])
                sg, sgb = sstg[(oc * NT5 + T) % 4]
                sh.op("scalar", lambda e, sg=sg, bank=bank, bg_ap=bg_ap, oc=oc:
                      e.activation(sg, bank[:], AF.Silu, bias=bg_ap[:, oc:oc + 1], scale=1.0),
                      reads=[bankb, bg_b], writes=[sgb])
                ga, gab = gstg[(oc * NT5 + T) % 4]
                sh.op("vector", lambda e, ga=ga, sg=sg, ua_all=ua_all, oc=oc:
                      e.tensor_mul(ga, sg, ua_all[:, oc * 512:(oc + 1) * 512]),
                      reads=[sgb, ua_allb], writes=[gab])
                sh.dma("sync", lambda e, ga=ga, oc=oc, T=T:
                       e.dma_start(out=g_d[:, oc, T * 512:(T + 1) * 512], in_=ga),
                       reads=[gab], writes=[gdb])
        if shared:
            load_wexp(W["wsd"])
        else:
            load_wexp(W["wdown"], eidx)
        for t in range(OT):
            gc, gcb = gcol[t % 2]
            sh.dma("gpsimd", lambda e, gc=gc, t=t:
                   e.dma_start(out=gc.rearrange("p (c k) -> p c k", k=128),
                               in_=g_d[:, :, t * 128:(t + 1) * 128]),
                   reads=[gdb], writes=[gcb])
            for dg in range(D // 512):
                bank, bankb = pb[(t * 4 + dg) % 4], pbb[(t * 4 + dg) % 4]
                for jc in range(DC):
                    wa, wab = wex[jc]
                    sh.op("tensor",
                          lambda e, bank=bank, gc=gc, wa=wa, jc=jc, dg=dg:
                          e.matmul(bank[:], gc[:, jc * 128:(jc + 1) * 128],
                                   wa[:, dg * 512:(dg + 1) * 512],
                                   start=(jc == 0), stop=(jc == DC - 1)),
                          reads=[gcb, wab], writes=[bankb])
                sg, sgb = astg[(t * 4 + dg) % 4]
                if eidx == 0:
                    sh.op("vector", lambda e, sg=sg, bank=bank, t=t:
                          e.tensor_scalar(sg, bank[:], probs_all[:, t * 8:t * 8 + 1], None, ALU.mult),
                          reads=[bankb, probsb], writes=[sgb])
                else:
                    aci, acib = acin[(t * 4 + dg) % 4]
                    sh.dma("scalar", lambda e, aci=aci, t=t, dg=dg:
                           e.dma_start(out=aci, in_=acc_d[t * 128:(t + 1) * 128, dg * 512:(dg + 1) * 512]),
                           reads=[accb], writes=[acib])
                    if shared:
                        sh.op("vector", lambda e, sg=sg, bank=bank, aci=aci:
                              e.scalar_tensor_tensor(sg, bank[:], 1.0, aci, ALU.mult, ALU.add),
                              reads=[bankb, acib], writes=[sgb])
                    else:
                        sh.op("vector", lambda e, sg=sg, bank=bank, aci=aci, t=t, eidx=eidx:
                              e.scalar_tensor_tensor(sg, bank[:], probs_all[:, t * 8 + eidx:t * 8 + eidx + 1],
                                                     aci, ALU.mult, ALU.add),
                              reads=[bankb, acib, probsb], writes=[sgb])
                sh.dma("sync", lambda e, sg=sg, t=t, dg=dg:
                       e.dma_start(out=acc_d[t * 128:(t + 1) * 128, dg * 512:(dg + 1) * 512], in_=sg),
                       reads=[sgb], writes=[accb])

    for eidx in range(E + 1):
        expert_pass(eidx)
    sb.pop(); sf.pop()

    sb.push(); sf.push()
    ot2 = [sf.alloc(D, f"m3o{i}") for i in range(2)]
    xg2 = [sf.alloc(D, f"m3x{i}") for i in range(2)]
    ost = [sf.alloc(8, f"m3s{i}") for i in range(2)]
    junk3, junk3b = sb.alloc(D, "m3junk")
    for t in range(OT):
        oa, oab = ot2[t % 2]
        xa, xab = xg2[t % 2]
        ra, rab = ost[t % 2]
        sh.dma("gpsimd", lambda e, xa=xa, t=t:
               e.indirect_dma_start(xa, None, x_in[:, :],
                                    IndirectOffsetOnAxis(ap=rsel_i[:, t:t + 1], axis=0)),
               reads=[xin_b, rsel_ib], writes=[xab])
        sh.dma("sync", lambda e, oa=oa, t=t: e.dma_start(out=oa, in_=acc_d[t * 128:(t + 1) * 128, :]),
               reads=[accb], writes=[oab])
        for dg in range(D // 512):
            bank, bankb = pb[dg % 2], pbb[dg % 2]
            sh.op("tensor",
                  lambda e, bank=bank, t=t, dg=dg:
                  e.matmul(bank[:], psT[0:8, t * 128:(t + 1) * 128],
                           bdall[0:8, dg * 512:(dg + 1) * 512], start=True, stop=True),
                  reads=[psTb, bdallb], writes=[bankb])
            sh.op("vector", lambda e, oa=oa, bank=bank, dg=dg:
                  e.tensor_add(oa[:, dg * 512:(dg + 1) * 512], oa[:, dg * 512:(dg + 1) * 512], bank[:]),
                  reads=[bankb, oab], writes=[oab])
        emit_ln_stats(sh, oa, oab, D, 1e-5, junk3, junk3b, ra, rab)
        sh.op("vector", lambda e, oa=oa, ra=ra:
              e.tensor_scalar(oa, oa, ra[:, 2:3], ra[:, 3:4], ALU.subtract, ALU.mult),
              reads=[oab, rab], writes=[oab])
        sh.op("vector", lambda e, oa=oa, xa=xa: e.tensor_add(oa, oa, xa),
              reads=[oab, xab], writes=[oab])
        sh.dma("sync", lambda e, oa=oa, t=t: e.dma_start(out=y_out[t * 128:(t + 1) * 128, :], in_=oa),
               reads=[oab], writes=[yout_b])
    sb.pop(); sf.pop()
    sb.pop(); sf.pop()



def make_consts():
    ident = np.eye(128, dtype=ml_dtypes.bfloat16)
    ones = np.ones((128, 128), dtype=ml_dtypes.bfloat16)
    col = np.arange(896, dtype=np.float32)[None, :]
    row = np.arange(128, dtype=np.float32)[:, None]
    mask = np.where(col <= row + 384, 0.0, NEG).astype(np.float32)
    return ident, ones, mask


ZIG0 = np.concatenate([np.arange(1024), np.arange(3072, 4096)]).astype(np.int32)
ZIG1 = np.arange(1024, 3072, dtype=np.int32)
FUSED = True


def _prep_host(inputs):
    ident, ones, mask = make_consts()
    identf = np.eye(128, dtype=np.float32)
    onesf = np.ones((128, 128), dtype=np.float32)
    T = lambda a: np.ascontiguousarray(np.asarray(a).T)
    A = lambda a: np.ascontiguousarray(np.asarray(a))
    shared = {
        "identc": ident, "onesc": ones, "maskc": mask,
        "identfc": identf, "onesfc": onesf,
        "wq": T(inputs["L0_attn_q_proj"]), "wk": T(inputs["L0_attn_k_proj"]),
        "wv": T(inputs["L0_attn_v_proj"]), "wo": T(inputs["L0_attn_o_proj"]),
        "wfc": T(inputs["L0_ffn_w_fc"]), "wproj": T(inputs["L0_ffn_w_proj"]),
        "wqkv": T(inputs["L1_attn_w_qkv"]), "wo1": T(inputs["L1_attn_w_o"]),
        "wup": A(inputs["L1_ffn_Wup"]), "wgate": A(inputs["L1_ffn_Wgate"]),
        "wdown": A(inputs["L1_ffn_Wdown"]),
        "wsu": A(inputs["L1_ffn_Wsu"]), "wsg": A(inputs["L1_ffn_Wsg"]),
        "wsd": A(inputs["L1_ffn_Wsd"]), "wr": A(inputs["L1_ffn_Wr"]),
        "br": A(inputs["L1_ffn_br"]).reshape(1, E),
        "gbias": A(inputs["L1_ffn_gate_bias"]).reshape(1, E),
        "bup": A(inputs["L1_ffn_bup"]).reshape(E, DC, 128),
        "bgate": A(inputs["L1_ffn_bgate"]).reshape(E, DC, 128),
        "bdown": A(inputs["L1_ffn_bdown"]).reshape(E, DC, 128),
        "bdown_r": A(inputs["L1_ffn_bdown"]),
        "bsu": A(inputs["L1_ffn_bsu"]).reshape(DC, 128),
        "bsg": A(inputs["L1_ffn_bsg"]).reshape(DC, 128),
        "bsd": A(inputs["L1_ffn_bsd"]).reshape(1, D),
    }
    x_full = np.asarray(inputs["L0_attn_in_features"])
    per_core = []
    for core in range(8):
        b, sub = core // 2, core % 2
        rows = ZIG0 if sub == 0 else ZIG1
        m = dict(shared)
        m["x0"] = np.ascontiguousarray(x_full[b])
        m["rowsel"] = np.ascontiguousarray(rows.reshape(OWN // 128, 128).T)
        per_core.append(m)
    return per_core


_BLOCK_KEYS = {
    "a0": ["x0", "identc", "onesc", "maskc", "identfc", "onesfc",
           "wq", "wk", "wv", "wo"],
    "f0": ["identc", "onesc", "maskc", "identfc", "onesfc", "wfc", "wproj"],
    "a1": ["identc", "onesc", "maskc", "identfc", "onesfc", "wqkv", "wo1"],
    "moe": ["identc", "onesc", "maskc", "identfc", "onesfc",
            "wup", "wgate", "wdown", "wsu", "wsg", "wsd", "wr", "br", "gbias",
            "bup", "bgate", "bdown", "bdown_r", "bsu", "bsg", "bsd", "rowsel"],
}


def kernel(**inputs):
    per_core = _prep_host(inputs)
    if FUSED:
        nc = build_program(blocks={"a0", "f0", "a1", "moe"}, boundaries={})
        keys = sorted(set(sum(_BLOCK_KEYS.values(), [])))
        in_maps = [{k: per_core[core][k] for k in keys} for core in range(8)]
        res = run_bass_kernel_spmd(nc, in_maps, core_ids=list(range(8)),
                                   trace_cores=[0])
        out = np.empty((4, S, D), dtype=np.float32)
        for core in range(8):
            b, sub = core // 2, core % 2
            rows = ZIG0 if sub == 0 else ZIG1
            out[b, rows, :] = np.asarray(res.results[core]["y"])
        return out
    carry = [dict() for _ in range(8)]
    stage_io = [
        ("a0", {"x1": "out"}, ["x1"]),
        ("f0", {"x1": "in", "x2": "out"}, ["x2"]),
        ("a1", {"x2": "in", "x3": "out"}, ["x3"]),
        ("moe", {"x3": "in"}, ["y"]),
    ]
    for blk, bnd, outs in stage_io:
        nc = build_program(blocks={blk}, boundaries=bnd)
        in_maps = []
        for core in range(8):
            m = {k: per_core[core][k] for k in _BLOCK_KEYS[blk]}
            for k, v in bnd.items():
                if v == "in":
                    m[k] = carry[core][k]
            in_maps.append(m)
        res = run_bass_kernel_spmd(nc, in_maps, core_ids=list(range(8)))
        for core in range(8):
            for k in outs:
                carry[core][k] = np.ascontiguousarray(res.results[core][k])
    out = np.empty((4, S, D), dtype=np.float32)
    for core in range(8):
        b, sub = core // 2, core % 2
        rows = ZIG0 if sub == 0 else ZIG1
        out[b, rows, :] = carry[core]["y"]
    return out
